# version 04
# baseline (speedup 1.0000x reference)
import sys
if "/opt/trn_rl_repo" not in sys.path:
    sys.path.insert(0, "/opt/trn_rl_repo")
import numpy as np
from contextlib import ExitStack

import concourse.bass as bass
import concourse.mybir as mybir
from concourse.bass_utils import run_bass_kernel_spmd
from concourse.bass_core_rust import IndirectOffsetOnAxis

F32 = mybir.dt.float32
F32R = mybir.dt.float32r
BF16 = mybir.dt.bfloat16
U32 = mybir.dt.uint32
AF = mybir.ActivationFunctionType
ALU = mybir.AluOpType
AX = mybir.AxisListType

T = 4096
C = 1024
TC = 512
NCH = T // TC
NCT = C // 128
NCORE = 8

LN_EPS = 1e-5


class P:

    def __init__(self, nc, stack):
        self.nc = nc
        self.stack = stack
        self.frames = [stack]
        self.c = {}
        self.S = {}
        self.uid = 0
        for name in ["pe", "v", "s", "g", "dsp", "dact", "dpool", "cc"]:
            self.S[name] = stack.enter_context(nc.semaphore("sem_" + name))
            self.c[name] = 0

    def nm(self, base):
        self.uid += 1
        return f"{base}_{self.uid}"

    def sbuf(self, base, shape, dtype):
        return self.frames[-1].enter_context(self.nc.sbuf_tensor(self.nm(base), shape, dtype))

    def psum(self, base, shape, dtype=F32):
        return self.frames[-1].enter_context(self.nc.psum_tensor(self.nm(base), shape, dtype))

    from contextlib import contextmanager as _cm

    @_cm
    def frame(self):
        st = ExitStack()
        self.frames.append(st)
        try:
            with st:
                yield
        finally:
            self.frames.pop()

    def dram(self, base, shape, dtype):
        return self.nc.dram_tensor(self.nm(base), shape, dtype, kind="Internal").ap()

    def inc(self, ins, sem, n=1):
        ins.then_inc(self.S[sem], n)
        self.c[sem] += n
        return self.c[sem]

    def wait(self, eng, sem, cnt=None):
        eng.wait_ge(self.S[sem], self.c[sem] if cnt is None else cnt)

    def vdo(self, ins):
        return self.inc(ins, "v")

    def sdo(self, ins):
        return self.inc(ins, "s")

    def gdo(self, ins):
        return self.inc(ins, "g")


def vch(v, p):
    def issue(ins):
        p.inc(ins, "v")
        v.wait_ge(p.S["v"], p.c["v"])
        return ins
    return issue


def sch(s, p):
    def issue(ins):
        p.inc(ins, "s")
        s.wait_ge(p.S["s"], p.c["s"])
        return ins
    return issue


def gch(g, p):
    def issue(ins):
        p.inc(ins, "g")
        g.wait_ge(p.S["g"], p.c["g"])
        return ins
    return issue


def dram_cm(x):
    return x.rearrange("(ct p) t -> p ct t", p=128)


def load_cm_chunk(e, p, dst, dst_off, x_dram, t0, t1, nct=NCT, stride=None):
    w = t1 - t0
    st = stride if stride is not None else w
    dview = dst[:].rearrange("p (ct t) -> p ct t", t=st)[:, 0:nct, dst_off:dst_off + w]
    sview = dram_cm(x_dram)[:, 0:nct, t0:t1]
    return e.dma_start(dview, sview)


def store_cm_chunk(e, p, src, src_off, x_dram, t0, t1, nct=NCT, stride=None):
    w = t1 - t0
    st = stride if stride is not None else w
    sview = src[:].rearrange("p (ct t) -> p ct t", t=st)[:, 0:nct, src_off:src_off + w]
    dview = dram_cm(x_dram)[:, 0:nct, t0:t1]
    return e.dma_start(dview, sview)


class Consts:

    def __init__(self, p, consts_dram):
        nc = p.nc
        self.tt_e1 = p.sbuf("tt_e1", [32, 512], F32R)
        self.tt_e2 = p.sbuf("tt_e2", [16, 128], F32R)
        self.sel_lo = p.sbuf("sel_lo", [1, 128], F32R)
        self.sel_hi = p.sbuf("sel_hi", [1, 128], F32R)
        self.ones_col = p.sbuf("ones_col", [128, 1], F32R)
        self.ones_row = p.sbuf("ones_row", [1, 128], F32R)
        self.ident = p.sbuf("ident", [128, 128], F32)
        self.ident_r = p.sbuf("ident_r", [128, 128], F32R)
        self.ident_bf = p.sbuf("ident_bf", [128, 128], BF16)
        self.ustrict = p.sbuf("ustrict", [128, 128], F32R)
        self.iota32 = p.sbuf("iota32", [128, 32], F32)
        self.iota_col = p.sbuf("iota_col", [128, 1], F32)
        self.ones_bf_col = p.sbuf("ones_bf_col", [128, 1], BF16)
        self.c = consts_dram

    def emit_load(self, p):
        nc = p.nc
        with nc.Block() as blk:
            @blk.sync
            def _(e):
                e.dma_start(self.ident[:], self.c["ident"]).then_inc(p.S["dsp"], 16)
                e.dma_start(self.ident_r[:], self.c["ident_r"]).then_inc(p.S["dsp"], 16)
                p.c["dsp"] += 16
                e.dma_start(self.ident_bf[:], self.c["ident_bf"]).then_inc(p.S["dsp"], 16)
                e.dma_start(self.ustrict[:], self.c["ustrict"]).then_inc(p.S["dsp"], 16)
                e.dma_start(self.ones_col[:], self.c["ones_col"]).then_inc(p.S["dsp"], 16)
                e.dma_start(self.ones_row[:], self.c["ones_col"].rearrange("a b -> b a")).then_inc(p.S["dsp"], 16)
                e.dma_start(self.ones_bf_col[:], self.c["ones_bf_col"]).then_inc(p.S["dsp"], 16)
                e.dma_start(self.tt_e1[:], self.c["tt_e1"]).then_inc(p.S["dsp"], 16)
                e.dma_start(self.tt_e2[:], self.c["tt_e2"]).then_inc(p.S["dsp"], 16)
                e.dma_start(self.sel_lo[:], self.c["selhalf"][0:1, :]).then_inc(p.S["dsp"], 16)
                e.dma_start(self.sel_hi[:], self.c["selhalf"][1:2, :]).then_inc(p.S["dsp"], 16)
                p.c["dsp"] += 160
                e.wait_ge(p.S["dsp"], p.c["dsp"])

            @blk.gpsimd
            def _(g):
                gi = gch(g, p)
                gi(g.iota(self.iota32[:], [[1, 32]], base=0, channel_multiplier=0,
                          allow_small_or_imprecise_dtypes=True))
                gi(g.iota(self.iota_col[:], [[0, 1]], base=0, channel_multiplier=1,
                          allow_small_or_imprecise_dtypes=True))


def const_dtype(k):
    if k in ("ustrict", "ones_col", "ones_row", "tt_e1", "tt_e2", "ident_r",
             "selhalf"):
        return F32R
    if k.endswith("_bf") or k == "ones_bf_col":
        return BF16
    return F32


def host_consts():
    import ml_dtypes
    ident = np.eye(128, dtype=np.float32)
    ident_bf = np.eye(128, dtype=np.float32).astype(ml_dtypes.bfloat16)
    iu = (np.arange(128)[:, None] < np.arange(128)[None, :]).astype(np.float32)
    e1 = np.zeros((32, 512), np.float32)
    for j in range(512):
        e1[j // 16, j] = 1.0
    e2 = np.zeros((16, 128), np.float32)
    for j in range(128):
        e2[j % 16, j] = 1.0
    sh = np.zeros((2, 128), np.float32)
    sh[0, :64] = 1.0
    sh[1, 64:] = 1.0
    return {"ident": ident, "ident_r": ident, "ident_bf": ident_bf, "ustrict": iu,
            "selhalf": sh,
            "tt_e1": e1, "tt_e2": e2,
            "ones_col": np.ones((128, 1), np.float32),
            "ones_bf_col": np.ones((128, 1), np.float32).astype(ml_dtypes.bfloat16)}



def ln_chunk(p, cn, *, z_tiles, out_tiles, g_ap=None, b_ap=None, eps=LN_EPS,
             nct=NCT, width=TC, sq=None, stats=None, ps_sum=None, ps_b=None):
    nc = p.nc
    s_sum = stats[0:1, 0 * width:1 * width]
    s_ssq = stats[0:1, 1 * width:2 * width]
    s_nmean = stats[0:1, 2 * width:3 * width]
    s_rstd = stats[0:1, 3 * width:4 * width]
    mb = ps_b[:, 0:width]
    rb = ps_b[:, width:2 * width]

    sc_marks = []
    with nc.Block() as blk:
        pe0 = p.c["pe"]

        @blk.scalar
        def _(s):
            si = sch(s, p)
            for ci in range(nct):
                if ci > 0:
                    s.wait_ge(p.S["pe"], pe0 + 2 * ci)
                si(s.activation(sq[:], z_tiles(ci), AF.Square))
                sc_marks.append(p.c["s"])

        @blk.tensor
        def _(t):
            for ci in range(nct):
                t.wait_ge(p.S["s"], sc_marks[ci])
                ins = t.matmul(ps_sum[0:1, 0:width], cn.ones_col[:], z_tiles(ci),
                               start=(ci == 0), stop=(ci == nct - 1))
                p.inc(ins, "pe")
                ins = t.matmul(ps_sum[0:1, width:2 * width], cn.ones_col[:], sq[:],
                               start=(ci == 0), stop=(ci == nct - 1))
                p.inc(ins, "pe")

        pe_done = p.c["pe"]

        @blk.vector
        def _(v):
            vi = vch(v, p)
            v.wait_ge(p.S["pe"], pe_done)
            vi(v.tensor_scalar(s_nmean, ps_sum[0:1, 0:width], -1.0 / float(C), None, ALU.mult))
            vi(v.tensor_scalar(s_rstd, ps_sum[0:1, width:2 * width], 1.0 / float(C), None, ALU.mult))
            vi(v.scalar_tensor_tensor(s_ssq, s_nmean, 1.0,
                                      s_nmean, ALU.mult, ALU.mult))
            vi(v.tensor_sub(s_rstd, s_rstd, s_ssq))
            vi(v.tensor_scalar(s_rstd, s_rstd, float(eps),
                               None, ALU.add))

        vmark = p.c["v"]

        @blk.scalar
        def _(s):
            si = sch(s, p)
            s.wait_ge(p.S["v"], vmark)
            si(s.activation(s_rstd, s_rstd, AF.Ln))
            si(s.activation(s_rstd, s_rstd, AF.Exp, scale=-0.5))

    with nc.Block() as blk:
        pe0 = p.c["pe"]

        @blk.tensor
        def _(t):
            ins = t.matmul(mb, cn.ones_row[:], s_nmean, start=True, stop=True)
            p.inc(ins, "pe")
            ins = t.matmul(rb, cn.ones_row[:], s_rstd, start=True, stop=True)
            p.inc(ins, "pe")

        pe_done = p.c["pe"]

        @blk.vector
        def _(v):
            vi = vch(v, p)
            v.wait_ge(p.S["pe"], pe_done)
            for ci in range(nct):
                vi(v.tensor_add(out_tiles(ci), z_tiles(ci), mb))
                vi(v.tensor_mul(out_tiles(ci), out_tiles(ci), rb))
                if g_ap is not None:
                    vi(v.tensor_scalar(out_tiles(ci),
                                       out_tiles(ci),
                                       g_ap(ci), b_ap(ci), ALU.mult, ALU.add))


def make_ln_scratch(p, width=TC):
    return dict(
        sq=p.sbuf("lns_sq", [128, width], F32R),
        stats=p.sbuf("lns_stats", [1, 4 * width], F32R),
        ps_sum=p.psum("lns_psum", [1, 2 * width], F32),
        ps_b=p.psum("lns_pb", [128, 2 * width], F32),
    )



def emit_l1_ffn(p, cn, W, xin, xout):
    nc = p.nc
    w_sb = p.sbuf("l1f_w", [128, 2 * NCT * C], F32R)
    wcols = [p.sbuf("l1f_wc", [128, NCT * 128], F32R) for _ in range(2)]
    mu = p.sbuf("l1f_mu", [128, 16], F32)
    xch = p.sbuf("l1f_x", [128, NCT * (TC + 1)], F32R)
    xr = p.sbuf("l1f_xr", [128, NCT * TC], F32R)
    xk = p.sbuf("l1f_xk", [128, NCT * TC], F32R)
    rsig = p.sbuf("l1f_r", [128, NCT * TC], F32)
    k2 = p.sbuf("l1f_k2", [128, NCT * TC], F32R)
    yv = p.sbuf("l1f_y", [128, NCT * TC], F32R)
    lns = make_ln_scratch(p)
    psm = [p.psum("l1f_ps", [128, TC], F32) for _ in range(4)]

    def wtile(which, ot, ci):
        base = which * NCT * C
        return w_sb[:, base + ci * C + ot * 128: base + ci * C + ot * 128 + 128]

    with nc.Block() as blk:
        @blk.sync
        def _(e):
            for wi, nmw in enumerate(["wrT", "wkT"]):
                e.dma_start(
                    w_sb[:, wi * NCT * C:(wi + 1) * NCT * C]
                    .rearrange("p (ci o) -> p ci o", ci=NCT),
                    W[nmw].rearrange("(ci p) o -> p ci o", p=128),
                ).then_inc(p.S["dsp"], 16)
            e.dma_start(mu[:, 0:8], W["mu_r"]).then_inc(p.S["dsp"], 16)
            e.dma_start(mu[:, 8:16], W["mu_k"]).then_inc(p.S["dsp"], 16)
            p.c["dsp"] += 4 * 16
            e.wait_ge(p.S["dsp"], p.c["dsp"])

    for ch in range(NCH):
        t0, t1 = ch * TC, (ch + 1) * TC

        def xt(ci, off=1, w=TC):
            return xch[:, ci * (TC + 1) + off: ci * (TC + 1) + off + w]

        with nc.Block() as blk:
            @blk.sync
            def _(e):
                if ch == 0:
                    ins = load_cm_chunk(e, p, xch, 1, xin, 0, TC, stride=TC + 1)
                    p.inc(ins, "dsp", 16)
                else:
                    ins = load_cm_chunk(e, p, xch, 0, xin, t0 - 1, t1, stride=TC + 1)
                    p.inc(ins, "dsp", 16)
                e.wait_ge(p.S["dsp"], p.c["dsp"])

            if ch == 0:
                @blk.vector
                def _(v):
                    vi = vch(v, p)
                    for ci in range(NCT):
                        vi(v.memset(xt(ci, 0, 1).bitcast(F32), 0.0))

        with nc.Block() as blk:
            @blk.vector
            def _(v):
                vi = vch(v, p)
                for ci in range(NCT):
                    dd = xr[:, ci * TC:(ci + 1) * TC]
                    vi(v.tensor_sub(dd, xt(ci, 1), xt(ci, 0)))
                    vi(v.scalar_tensor_tensor(
                        xk[:, ci * TC:(ci + 1) * TC], dd,
                        mu[:, 8 + ci:9 + ci], xt(ci, 0), ALU.mult, ALU.add))
                    vi(v.scalar_tensor_tensor(
                        xr[:, ci * TC:(ci + 1) * TC], dd,
                        mu[:, ci:ci + 1], xt(ci, 0), ALU.mult, ALU.add))

        with nc.Block() as blk:
            pe0 = p.c["pe"]
            sc0 = p.c["s"]

            @blk.tensor
            def _(t):
                for ot in range(NCT):
                    for ci in range(NCT):
                        ins = t.matmul(psm[ot % 2][:], wtile(0, ot, ci),
                                       xr[:, ci * TC:(ci + 1) * TC],
                                       start=(ci == 0), stop=(ci == NCT - 1))
                    p.inc(ins, "pe")
                    for ci in range(NCT):
                        ins = t.matmul(psm[2 + ot % 2][:], wtile(1, ot, ci),
                                       xk[:, ci * TC:(ci + 1) * TC],
                                       start=(ci == 0), stop=(ci == NCT - 1))
                    p.inc(ins, "pe")
                    if ot < NCT - 1:
                        t.wait_ge(p.S["s"], sc0 + 3 * ot + 1)

            @blk.scalar
            def _(s):
                si = sch(s, p)
                for ot in range(NCT):
                    s.wait_ge(p.S["pe"], pe0 + 2 * ot + 1)
                    si(s.activation(rsig[:, ot * TC:(ot + 1) * TC], psm[ot % 2][:],
                                    AF.Sigmoid))
                    s.wait_ge(p.S["pe"], pe0 + 2 * ot + 2)
                    si(s.activation(k2[:, ot * TC:(ot + 1) * TC],
                                    psm[2 + ot % 2][:], AF.Relu))
                    si(s.activation(k2[:, ot * TC:(ot + 1) * TC],
                                    k2[:, ot * TC:(ot + 1) * TC], AF.Square))

        def ep_kv(eng, issue, ot, ps):
            issue(eng.tensor_mul(yv[:, ot * TC:(ot + 1) * TC],
                                 rsig[:, ot * TC:(ot + 1) * TC], ps))
        _proj_stream(p, cn, wT=W["wvT"], Kd=C, Nd=C,
                     x_tiles=lambda ci: k2[:, ci * TC:(ci + 1) * TC],
                     ep_fn=ep_kv, ep_eng="v", ep_nops=1, psms=psm[:2],
                     wcols=wcols)

        with nc.Block() as blk:
            @blk.vector
            def _(v):
                vi = vch(v, p)
                for ci in range(NCT):
                    vi(v.tensor_add(yv[:, ci * TC:(ci + 1) * TC],
                                    yv[:, ci * TC:(ci + 1) * TC],
                                    xt(ci, 1)))

        ln_chunk(p, cn,
                 z_tiles=lambda ci: yv[:, ci * TC:(ci + 1) * TC],
                 out_tiles=lambda ci: xr[:, ci * TC:(ci + 1) * TC],
                 **{k: v for k, v in lns.items()})

        with nc.Block() as blk:
            @blk.sync
            def _(e):
                ins = store_cm_chunk(e, p, xr, 0, xout, t0, t1)
                p.inc(ins, "dsp", 16)
                e.wait_ge(p.S["dsp"], p.c["dsp"])



def vec8(v):
    return np.ascontiguousarray(np.asarray(v, np.float32).reshape(NCT, 128).T)





class Ep:

    def __init__(self, eng, nops, fn):
        self.eng = eng
        self.nops = nops
        self.fn = fn


class Ep3:

    def __init__(self, nv_pre, ns, nv_post, fn_vpre, fn_s, fn_vpost):
        self.eng = "sv3"
        self.nv_pre = nv_pre
        self.ns = ns
        self.nv_post = nv_post
        self.fn_vpre = fn_vpre
        self.fn_s = fn_s
        self.fn_vpost = fn_vpost


def lin_pass(p, cn, *, wT=None, w_sb=None, Kd, Nd, src=None, src_sb=None,
             dst=None, ep=None, aux=None, finalize=None):
    with p.frame():
        _lin_pass(p, cn, wT=wT, w_sb=w_sb, Kd=Kd, Nd=Nd, src=src, src_sb=src_sb,
                  dst=dst, ep=ep, aux=aux, finalize=finalize)


def _lin_pass(p, cn, *, wT=None, w_sb=None, Kd, Nd, src=None, src_sb=None,
              dst=None, ep=None, aux=None, finalize=None):
    nc = p.nc
    kt, nt = Kd // 128, Nd // 128
    need_pp = (Kd * Nd * 4) // 128 + 2 * (Kd * TC * 4) // 128 + 40960
    w_res = w_sb is not None or (wT is not None and need_pp <= nc.sbuf_bytes_remaining)
    if w_sb is None and w_res:
        w_sb = p.sbuf("lp_w", [128, kt * Nd], F32R)
        with nc.Block() as blk:
            @blk.sync
            def _(e):
                ins = e.dma_start(
                    w_sb[:].rearrange("p (ci o) -> p ci o", ci=kt),
                    wT.rearrange("(ci p) o -> p ci o", p=128))
                p.inc(ins, "dsp", 16)
                e.wait_ge(p.S["dsp"], p.c["dsp"])
    wcols = None if w_res else [p.sbuf("lp_wc", [128, kt * 128], F32R)
                                for _ in range(2)]
    collect = dst is None
    nsrc = 1 if (collect or src_sb is not None) else 2
    srcbs = [src_sb] if src_sb is not None else         [p.sbuf("lp_src", [128, kt * TC], F32R) for _ in range(nsrc)]
    stages = None if collect else [p.sbuf("lp_stg", [128, TC], F32R) for _ in range(2)]
    out_sb = p.sbuf("lp_out", [128, nt * TC], F32R) if collect else None
    auxb = [p.sbuf("lp_aux", [128, TC], F32R) for _ in range(2)] if aux else None
    collect_pre = dst is None
    NB = 2 if collect_pre else 4
    psm = [p.psum("lp_ps", [128, TC], F32) for _ in range(NB)]
    epname = ep.eng
    scrs = [p.sbuf("lp_scr", [128, TC], F32R) for _ in range(2)] \
        if epname == "sv3" else None
    fin_eng = "v" if epname in ("v", "sv3") else "s"

    if epname == "sv3":
        nfin = ep.nv_pre + ep.nv_post
    else:
        nfin = ep.nops

    if collect:
        for ch in range(NCH):
            t0, t1 = ch * TC, (ch + 1) * TC
            srcb = srcbs[0]
            with nc.Block() as blk:
                @blk.sync
                def _(e):
                    if src is not None:
                        ins = load_cm_chunk(e, p, srcb, 0, src, t0, t1, nct=kt)
                        p.inc(ins, "dsp", 16)
                    if not w_res:
                        ins = e.dma_start(
                            wcols[0][:].rearrange("p (ci o) -> p ci o", ci=kt),
                            wT.rearrange("(ci p) o -> p ci o", p=128)[:, :, 0:128])
                        p.inc(ins, "dsp", 16)
                    e.wait_ge(p.S["dsp"], p.c["dsp"])
            pe0 = p.c["pe"]
            v00 = p.c["v"]
            s00 = p.c["s"]

            def fin_count(k):
                return (v00 if fin_eng == "v" else s00) + nfin * k

            wready = [p.c["dsp"]] * nt
            with nc.Block() as blk:
                if not w_res:
                    @blk.sync
                    def _(e):
                        for k in range(nt - 1):
                            if k >= 1:
                                e.wait_ge(p.S["pe"], pe0 + k)
                            ins = e.dma_start(
                                wcols[(k + 1) % 2][:].rearrange(
                                    "p (ci o) -> p ci o", ci=kt),
                                wT.rearrange("(ci p) o -> p ci o", p=128)
                                [:, :, (k + 1) * 128:(k + 2) * 128])
                            p.inc(ins, "dsp", 16)
                            wready[k + 1] = p.c["dsp"]

                @blk.tensor
                def _(t):
                    for ot in range(nt):
                        if not w_res:
                            t.wait_ge(p.S["dsp"], wready[ot])
                        if ot >= 2:
                            t.wait_ge(p.S[fin_eng], fin_count(ot - 1))
                        for ci in range(kt):
                            ins = t.matmul(
                                psm[ot % 2][:],
                                (w_sb[:, ci * Nd + ot * 128:
                                      ci * Nd + ot * 128 + 128] if w_res else
                                 wcols[ot % 2][:, ci * 128:(ci + 1) * 128]),
                                srcb[:, ci * TC:(ci + 1) * TC],
                                start=(ci == 0), stop=(ci == kt - 1))
                        p.inc(ins, "pe")

                def ep_sec(eng):
                    issue = (sch if epname == "s" else vch)(eng, p)
                    for ot in range(nt):
                        eng.wait_ge(p.S["pe"], pe0 + ot + 1)
                        ep.fn(eng, issue, ot,
                              psm[ot % 2][:],
                              out_sb[:, ot * TC:(ot + 1) * TC], ch, None)

                if epname == "s":
                    @blk.scalar
                    def _(s):
                        ep_sec(s)
                else:
                    @blk.vector
                    def _(v):
                        ep_sec(v)

            if finalize is not None:
                finalize(ch, out_sb, srcb)
        return

    pe0 = p.c["pe"]
    v00 = p.c["v"]
    s00 = p.c["s"]
    dp0 = p.c["dpool"]

    def fin_count(k):
        return (v00 if fin_eng == "v" else s00) + nfin * k

    def issue_plain(eng_sem):
        def f(ins):
            p.inc(ins, eng_sem)
            return ins
        return f

    sched = []
    if w_res:
        for ch in range(NCH):
            for ot in range(nt):
                sched.append((ch, ot, None))
    else:
        for pr in range(NCH // 2):
            for ot in range(nt):
                for sub in range(2):
                    sched.append((2 * pr + sub, ot, pr * nt + ot))
    NG = len(sched)
    loadmark = [0] * NCH
    wmark = [0] * (NCH // 2 * nt + 1)
    auxmark = [0] * NG
    last_g_of_ch = {}
    for g, (ch, ot, ws) in enumerate(sched):
        last_g_of_ch[ch] = g

    with nc.Block() as blk:
        @blk.sync
        def _(e):
            gw = 0
            if w_res:
                for ch in range(NCH):
                    if src is not None:
                        if ch >= nsrc:
                            e.wait_ge(p.S["pe"],
                                      pe0 + last_g_of_ch[ch - nsrc] + 1)
                        ins = load_cm_chunk(e, p, srcbs[ch % nsrc], 0, src,
                                            ch * TC, (ch + 1) * TC, nct=kt)
                        p.inc(ins, "dsp", 16)
                    loadmark[ch] = p.c["dsp"]
                    if aux:
                        for ot in range(nt):
                            g = ch * nt + ot
                            if g >= 2:
                                e.wait_ge(p.S[fin_eng], fin_count(g - 1))
                            ins = e.dma_start(
                                auxb[g % 2][:],
                                dram_cm(aux)[:, ot, ch * TC:(ch + 1) * TC])
                            p.inc(ins, "dsp", 16)
                            auxmark[g] = p.c["dsp"]
            else:
                for pr in range(NCH // 2):
                    for sub in range(2):
                        ch = 2 * pr + sub
                        if src is not None:
                            if ch >= nsrc:
                                e.wait_ge(p.S["pe"],
                                          pe0 + last_g_of_ch[ch - nsrc] + 1)
                            ins = load_cm_chunk(e, p, srcbs[ch % nsrc], 0, src,
                                                ch * TC, (ch + 1) * TC, nct=kt)
                            p.inc(ins, "dsp", 16)
                        loadmark[ch] = p.c["dsp"]
                    for ot in range(nt):
                        ws = pr * nt + ot
                        if ws >= 2:
                            e.wait_ge(p.S["pe"], pe0 + 2 * (ws - 1))
                        ins = e.dma_start(
                            wcols[ws % 2][:].rearrange("p (ci o) -> p ci o",
                                                       ci=kt),
                            wT.rearrange("(ci p) o -> p ci o", p=128)
                            [:, :, ot * 128:(ot + 1) * 128])
                        p.inc(ins, "dsp", 16)
                        wmark[ws] = p.c["dsp"]

        @blk.tensor
        def _(t):
            for g, (ch, ot, ws) in enumerate(sched):
                need = loadmark[ch]
                if ws is not None:
                    need = max(need, wmark[ws])
                if aux:
                    need = max(need, auxmark[g])
                t.wait_ge(p.S["dsp"], need)
                if g >= NB:
                    t.wait_ge(p.S[fin_eng], fin_count(g - NB + 1))
                for ci in range(kt):
                    ins = t.matmul(
                        psm[g % NB][:],
                        (w_sb[:, ci * Nd + ot * 128:
                              ci * Nd + ot * 128 + 128] if w_res else
                         wcols[ws % 2][:, ci * 128:(ci + 1) * 128]),
                        srcbs[ch % nsrc][:, ci * TC:(ci + 1) * TC],
                        start=(ci == 0), stop=(ci == kt - 1))
                p.inc(ins, "pe")

        if epname == "sv3":
            @blk.vector
            def _(v):
                issue = vch(v, p)
                for g, (ch, ot, ws) in enumerate(sched):
                    v.wait_ge(p.S["pe"], pe0 + g + 1)
                    if aux:
                        v.wait_ge(p.S["dsp"], auxmark[g])
                    if g >= 2:
                        v.wait_ge(p.S["dpool"], dp0 + 16 * (g - 1))
                    ep.fn_vpre(v, issue, ot, psm[g % NB][:],
                               stages[g % 2][:], ch,
                               auxb[g % 2][:] if aux else None,
                               scrs[g % 2][:])
                    v.wait_ge(p.S["s"], s00 + ep.ns * (g + 1))
                    ep.fn_vpost(v, issue, ot, psm[g % NB][:],
                                stages[g % 2][:], ch,
                                auxb[g % 2][:] if aux else None,
                                scrs[g % 2][:])

            @blk.scalar
            def _(s):
                issue = sch(s, p)
                for g, (ch, ot, ws) in enumerate(sched):
                    s.wait_ge(p.S["v"],
                              v00 + (ep.nv_pre + ep.nv_post) * g + ep.nv_pre)
                    ep.fn_s(s, issue, ot, psm[g % NB][:], stages[g % 2][:],
                            ch, auxb[g % 2][:] if aux else None,
                            scrs[g % 2][:])
        else:
            def ep_sec(eng):
                issue = issue_plain(epname)
                for g, (ch, ot, ws) in enumerate(sched):
                    eng.wait_ge(p.S["pe"], pe0 + g + 1)
                    if aux:
                        eng.wait_ge(p.S["dsp"], auxmark[g])
                    if g >= 2:
                        eng.wait_ge(p.S["dpool"], dp0 + 16 * (g - 1))
                    ep.fn(eng, issue, ot, psm[g % NB][:], stages[g % 2][:],
                          ch, auxb[g % 2][:] if aux else None)

            if epname == "s":
                @blk.scalar
                def _(s):
                    ep_sec(s)
            else:
                @blk.vector
                def _(v):
                    ep_sec(v)

        @blk.gpsimd
        def _(g_):
            for g, (ch, ot, ws) in enumerate(sched):
                g_.wait_ge(p.S[fin_eng], fin_count(g + 1))
                ins = g_.dma_start(dram_cm(dst)[:, ot,
                                               ch * TC:(ch + 1) * TC],
                                   stages[g % 2][:])
                p.inc(ins, "dpool", 16)

    with nc.Block() as blk:
        @blk.gpsimd
        def _(g_):
            g_.wait_ge(p.S["dpool"], p.c["dpool"])


def emit_resid_ln(p, cn, lns, *, ch, y_tiles, xres, out_dram, g_ap=None,
                  b_ap=None, xbuf=None, extra_res=None):
    nc = p.nc
    t0, t1 = ch * TC, (ch + 1) * TC
    with nc.Block() as blk:
        @blk.sync
        def _(e):
            ins = load_cm_chunk(e, p, xbuf, 0, xres, t0, t1)
            p.inc(ins, "dsp", 16)
            e.wait_ge(p.S["dsp"], p.c["dsp"])
    with nc.Block() as blk:
        @blk.vector
        def _(v):
            vi = vch(v, p)
            for ci in range(NCT):
                vi(v.tensor_add(y_tiles(ci), y_tiles(ci),
                                xbuf[:, ci * TC:(ci + 1) * TC]))
    ln_chunk(p, cn, z_tiles=y_tiles, out_tiles=y_tiles, g_ap=g_ap, b_ap=b_ap,
             **lns)
    if out_dram is not None:
        with nc.Block() as blk:
            @blk.gpsimd
            def _(g):
                for ci in range(NCT):
                    ins = g.dma_start(dram_cm(out_dram)[:, ci, t0:t1], y_tiles(ci))
                    p.inc(ins, "dpool", 16)
                g.wait_ge(p.S["dpool"], p.c["dpool"])



def emit_l0_ffn(p, cn, W, xin, xout):
    nc = p.nc
    sA = p.dram("l0f_sA", [4096, T], F32R)
    sB = p.dram("l0f_sB", [4096, T], F32R)
    sv = p.sbuf("l0f_scales", [128, 4 * 32 + 8], F32)
    with nc.Block() as blk:
        @blk.sync
        def _(e):
            for i, nmw in enumerate(["s1", "s2", "sh1", "sh2"]):
                ins = e.dma_start(sv[:, i * 32:(i + 1) * 32], W[nmw])
                p.inc(ins, "dsp", 16)
            ins = e.dma_start(sv[:, 128:136], W["s3"])
            p.inc(ins, "dsp", 16)
            e.wait_ge(p.S["dsp"], p.c["dsp"])

    def ep1(eng, issue, ot, ps, stage, ch, auxap):
        issue(eng.activation(stage, ps, AF.Silu, scale=sv[:, ot:ot + 1]))
    lin_pass(p, cn, wT=W["w1T"], Kd=C, Nd=4096, src=xin, dst=sA,
             ep=Ep("s", 1, ep1))

    def ep2(eng, issue, ot, ps, stage, ch, auxap):
        issue(eng.scalar_tensor_tensor(stage, ps, sv[:, 32 + ot:33 + ot],
                                       auxap, ALU.mult, ALU.mult))
    lin_pass(p, cn, wT=W["w2T"], Kd=C, Nd=4096, src=xin, dst=sB,
             ep=Ep("v", 1, ep2), aux=sA)

    def ep3(eng, issue, ot, ps, stage, ch, auxap):
        issue(eng.activation(stage, ps, AF.Gelu, scale=sv[:, 64 + ot:65 + ot]))
    lin_pass(p, cn, wT=W["h1T"], Kd=4096, Nd=4096, src=sB, dst=sA,
             ep=Ep("s", 1, ep3))

    def ep4(eng, issue, ot, ps, stage, ch, auxap):
        issue(eng.activation(stage, ps, AF.Gelu, scale=sv[:, 96 + ot:97 + ot]))
    lin_pass(p, cn, wT=W["h2T"], Kd=4096, Nd=4096, src=sA, dst=sB,
             ep=Ep("s", 1, ep4))

    lns = make_ln_scratch(p)
    xbuf = p.sbuf("l0f_xres", [128, NCT * TC], F32R)

    def ep5(eng, issue, ot, ps, stage, ch, auxap):
        issue(eng.tensor_scalar(stage, ps, sv[:, 128 + ot:129 + ot], None,
                                ALU.mult))

    def fin(ch, out_sb, srcb):
        emit_resid_ln(p, cn, lns, ch=ch,
                      y_tiles=lambda ci: out_sb[:, ci * TC:(ci + 1) * TC],
                      xres=xin, out_dram=xout, xbuf=xbuf)

    lin_pass(p, cn, wT=W["w3T"], Kd=4096, Nd=C, src=sB, dst=None,
             ep=Ep("v", 1, ep5), finalize=fin)



def emit_l2_ffn(p, cn, W, xin, xout):
    nc = p.nc
    w_sb = p.sbuf("l2f_w", [128, NCT * C], F32R)
    y1 = p.sbuf("l2f_y1", [128, 2 * 4 * 128], F32R)
    y2 = p.sbuf("l2f_y2", [128, 3 * 4 * 64], F32)
    g1 = p.sbuf("l2f_g1", [32, 2 * 128], F32R)
    g2 = p.sbuf("l2f_g2", [16, 2 * 64], F32R)
    bias = p.sbuf("l2f_b", [128, 8], F32)
    ps1 = p.psum("l2f_ps1", [128, 128], F32)
    ps2 = p.psum("l2f_ps2", [128, 64], F32)

    with nc.Block() as blk:
        @blk.sync
        def _(e):
            for i, nmw in enumerate(["g1r", "g1i"]):
                ins = e.dma_start(g1[:, i * 128:(i + 1) * 128], W[nmw])
                p.inc(ins, "dsp", 16)
            for i, nmw in enumerate(["g2r", "g2i"]):
                ins = e.dma_start(g2[:, i * 64:(i + 1) * 64], W[nmw])
                p.inc(ins, "dsp", 16)
            ins = e.dma_start(bias[:], W["bias"])
            p.inc(ins, "dsp", 16)
            e.wait_ge(p.S["dsp"], p.c["dsp"])

    for x in range(2):
        for rt in range(4):
            with nc.Block() as blk:
                @blk.tensor
                def _(t):
                    ins = t.matmul(ps1[:], cn.tt_e1[:, rt * 128:(rt + 1) * 128],
                                   g1[:, x * 128:(x + 1) * 128],
                                   start=True, stop=True)
                    p.inc(ins, "pe")
                    ins = t.matmul(ps2[:], cn.tt_e2[:],
                                   g2[:, x * 64:(x + 1) * 64],
                                   start=True, stop=True)
                    p.inc(ins, "pe")

                pe_done = p.c["pe"]

                @blk.vector
                def _(v):
                    vi = vch(v, p)
                    v.wait_ge(p.S["pe"], pe_done)
                    vi(v.tensor_copy(y1[:, (x * 4 + rt) * 128:(x * 4 + rt + 1) * 128],
                                     ps1[:]))
                    vi(v.tensor_copy(y2[:, (x * 4 + rt) * 64:(x * 4 + rt + 1) * 64],
                                     ps2[:]))
                    if x == 1:
                        vi(v.tensor_scalar(
                            y2[:, (8 + rt) * 64:(8 + rt + 1) * 64],
                            ps2[:], -1.0, None, ALU.mult))

    quads = [
        (0, 0, 0, 0),
        (4, 0, 1, 2),
        (0, 512, 1, 1),
        (4, 512, 0, 0),
    ]
    with nc.Block() as blk:
        @blk.vector
        def _(v):
            vi = vch(v, p)
            for cig, obase, x1, x2 in quads:
                for rt in range(4):
                    ci = cig + rt
                    for e in range(16):
                        base_col = ci * C + obase + e
                        tview = w_sb[:].rearrange("p (a b) -> p a b", b=16)[
                            :, (base_col // 16):(base_col // 16) + 32, e:e + 1
                        ].rearrange("p a b -> p (a b)")
                        for c in range(4):
                            y1s = y1[:, (x1 * 4 + rt) * 128 + c * 32:
                                     (x1 * 4 + rt) * 128 + c * 32 + 32]
                            y2c = y2[:, x2 * 4 * 64 + rt * 64 + c * 16 + e:
                                     x2 * 4 * 64 + rt * 64 + c * 16 + e + 1]
                            if c == 0:
                                vi(v.tensor_scalar(tview, y1s, y2c, None, ALU.mult))
                            else:
                                vi(v.scalar_tensor_tensor(tview, y1s, y2c, tview,
                                                          ALU.mult, ALU.add))

    lns = make_ln_scratch(p)
    xbuf = p.sbuf("l2f_xres", [128, NCT * TC], F32R)

    def epb(eng, issue, ot, ps, stage, ch, auxap):
        issue(eng.tensor_scalar(stage, ps, bias[:, ot:ot + 1], None, ALU.add))

    def fin(ch, out_sb, srcb):
        emit_resid_ln(p, cn, lns, ch=ch,
                      y_tiles=lambda ci: out_sb[:, ci * TC:(ci + 1) * TC],
                      xres=xin, out_dram=xout, xbuf=xbuf)

    lin_pass(p, cn, w_sb=w_sb, Kd=C, Nd=C, src=xin, dst=None,
             ep=Ep("v", 1, epb), finalize=fin)



def emit_l2_attn(p, cn, W, xin, xout, dbg=None):
    nc = p.nc
    EPS_SSM = 1e-4
    win = p.sbuf("l2a_win", [128, NCT * 80], F32R)
    msel = p.sbuf("l2a_msel", [80, 48], F32R)
    wout = p.sbuf("l2a_wout", [16, C], F32R)
    convw = p.sbuf("l2a_cw", [128, 32], F32)
    convb = p.sbuf("l2a_cb", [128, 8], F32)
    b_in = p.sbuf("l2a_bin", [80, 16], F32R)
    b_out = p.sbuf("l2a_bout", [128, 8], F32)
    alog = p.sbuf("l2a_alog", [16, 1], F32)
    acons = p.sbuf("l2a_acons", [16, 8], F32)
    xch = p.sbuf("l2a_x", [128, NCT * (TC + 3)], F32R)
    xc = p.sbuf("l2a_xc", [128, NCT * TC], F32R)
    winT = p.sbuf("l2a_winT", [80, NCT * 128], F32R)
    wsel3 = p.sbuf("l2a_wsel3", [128, NCT * 48], F32R)
    bsel = p.sbuf("l2a_bsel", [16, 3], F32)
    sel = p.sbuf("l2a_sel", [16, 3 * TC], F32R)
    tmp = p.sbuf("l2a_tmp", [16, 6 * TC], F32R)
    out_sb = p.sbuf("l2a_out", [128, NCT * TC], F32R)
    xbuf = p.sbuf("l2a_xres", [128, NCT * TC], F32R)

    with nc.Block() as blk:
        @blk.sync
        def _(e):
            ins = e.dma_start(win[:].rearrange("p (ci o) -> p ci o", ci=NCT),
                              W["win"].rearrange("(ci p) o -> p ci o", p=128))
            p.inc(ins, "dsp", 16)
            for nmw, tile in [("msel", msel), ("wout", wout), ("convw", convw),
                              ("convb", convb), ("b_in", b_in), ("b_out", b_out),
                              ("alog", alog)]:
                ins = e.dma_start(tile[:], W[nmw])
                p.inc(ins, "dsp", 16)
            e.wait_ge(p.S["dsp"], p.c["dsp"])

    a_col = acons[:, 0:1]
    mask_c = acons[:, 1:2]
    onem_c = acons[:, 2:3]
    inv_c = acons[:, 3:4]
    state_c = acons[:, 4:5]
    scr = acons[:, 5:6]
    with nc.Block() as blk:
        @blk.scalar
        def _(s):
            si = sch(s, p)
            si(s.activation(a_col, alog[:], AF.Exp, scale=1.0))
        sm = p.c["s"]

        @blk.vector
        def _(v):
            vi = vch(v, p)
            v.wait_ge(p.S["s"], sm)
            vi(v.tensor_scalar(a_col, a_col, -1.0, None, ALU.mult))
            vi(v.tensor_scalar(scr, a_col, -1.0, None, ALU.mult))
            vi(v.tensor_max(scr, scr, a_col))
            vi(v.tensor_scalar(mask_c, scr, -1.0, EPS_SSM, ALU.mult, ALU.add))
        vm = p.c["v"]

        @blk.scalar
        def _(s):
            si = sch(s, p)
            s.wait_ge(p.S["v"], vm)
            si(s.sign(mask_c, mask_c))
            si(s.sign(scr, a_col))
        sm2 = p.c["s"]

        @blk.vector
        def _(v):
            vi = vch(v, p)
            v.wait_ge(p.S["s"], sm2)
            vi(v.tensor_scalar_max(mask_c, mask_c, 0.0))
            vi(v.tensor_scalar(onem_c, mask_c, -1.0, 1.0, ALU.mult, ALU.add))
            vi(v.tensor_scalar(scr, scr, EPS_SSM, EPS_SSM, ALU.mult, ALU.add))
            vi(v.tensor_mul(scr, scr, mask_c))
            vi(v.scalar_tensor_tensor(inv_c, a_col, onem_c, scr, ALU.mult, ALU.add))
            vi(v.reciprocal(inv_c, inv_c))

    p.frames.append(ExitStack())
    ps_tr = p.psum("l2a_ptr", [128, 128], F32)
    with nc.Block() as blk:
        pe0 = p.c["pe"]
        v0 = p.c["v"]

        @blk.tensor
        def _(t):
            for ci in range(NCT):
                if ci >= 1:
                    t.wait_ge(p.S["v"], v0 + ci)
                ins = t.transpose(ps_tr[0:80, :].bitcast(F32R),
                                  win[:, ci * 80:(ci + 1) * 80],
                                  cn.ident_r[:])
                p.inc(ins, "pe")

        @blk.vector
        def _(v):
            vi = vch(v, p)
            for ci in range(NCT):
                v.wait_ge(p.S["pe"], pe0 + ci + 1)
                vi(v.tensor_copy(winT[:, ci * 128:(ci + 1) * 128],
                                 ps_tr[0:80, :]))
    with nc.Block() as blk:
        pe0 = p.c["pe"]
        v0 = p.c["v"]

        @blk.tensor
        def _(t):
            for ci in range(NCT):
                if ci >= 1:
                    t.wait_ge(p.S["v"], v0 + ci)
                ins = t.matmul(ps_tr[:, 0:48], winT[:, ci * 128:(ci + 1) * 128],
                               msel[:], start=True, stop=True)
                p.inc(ins, "pe")

        @blk.vector
        def _(v):
            vi = vch(v, p)
            for ci in range(NCT):
                v.wait_ge(p.S["pe"], pe0 + ci + 1)
                vi(v.tensor_copy(wsel3[:, ci * 48:(ci + 1) * 48],
                                 ps_tr[:, 0:48]))
    for g in range(3):
        with nc.Block() as blk:
            @blk.tensor
            def _(t):
                ins = t.matmul(ps_tr[0:16, 0:16], msel[:, 16 * g:16 * g + 16],
                               b_in[:], start=True, stop=True)
                p.inc(ins, "pe")

            pm = p.c["pe"]

            @blk.vector
            def _(v):
                vi = vch(v, p)
                v.wait_ge(p.S["pe"], pm)
                vi(v.tensor_copy(bsel[:, g:g + 1], ps_tr[0:16, 0:1]))

    p.frames.pop().close()
    ps_sel3 = p.psum("l2a_pss", [16, 3 * TC], F32)
    ps_out = [p.psum("l2a_pso", [128, TC], F32)]
    lns = make_ln_scratch(p)

    for ch in range(NCH):
        t0, t1 = ch * TC, (ch + 1) * TC

        def xt(ci, off, w=TC):
            return xch[:, ci * (TC + 3) + off: ci * (TC + 3) + off + w]

        lo = max(t0 - 1, 0)
        hi = min(t1 + 2, T)
        off = lo - (t0 - 1)
        with nc.Block() as blk:
            @blk.sync
            def _(e):
                ins = load_cm_chunk(e, p, xch, off, xin, lo, hi, stride=TC + 3)
                p.inc(ins, "dsp", 16)
                e.wait_ge(p.S["dsp"], p.c["dsp"])

            @blk.vector
            def _(v):
                vi = vch(v, p)
                if ch == 0:
                    for ci in range(NCT):
                        vi(v.memset(xt(ci, 0, 1).bitcast(F32), 0.0))
                if ch == NCH - 1:
                    for ci in range(NCT):
                        vi(v.memset(xt(ci, TC + 1, 2).bitcast(F32), 0.0))

        with nc.Block() as blk:
            @blk.vector
            def _(v):
                vi = vch(v, p)
                for ci in range(NCT):
                    o = xc[:, ci * TC:(ci + 1) * TC]
                    vi(v.tensor_scalar(o, xt(ci, 0), convw[:, ci * 4:ci * 4 + 1],
                                       None, ALU.mult))
                    for tap in range(1, 4):
                        vi(v.scalar_tensor_tensor(
                            o, xt(ci, tap), convw[:, ci * 4 + tap:ci * 4 + tap + 1],
                            o, ALU.mult, ALU.add))
                    vi(v.tensor_scalar(o, o, convb[:, ci:ci + 1], None, ALU.add))
                if ch == NCH - 1:
                    for ci in range(NCT):
                        vi(v.memset(xc[:, (ci + 1) * TC - 1:(ci + 1) * TC]
                                    .bitcast(F32), 0.0))

        with nc.Block() as blk:
            pe0 = p.c["pe"]

            @blk.tensor
            def _(t):
                for g in range(3):
                    for ci in range(NCT):
                        ins = t.matmul(ps_sel3[:, g * TC:(g + 1) * TC],
                                       wsel3[:, ci * 48 + g * 16:ci * 48 + g * 16 + 16],
                                       xc[:, ci * TC:(ci + 1) * TC],
                                       start=(ci == 0), stop=(ci == NCT - 1))
                    p.inc(ins, "pe")

            pm = p.c["pe"]

            @blk.vector
            def _(v):
                vi = vch(v, p)
                v.wait_ge(p.S["pe"], pm)
                for g in range(3):
                    vi(v.tensor_scalar(sel[:, g * TC:(g + 1) * TC],
                                       ps_sel3[:, g * TC:(g + 1) * TC],
                                       bsel[:, g:g + 1], None, ALU.add))

        delta = sel[:, 0:TC]
        Bin = sel[:, TC:2 * TC]
        Cc = sel[:, 2 * TC:3 * TC]
        if dbg is not None:
            with nc.Block() as blk:
                @blk.gpsimd
                def _(g):
                    ins = g.dma_start(dbg["sel"][:, ch * 3 * TC:(ch + 1) * 3 * TC], sel[:])
                    p.inc(ins, "dpool", 16)
                    ins = g.dma_start(dbg["xc"].rearrange("(ci p) t -> p ci t", p=128)[:, :, ch * TC:(ch + 1) * TC],
                                      xc[:].rearrange("p (ci t) -> p ci t", ci=NCT))
                    p.inc(ins, "dpool", 16)
                    g.wait_ge(p.S["dpool"], p.c["dpool"])
        sp = tmp[:, 0 * TC:1 * TC]
        t2 = tmp[:, 1 * TC:2 * TC]
        abar = tmp[:, 2 * TC:3 * TC]
        bbar = tmp[:, 3 * TC:4 * TC]
        hsc = tmp[:, 4 * TC:5 * TC]
        t3 = tmp[:, 5 * TC:6 * TC]

        with nc.Block() as blk:
            @blk.vector
            def _(v):
                vi = vch(v, p)
                vi(v.tensor_scalar(t2, delta, -1.0, None, ALU.mult))
                vi(v.tensor_max(t2, t2, delta))
            vm3 = p.c["v"]

            @blk.scalar
            def _(s):
                si = sch(s, p)
                s.wait_ge(p.S["v"], vm3)
                si(s.activation(t2, t2, AF.Exp, scale=-1.0))
                si(s.activation(t2, t2, AF.Ln, bias=1.0))
            sm3 = p.c["s"]

            @blk.vector
            def _(v):
                vi = vch(v, p)
                v.wait_ge(p.S["s"], sm3)
                vi(v.tensor_scalar(sp, delta, 0.0, None, ALU.max))
                vi(v.tensor_add(sp, sp, t2))
                vi(v.tensor_scalar(sp, sp, a_col, None, ALU.mult))
            vm4 = p.c["v"]

            @blk.scalar
            def _(s):
                si = sch(s, p)
                s.wait_ge(p.S["v"], vm4)
                si(s.activation(abar, sp, AF.Exp))
            sm4 = p.c["s"]

            @blk.vector
            def _(v):
                vi = vch(v, p)
                v.wait_ge(p.S["s"], sm4)
                vi(v.tensor_scalar(t2, sp, 1.0 / 6.0, 0.5, ALU.mult, ALU.add))
                vi(v.tensor_mul(t2, t2, sp))
                vi(v.tensor_scalar(t2, t2, 1.0, None, ALU.add))
                vi(v.tensor_scalar(t3, abar, -1.0, None, ALU.add))
                vi(v.tensor_scalar(t3, t3, inv_c, None, ALU.mult))
                vi(v.tensor_scalar(t2, t2, mask_c, None, ALU.mult))
                vi(v.scalar_tensor_tensor(t3, t3, onem_c, t2, ALU.mult, ALU.add))
                vi(v.tensor_mul(bbar, t3, Bin))
                vi(v.tensor_tensor_scan(hsc, abar, bbar,
                                        0.0 if ch == 0 else state_c,
                                        ALU.mult, ALU.add))
                vi(v.tensor_copy(state_c, hsc[:, TC - 1:TC]))
                vi(v.tensor_mul(hsc, hsc, Cc))
        if dbg is not None:
            with nc.Block() as blk:
                @blk.gpsimd
                def _(g):
                    ins = g.dma_start(dbg["h"][:, ch * TC:(ch + 1) * TC], hsc)
                    p.inc(ins, "dpool", 16)
                    ins = g.dma_start(dbg["abar"][:, ch * TC:(ch + 1) * TC], abar)
                    p.inc(ins, "dpool", 16)
                    ins = g.dma_start(dbg["bbar"][:, ch * TC:(ch + 1) * TC], bbar)
                    p.inc(ins, "dpool", 16)
                    g.wait_ge(p.S["dpool"], p.c["dpool"])

        with nc.Block() as blk:
            pe0 = p.c["pe"]
            v0 = p.c["v"]

            @blk.tensor
            def _(t):
                for ot in range(NCT):
                    if ot >= 1:
                        t.wait_ge(p.S["v"], v0 + ot)
                    ins = t.matmul(ps_out[0][:],
                                   wout[:, ot * 128:(ot + 1) * 128],
                                   hsc, start=True, stop=True)
                    p.inc(ins, "pe")

            @blk.vector
            def _(v):
                vi = vch(v, p)
                for ot in range(NCT):
                    v.wait_ge(p.S["pe"], pe0 + ot + 1)
                    vi(v.tensor_scalar(out_sb[:, ot * TC:(ot + 1) * TC],
                                       ps_out[0][:], b_out[:, ot:ot + 1],
                                       None, ALU.add))

        emit_resid_ln(p, cn, lns, ch=ch,
                      y_tiles=lambda ci: out_sb[:, ci * TC:(ci + 1) * TC],
                      xres=xin, out_dram=xout, xbuf=xbuf)



def ep_elu1(bias_tile):
    def vpre(v, vi, ot, ps, stage, ch, aux, scr):
        vi(v.tensor_scalar(stage, ps, bias_tile[:, ot:ot + 1], None, ALU.add))
        vi(v.tensor_scalar_min(scr, stage, 0.0))

    def fs(s, si, ot, ps, stage, ch, aux, scr):
        si(s.activation(scr, scr, AF.Exp))

    def vpost(v, vi, ot, ps, stage, ch, aux, scr):
        vi(v.tensor_scalar_max(stage, stage, 0.0))
        vi(v.tensor_add(stage, stage, scr))

    return Ep3(2, 1, 2, vpre, fs, vpost)


def emit_l1_attn(p, cn, W, xin, xout):
    nc = p.nc
    H, DH = 16, 64
    Qb = p.dram("l1a_Q", [C, T], F32R)
    Kb = p.dram("l1a_K", [C, T], F32R)
    Vb = p.dram("l1a_V", [C, T], F32R)
    xa = p.dram("l1a_xa", [C, T], F32R)
    y1b = p.dram("l1a_y1", [4096, T], F32R)

    vecs = p.sbuf("l1a_vecs", [128, 9 * 8 + 32], F32)

    def vc(i):
        return vecs[:, 8 * i:8 * (i + 1)]
    bq, bk, bv, bo, b2, g1, be1, g2, be2 = [vc(i) for i in range(9)]
    b1 = vecs[:, 72:104]
    with nc.Block() as blk:
        @blk.sync
        def _(e):
            for i, nmw in enumerate(["bq", "bk", "bv", "bo", "b2", "g1", "be1",
                                     "g2", "be2"]):
                ins = e.dma_start(vc(i), W[nmw])
                p.inc(ins, "dsp", 16)
            ins = e.dma_start(b1, W["b1"])
            p.inc(ins, "dsp", 16)
            e.wait_ge(p.S["dsp"], p.c["dsp"])

    lin_pass(p, cn, wT=W["wq"], Kd=C, Nd=C, src=xin, dst=Qb, ep=ep_elu1(bq))
    lin_pass(p, cn, wT=W["wk"], Kd=C, Nd=C, src=xin, dst=Kb, ep=ep_elu1(bk))

    def epv(eng, issue, ot, ps, stage, ch, aux):
        issue(eng.tensor_scalar(stage, ps, bv[:, ot:ot + 1], None, ALU.add))
    lin_pass(p, cn, wT=W["wv"], Kd=C, Nd=C, src=xin, dst=Vb, ep=Ep("v", 1, epv))

    with p.frame():
        kvacc = p.sbuf("l1a_kv", [128, 1024], F32)
        ksum = p.sbuf("l1a_ks", [128, 8], F32)
        kred = p.sbuf("l1a_kred", [128, 8], F32)
        ksr = p.sbuf("l1a_ksr", [128, 8], F32R)
        kvr = p.sbuf("l1a_kvr", [128, 1024], F32R)
        p.frames.append(ExitStack())
        kch = p.sbuf("l1a_kch", [128, NCT * TC], F32R)
        vch_ = p.sbuf("l1a_vch", [128, NCT * TC], F32R)
        ktb = p.sbuf("l1a_kt", [128, 4 * C], F32R)
        vtb = p.sbuf("l1a_vt", [128, 4 * C], F32R)
        ptr = p.psum("l1a_ptr", [128, 128], F32)
        pkv = p.psum("l1a_pkv", [128, 128], F32)

        with nc.Block() as blk:
            @blk.vector
            def _(v):
                vi = vch(v, p)
                vi(v.memset(kvacc[:], 0.0))
                vi(v.memset(ksum[:], 0.0))

        for ch in range(NCH):
            t0, t1 = ch * TC, (ch + 1) * TC
            with nc.Block() as blk:
                @blk.sync
                def _(e):
                    ins = load_cm_chunk(e, p, kch, 0, Kb, t0, t1)
                    p.inc(ins, "dsp", 16)
                    ins = load_cm_chunk(e, p, vch_, 0, Vb, t0, t1)
                    p.inc(ins, "dsp", 16)
                    e.wait_ge(p.S["dsp"], p.c["dsp"])

            with nc.Block() as blk:
                @blk.vector
                def _(v):
                    vi = vch(v, p)
                    for ci in range(NCT):
                        vi(v.reduce_sum(kred[:, ci:ci + 1],
                                        kch[:, ci * TC:(ci + 1) * TC], axis=AX.X))
                    vi(v.tensor_add(ksum[:], ksum[:], kred[:]))

            with nc.Block() as blk:
                pe0 = p.c["pe"]
                v0 = p.c["v"]

                @blk.tensor
                def _(t):
                    n = 0
                    for src_sb, dst_sb in ((kch, ktb), (vch_, vtb)):
                        for ci in range(NCT):
                            for st in range(4):
                                if n >= 1:
                                    t.wait_ge(p.S["v"], v0 + n)
                                ins = t.transpose(
                                    ptr[:].bitcast(F32R),
                                    src_sb[:, ci * TC + st * 128:
                                           ci * TC + (st + 1) * 128],
                                    cn.ident_r[:])
                                p.inc(ins, "pe")
                                n += 1

                @blk.vector
                def _(v):
                    vi = vch(v, p)
                    n = 0
                    for src_sb, dst_sb in ((kch, ktb), (vch_, vtb)):
                        for ci in range(NCT):
                            for st in range(4):
                                v.wait_ge(p.S["pe"], pe0 + n + 1)
                                vi(v.tensor_copy(
                                    dst_sb[:, st * C + ci * 128:
                                           st * C + (ci + 1) * 128],
                                    ptr[:]))
                                n += 1

            with nc.Block() as blk:
                pe0 = p.c["pe"]
                v0 = p.c["v"]

                @blk.tensor
                def _(t):
                    for hp in range(8):
                        if hp >= 1:
                            t.wait_ge(p.S["v"], v0 + hp)
                        for st in range(4):
                            ins = t.matmul(
                                pkv[:],
                                ktb[:, st * C + hp * 128: st * C + (hp + 1) * 128],
                                vtb[:, st * C + hp * 128: st * C + (hp + 1) * 128],
                                start=(st == 0), stop=(st == 3))
                        p.inc(ins, "pe")

                @blk.vector
                def _(v):
                    vi = vch(v, p)
                    for hp in range(8):
                        v.wait_ge(p.S["pe"], pe0 + hp + 1)
                        sl = kvacc[:, hp * 128:(hp + 1) * 128]
                        vi(v.tensor_add(sl, sl, pkv[:]))

        p.frames.pop().close()
        with nc.Block() as blk:
            @blk.vector
            def _(v):
                vi = vch(v, p)
                for hp in range(8):
                    vi(v.memset(kvacc[0:64, hp * 128 + 64:hp * 128 + 128], 0.0))
                    vi(v.memset(kvacc[64:128, hp * 128:hp * 128 + 64], 0.0))
                vi(v.tensor_copy(kvr[:], kvacc[:]))
                vi(v.tensor_copy(ksr[:], ksum[:]))

        wo_sb = p.sbuf("l1a_wo", [128, NCT * C], F32R)
        with nc.Block() as blk:
            @blk.sync
            def _(e):
                ins = e.dma_start(
                    wo_sb[:].rearrange("p (ci o) -> p ci o", ci=NCT),
                    W["wo"].rearrange("(ci p) o -> p ci o", p=128))
                p.inc(ins, "dsp", 16)
                e.wait_ge(p.S["dsp"], p.c["dsp"])

        qch = p.sbuf("l1a_qch", [128, NCT * TC], F32R)
        qs = p.sbuf("l1a_qs", [128, NCT * TC], F32R)
        zbuf = p.sbuf("l1a_z", [1, 16 * TC], F32R)
        zrepb = p.sbuf("l1a_zrep", [128, TC], F32)
        osb = p.sbuf("l1a_o", [128, NCT * TC], F32R)
        out_sb = p.sbuf("l1a_attn", [128, NCT * TC], F32R)
        xbuf = p.sbuf("l1a_xres", [128, NCT * TC], F32R)
        pz = p.psum("l1a_pz", [1, TC], F32)
        pzr = p.psum("l1a_pzr", [128, TC], F32)
        po = p.psum("l1a_po", [128, TC], F32)
        pat = p.psum("l1a_pat", [128, TC], F32)
        lns = make_ln_scratch(p)

        for ch in range(NCH):
            t0, t1 = ch * TC, (ch + 1) * TC
            with nc.Block() as blk:
                @blk.sync
                def _(e):
                    ins = load_cm_chunk(e, p, qch, 0, Qb, t0, t1)
                    p.inc(ins, "dsp", 16)
                    e.wait_ge(p.S["dsp"], p.c["dsp"])

            with nc.Block() as blk:
                pe0 = p.c["pe"]
                v0 = p.c["v"]

                @blk.tensor
                def _(t):
                    for h in range(16):
                        ci, half = h // 2, h % 2
                        if h >= 1:
                            t.wait_ge(p.S["v"], v0 + 2 * h)
                        ins = t.matmul(
                            pz[:],
                            ksr[half * 64:(half + 1) * 64, ci:ci + 1],
                            qch[half * 64:(half + 1) * 64,
                                ci * TC:(ci + 1) * TC],
                            start=True, stop=True)
                        p.inc(ins, "pe")

                @blk.vector
                def _(v):
                    vi = vch(v, p)
                    for h in range(16):
                        v.wait_ge(p.S["pe"], pe0 + h + 1)
                        sl = zbuf[0:1, h * TC:(h + 1) * TC]
                        vi(v.tensor_scalar(sl, pz[:], 1e-6, None, ALU.add))
                        with nc.allow_low_precision("f32r bits are fp32"):
                            vi(v.reciprocal(sl, sl))


            with nc.Block() as blk:
                pe0 = p.c["pe"]
                v0 = p.c["v"]

                @blk.tensor
                def _(t):
                    for ci in range(NCT):
                        if ci >= 1:
                            t.wait_ge(p.S["v"], v0 + 2 * ci)
                        h0, h1 = 2 * ci, 2 * ci + 1
                        ins = t.matmul(pzr[:], cn.sel_lo[:],
                                       zbuf[0:1, h0 * TC:(h0 + 1) * TC],
                                       start=True, stop=False)
                        ins = t.matmul(pzr[:], cn.sel_hi[:],
                                       zbuf[0:1, h1 * TC:(h1 + 1) * TC],
                                       start=False, stop=True)
                        p.inc(ins, "pe")

                @blk.vector
                def _(v):
                    vi = vch(v, p)
                    for ci in range(NCT):
                        v.wait_ge(p.S["pe"], pe0 + ci + 1)
                        vi(v.tensor_copy(zrepb[:], pzr[:]))
                        vi(v.tensor_mul(qs[:, ci * TC:(ci + 1) * TC],
                                        qch[:, ci * TC:(ci + 1) * TC], zrepb[:]))

            with nc.Block() as blk:
                pe0 = p.c["pe"]
                v0 = p.c["v"]

                @blk.tensor
                def _(t):
                    for ci in range(NCT):
                        if ci >= 1:
                            t.wait_ge(p.S["v"], v0 + ci)
                        ins = t.matmul(po[:],
                                       kvr[:, ci * 128:(ci + 1) * 128],
                                       qs[:, ci * TC:(ci + 1) * TC],
                                       start=True, stop=True)
                        p.inc(ins, "pe")

                @blk.vector
                def _(v):
                    vi = vch(v, p)
                    for ci in range(NCT):
                        v.wait_ge(p.S["pe"], pe0 + ci + 1)
                        vi(v.tensor_copy(osb[:, ci * TC:(ci + 1) * TC], po[:]))

            with nc.Block() as blk:
                pe0 = p.c["pe"]
                v0 = p.c["v"]

                @blk.tensor
                def _(t):
                    for ot in range(NCT):
                        if ot >= 1:
                            t.wait_ge(p.S["v"], v0 + ot)
                        for ci in range(NCT):
                            ins = t.matmul(pat[:],
                                           wo_sb[:, ci * C + ot * 128:
                                                 ci * C + ot * 128 + 128],
                                           osb[:, ci * TC:(ci + 1) * TC],
                                           start=(ci == 0), stop=(ci == NCT - 1))
                        p.inc(ins, "pe")

                @blk.vector
                def _(v):
                    vi = vch(v, p)
                    for ot in range(NCT):
                        v.wait_ge(p.S["pe"], pe0 + ot + 1)
                        vi(v.tensor_scalar(out_sb[:, ot * TC:(ot + 1) * TC],
                                           pat[:], bo[:, ot:ot + 1], None,
                                           ALU.add))

            emit_resid_ln(p, cn, lns, ch=ch,
                          y_tiles=lambda ci: out_sb[:, ci * TC:(ci + 1) * TC],
                          xres=xin, out_dram=xa, xbuf=xbuf,
                          g_ap=lambda ci: g1[:, ci:ci + 1],
                          b_ap=lambda ci: be1[:, ci:ci + 1])

    def ep4(eng, issue, ot, ps, stage, ch, aux):
        issue(eng.activation(stage, ps, AF.Gelu, bias=b1[:, ot:ot + 1]))
    lin_pass(p, cn, wT=W["w1"], Kd=C, Nd=4096, src=xa, dst=y1b, ep=Ep("s", 1, ep4))

    with p.frame():
        lns = make_ln_scratch(p)
        xbuf = p.sbuf("l1a_xres2", [128, NCT * TC], F32R)

        def ep5(eng, issue, ot, ps, stage, ch, aux):
            issue(eng.tensor_scalar(stage, ps, b2[:, ot:ot + 1], None, ALU.add))

        def fin(ch, out_sb, srcb):
            emit_resid_ln(p, cn, lns, ch=ch,
                          y_tiles=lambda ci: out_sb[:, ci * TC:(ci + 1) * TC],
                          xres=xa, out_dram=None, xbuf=xbuf,
                          g_ap=lambda ci: g2[:, ci:ci + 1],
                          b_ap=lambda ci: be2[:, ci:ci + 1])
            emit_resid_ln(p, cn, lns, ch=ch,
                          y_tiles=lambda ci: out_sb[:, ci * TC:(ci + 1) * TC],
                          xres=xin, out_dram=xout, xbuf=xbuf)

        lin_pass(p, cn, wT=W["w2"], Kd=4096, Nd=C, src=y1b, dst=None,
                 ep=Ep("v", 1, ep5), finalize=fin)



def _proj_stream(p, cn, *, wT, Kd, Nd, x_tiles, ep_fn, ep_eng, ep_nops, psms,
                 wcols):
    nc = p.nc
    kt, nt = Kd // 128, Nd // 128
    d0 = p.c["dsp"]
    pe0 = p.c["pe"]
    e0 = p.c[ep_eng]
    wready = [0] * nt
    with nc.Block() as blk:
        @blk.sync
        def _(e):
            for k in range(nt):
                if k >= 2:
                    e.wait_ge(p.S["pe"], pe0 + k - 1)
                ins = e.dma_start(
                    wcols[k % 2][:, 0:kt * 128].rearrange(
                        "p (ci o) -> p ci o", ci=kt),
                    wT.rearrange("(ci p) o -> p ci o", p=128)
                    [:, :, k * 128:(k + 1) * 128])
                p.inc(ins, "dsp", 16)
                wready[k] = p.c["dsp"]

        @blk.tensor
        def _(t):
            for ot in range(nt):
                t.wait_ge(p.S["dsp"], wready[ot])
                if ot >= 2:
                    t.wait_ge(p.S[ep_eng], e0 + ep_nops * (ot - 1))
                for ci in range(kt):
                    ins = t.matmul(psms[ot % 2][:],
                                   wcols[ot % 2][:, ci * 128:(ci + 1) * 128],
                                   x_tiles(ci), start=(ci == 0),
                                   stop=(ci == kt - 1))
                p.inc(ins, "pe")

        def _ep(eng):
            issue = (sch if ep_eng == "s" else vch)(eng, p)
            for ot in range(nt):
                eng.wait_ge(p.S["pe"], pe0 + ot + 1)
                ep_fn(eng, issue, ot, psms[ot % 2][:])

        if ep_eng == "s":
            @blk.scalar
            def _(s):
                _ep(s)
        else:
            @blk.vector
            def _(v):
                _ep(v)


def emit_l0_attn(p, cn, W, x0, xout):
    nc = p.nc
    hb = p.dram("l0a_h", [C, T], F32R)
    wkvb = p.dram("l0a_wkv", [C, T], F32R)
    zb = p.dram("l0a_z", [C, T], F32R)
    h2b = p.dram("l0a_h2", [C, T], F32R)
    kkb = p.dram("l0a_kk", [4096, T], F32R)
    srb = p.dram("l0a_sr", [C, T], F32R)

    vecs = p.sbuf("l0a_vecs", [128, 11 * 8], F32)
    names = ["ln1w", "ln1b", "ln2w", "ln2b", "decay", "tfirst", "tmk", "tmv",
             "tmr", "ftmk", "ftmr"]

    def vc(nmw):
        i = names.index(nmw)
        return vecs[:, 8 * i:8 * (i + 1)]

    lamc = p.sbuf("l0a_lam", [128, 8], F32)
    euc = p.sbuf("l0a_eu", [128, 8], F32)
    lam_rep = p.sbuf("l0a_lamrep", [128, NCT * TC], F32)
    stA = p.sbuf("l0a_stA", [128, 16], F32)

    with nc.Block() as blk:
        @blk.sync
        def _(e):
            for nmw in names:
                ins = e.dma_start(vc(nmw), W[nmw])
                p.inc(ins, "dsp", 16)
            e.wait_ge(p.S["dsp"], p.c["dsp"])

    with nc.Block() as blk:
        @blk.scalar
        def _(s):
            si = sch(s, p)
            si(s.activation(lamc[:], vc("decay"), AF.Exp))
            si(s.activation(euc[:], vc("tfirst"), AF.Exp))
        sm = p.c["s"]

        @blk.vector
        def _(v):
            vi = vch(v, p)
            v.wait_ge(p.S["s"], sm)
            vi(v.tensor_scalar(lamc[:], lamc[:], -1.0, None, ALU.mult))
        vm = p.c["v"]

        @blk.scalar
        def _(s):
            si = sch(s, p)
            s.wait_ge(p.S["v"], vm)
            si(s.activation(lamc[:], lamc[:], AF.Exp))
        sm2 = p.c["s"]

        @blk.vector
        def _(v):
            vi = vch(v, p)
            v.wait_ge(p.S["s"], sm2)
            vi(v.memset(stA[:], 0.0))
            vi(v.memset(lam_rep[:], 1.0))
            for ci in range(NCT):
                vi(v.tensor_scalar(lam_rep[:, ci * TC:(ci + 1) * TC],
                                   lam_rep[:, ci * TC:(ci + 1) * TC],
                                   lamc[:, ci:ci + 1], None, ALU.mult))

    with p.frame():
        xch = p.sbuf("l0a_x", [128, NCT * TC], F32R)
        hbuf = p.sbuf("l0a_hb", [128, NCT * (TC + 1)], F32R)
        ekbuf = p.sbuf("l0a_ek", [128, NCT * (TC + 1)], F32R)
        ekubuf = p.sbuf("l0a_eku", [128, NCT * (TC + 1)], F32R)
        xmix = p.sbuf("l0a_xm", [128, NCT * TC], F32R)
        vsb = p.sbuf("l0a_v", [128, NCT * TC], F32R)
        pa = p.sbuf("l0a_pa", [128, NCT * TC], F32R)
        pb = p.sbuf("l0a_pb", [128, NCT * TC], F32R)
        wcols = [p.sbuf("l0a_wc", [128, NCT * 128], F32R) for _ in range(2)]
        psms = [p.psum("l0a_ps", [128, TC], F32) for _ in range(2)]
        lns = make_ln_scratch(p)

        def hv(ci, off=1, w=TC):
            return hbuf[:, ci * (TC + 1) + off: ci * (TC + 1) + off + w]

        def ekv(ci, off=1, w=TC):
            return ekbuf[:, ci * (TC + 1) + off: ci * (TC + 1) + off + w]

        def ekuv(ci, off=1, w=TC):
            return ekubuf[:, ci * (TC + 1) + off: ci * (TC + 1) + off + w]

        for ch in range(NCH):
            t0, t1 = ch * TC, (ch + 1) * TC
            with nc.Block() as blk:
                @blk.sync
                def _(e):
                    ins = load_cm_chunk(e, p, xch, 0, x0, t0, t1)
                    p.inc(ins, "dsp", 16)
                    e.wait_ge(p.S["dsp"], p.c["dsp"])

                @blk.vector
                def _(v):
                    vi = vch(v, p)
                    for ci in range(NCT):
                        if ch == 0:
                            vi(v.memset(hv(ci, 0, 1).bitcast(F32), 0.0))
                            vi(v.memset(ekv(ci, 0, 1).bitcast(F32), 0.0))
                            vi(v.memset(ekuv(ci, 0, 1).bitcast(F32), 0.0))
                        else:
                            vi(v.tensor_copy(hv(ci, 0, 1), hv(ci, TC, 1)))
                            vi(v.tensor_copy(ekv(ci, 0, 1), ekv(ci, TC, 1)))
                            vi(v.tensor_copy(ekuv(ci, 0, 1), ekuv(ci, TC, 1)))

            ln_chunk(p, cn, z_tiles=lambda ci: xch[:, ci * TC:(ci + 1) * TC],
                     out_tiles=lambda ci: hv(ci),
                     g_ap=lambda ci: vc("ln1w")[:, ci:ci + 1],
                     b_ap=lambda ci: vc("ln1b")[:, ci:ci + 1], **lns)

            with nc.Block() as blk:
                @blk.gpsimd
                def _(g):
                    for ci in range(NCT):
                        ins = g.dma_start(dram_cm(hb)[:, ci, t0:t1], hv(ci))
                        p.inc(ins, "dpool", 16)

            with nc.Block() as blk:
                @blk.vector
                def _(v):
                    vi = vch(v, p)
                    for ci in range(NCT):
                        vi(v.tensor_sub(pa[:, ci * TC:(ci + 1) * TC],
                                        hv(ci), hv(ci, 0)))
                        vi(v.scalar_tensor_tensor(
                            xmix[:, ci * TC:(ci + 1) * TC],
                            pa[:, ci * TC:(ci + 1) * TC],
                            vc("tmk")[:, ci:ci + 1], hv(ci, 0),
                            ALU.mult, ALU.add))

            def ep_ek(eng, issue, ot, ps):
                issue(eng.activation(ekv(ot), ps, AF.Exp))
            _proj_stream(p, cn, wT=W["kwT"], Kd=C, Nd=C,
                         x_tiles=lambda ci: xmix[:, ci * TC:(ci + 1) * TC],
                         ep_fn=ep_ek, ep_eng="s", ep_nops=1, psms=psms,
                         wcols=wcols)

            with nc.Block() as blk:
                @blk.vector
                def _(v):
                    vi = vch(v, p)
                    for ci in range(NCT):
                        vi(v.scalar_tensor_tensor(
                            xmix[:, ci * TC:(ci + 1) * TC],
                            pa[:, ci * TC:(ci + 1) * TC],
                            vc("tmv")[:, ci:ci + 1], hv(ci, 0),
                            ALU.mult, ALU.add))

            def ep_v(eng, issue, ot, ps):
                issue(eng.tensor_copy(vsb[:, ot * TC:(ot + 1) * TC], ps))
            _proj_stream(p, cn, wT=W["vwT"], Kd=C, Nd=C,
                         x_tiles=lambda ci: xmix[:, ci * TC:(ci + 1) * TC],
                         ep_fn=ep_v, ep_eng="v", ep_nops=1, psms=psms,
                         wcols=wcols)

            with nc.Block() as blk:
                @blk.vector
                def _(v):
                    vi = vch(v, p)
                    for ci in range(NCT):
                        vi(v.tensor_mul(ekuv(ci), ekv(ci),
                                        vsb[:, ci * TC:(ci + 1) * TC]))
                    for ci in range(NCT):
                        vi(v.tensor_tensor_scan(
                            pa[:, ci * TC:(ci + 1) * TC],
                            lam_rep[:, ci * TC:(ci + 1) * TC],
                            ekuv(ci, 0), stA[:, ci:ci + 1],
                            ALU.mult, ALU.add))
                        vi(v.tensor_tensor_scan(
                            pb[:, ci * TC:(ci + 1) * TC],
                            lam_rep[:, ci * TC:(ci + 1) * TC],
                            ekv(ci, 0), stA[:, 8 + ci:9 + ci],
                            ALU.mult, ALU.add))
                        vi(v.scalar_tensor_tensor(
                            stA[:, ci:ci + 1],
                            pa[:, (ci + 1) * TC - 1:(ci + 1) * TC],
                            lamc[:, ci:ci + 1], ekuv(ci, TC, 1),
                            ALU.mult, ALU.add))
                        vi(v.scalar_tensor_tensor(
                            stA[:, 8 + ci:9 + ci],
                            pb[:, (ci + 1) * TC - 1:(ci + 1) * TC],
                            lamc[:, ci:ci + 1], ekv(ci, TC, 1),
                            ALU.mult, ALU.add))
                    for ci in range(NCT):
                        vi(v.scalar_tensor_tensor(
                            pa[:, ci * TC:(ci + 1) * TC], ekuv(ci),
                            euc[:, ci:ci + 1], pa[:, ci * TC:(ci + 1) * TC],
                            ALU.mult, ALU.add))
                        vi(v.scalar_tensor_tensor(
                            pb[:, ci * TC:(ci + 1) * TC], ekv(ci),
                            euc[:, ci:ci + 1], pb[:, ci * TC:(ci + 1) * TC],
                            ALU.mult, ALU.add))
                        with nc.allow_low_precision("f32r bits are fp32"):
                            vi(v.reciprocal(pb[:, ci * TC:(ci + 1) * TC],
                                            pb[:, ci * TC:(ci + 1) * TC]))
                        vi(v.tensor_mul(pa[:, ci * TC:(ci + 1) * TC],
                                        pa[:, ci * TC:(ci + 1) * TC],
                                        pb[:, ci * TC:(ci + 1) * TC]))

            with nc.Block() as blk:
                @blk.gpsimd
                def _(g):
                    for ci in range(NCT):
                        ins = g.dma_start(dram_cm(wkvb)[:, ci, t0:t1],
                                          pa[:, ci * TC:(ci + 1) * TC])
                        p.inc(ins, "dpool", 16)
                    g.wait_ge(p.S["dpool"], p.c["dpool"])

    with p.frame():
        hbuf = p.sbuf("l0a2_hb", [128, NCT * (TC + 1)], F32R)
        xr = p.sbuf("l0a2_xr", [128, NCT * TC], F32R)
        srw = p.sbuf("l0a2_sr", [128, NCT * TC], F32R)
        out_sb = p.sbuf("l0a2_out", [128, NCT * TC], F32R)
        xch = p.sbuf("l0a2_x", [128, NCT * TC], F32R)
        w_rw = p.sbuf("l0a2_wrw", [128, NCT * C], F32R)
        w_ow = p.sbuf("l0a2_wow", [128, NCT * C], F32R)
        psms = [p.psum("l0a2_ps", [128, TC], F32) for _ in range(2)]

        def hv(ci, off=1, w=TC):
            return hbuf[:, ci * (TC + 1) + off: ci * (TC + 1) + off + w]

        with nc.Block() as blk:
            @blk.sync
            def _(e):
                for wsb, nmw in ((w_rw, "rwT"), (w_ow, "owT")):
                    ins = e.dma_start(
                        wsb[:].rearrange("p (ci o) -> p ci o", ci=NCT),
                        W[nmw].rearrange("(ci p) o -> p ci o", p=128))
                    p.inc(ins, "dsp", 16)
                e.wait_ge(p.S["dsp"], p.c["dsp"])

        for ch in range(NCH):
            t0, t1 = ch * TC, (ch + 1) * TC
            lo = max(t0 - 1, 0)
            off = 1 - (t0 - lo)
            with nc.Block() as blk:
                @blk.sync
                def _(e):
                    ins = load_cm_chunk(e, p, hbuf, off, hb, lo, t1,
                                        stride=TC + 1)
                    p.inc(ins, "dsp", 16)
                    ins = load_cm_chunk(e, p, srw, 0, wkvb, t0, t1)
                    p.inc(ins, "dsp", 16)
                    ins = load_cm_chunk(e, p, xch, 0, x0, t0, t1)
                    p.inc(ins, "dsp", 16)
                    e.wait_ge(p.S["dsp"], p.c["dsp"])

                if ch == 0:
                    @blk.vector
                    def _(v):
                        vi = vch(v, p)
                        for ci in range(NCT):
                            vi(v.memset(hv(ci, 0, 1).bitcast(F32), 0.0))

            with nc.Block() as blk:
                @blk.vector
                def _(v):
                    vi = vch(v, p)
                    for ci in range(NCT):
                        vi(v.tensor_sub(xr[:, ci * TC:(ci + 1) * TC],
                                        hv(ci), hv(ci, 0)))
                        vi(v.scalar_tensor_tensor(
                            xr[:, ci * TC:(ci + 1) * TC],
                            xr[:, ci * TC:(ci + 1) * TC],
                            vc("tmr")[:, ci:ci + 1], hv(ci, 0),
                            ALU.mult, ALU.add))

            with nc.Block() as blk:
                pe0 = p.c["pe"]
                s0 = p.c["s"]
                v0 = p.c["v"]

                @blk.tensor
                def _(t):
                    for ot in range(NCT):
                        if ot >= 2:
                            t.wait_ge(p.S["v"], v0 + ot - 1)
                        for ci in range(NCT):
                            ins = t.matmul(psms[ot % 2][:],
                                           w_rw[:, ci * C + ot * 128:
                                                ci * C + ot * 128 + 128],
                                           xr[:, ci * TC:(ci + 1) * TC],
                                           start=(ci == 0),
                                           stop=(ci == NCT - 1))
                        p.inc(ins, "pe")

                @blk.scalar
                def _(s):
                    si = sch(s, p)
                    for ot in range(NCT):
                        s.wait_ge(p.S["pe"], pe0 + ot + 1)
                        si(s.activation(out_sb[:, ot * TC:(ot + 1) * TC],
                                        psms[ot % 2][:], AF.Sigmoid))

                @blk.vector
                def _(v):
                    vi = vch(v, p)
                    for ot in range(NCT):
                        v.wait_ge(p.S["s"], s0 + ot + 1)
                        vi(v.tensor_mul(srw[:, ot * TC:(ot + 1) * TC],
                                        srw[:, ot * TC:(ot + 1) * TC],
                                        out_sb[:, ot * TC:(ot + 1) * TC]))

            with nc.Block() as blk:
                pe0 = p.c["pe"]
                v0 = p.c["v"]

                @blk.tensor
                def _(t):
                    for ot in range(NCT):
                        if ot >= 2:
                            t.wait_ge(p.S["v"], v0 + ot - 1)
                        for ci in range(NCT):
                            ins = t.matmul(psms[ot % 2][:],
                                           w_ow[:, ci * C + ot * 128:
                                                ci * C + ot * 128 + 128],
                                           srw[:, ci * TC:(ci + 1) * TC],
                                           start=(ci == 0),
                                           stop=(ci == NCT - 1))
                        p.inc(ins, "pe")

                @blk.vector
                def _(v):
                    vi = vch(v, p)
                    for ot in range(NCT):
                        v.wait_ge(p.S["pe"], pe0 + ot + 1)
                        vi(v.tensor_add(out_sb[:, ot * TC:(ot + 1) * TC],
                                        xch[:, ot * TC:(ot + 1) * TC],
                                        psms[ot % 2][:]))

            with nc.Block() as blk:
                @blk.gpsimd
                def _(g):
                    for ci in range(NCT):
                        ins = g.dma_start(dram_cm(zb)[:, ci, t0:t1],
                                          out_sb[:, ci * TC:(ci + 1) * TC])
                        p.inc(ins, "dpool", 16)
                    g.wait_ge(p.S["dpool"], p.c["dpool"])

    with p.frame():
        zch = p.sbuf("l0b_z", [128, NCT * TC], F32R)
        lns = make_ln_scratch(p)
        for ch in range(NCH):
            t0, t1 = ch * TC, (ch + 1) * TC
            with nc.Block() as blk:
                @blk.sync
                def _(e):
                    ins = load_cm_chunk(e, p, zch, 0, zb, t0, t1)
                    p.inc(ins, "dsp", 16)
                    e.wait_ge(p.S["dsp"], p.c["dsp"])
            ln_chunk(p, cn, z_tiles=lambda ci: zch[:, ci * TC:(ci + 1) * TC],
                     out_tiles=lambda ci: zch[:, ci * TC:(ci + 1) * TC],
                     g_ap=lambda ci: vc("ln2w")[:, ci:ci + 1],
                     b_ap=lambda ci: vc("ln2b")[:, ci:ci + 1], **lns)
            with nc.Block() as blk:
                @blk.gpsimd
                def _(g):
                    for ci in range(NCT):
                        ins = g.dma_start(dram_cm(h2b)[:, ci, t0:t1],
                                          zch[:, ci * TC:(ci + 1) * TC])
                        p.inc(ins, "dpool", 16)
                    g.wait_ge(p.S["dpool"], p.c["dpool"])

    def mixed_pass(wT, Nd, mu_name, dstd, ep_fn, ep_eng, ep_nops, w_res):
        with p.frame():
            hbuf = p.sbuf("l0m_hb", [128, NCT * (TC + 1)], F32R)
            xk = p.sbuf("l0m_xk", [128, NCT * TC], F32R)
            stg = p.sbuf("l0m_stg", [128, 2 * TC], F32R)
            psms = [p.psum("l0m_ps", [128, TC], F32) for _ in range(2)]
            w_sb = None
            wcols = None
            if w_res:
                w_sb = p.sbuf("l0m_w", [128, NCT * Nd], F32R)
                with nc.Block() as blk:
                    @blk.sync
                    def _(e):
                        ins = e.dma_start(
                            w_sb[:].rearrange("p (ci o) -> p ci o", ci=NCT),
                            wT.rearrange("(ci p) o -> p ci o", p=128))
                        p.inc(ins, "dsp", 16)
                        e.wait_ge(p.S["dsp"], p.c["dsp"])
            else:
                wcols = [p.sbuf("l0m_wc", [128, NCT * 128], F32R)
                         for _ in range(2)]

            def hv(ci, off=1, w=TC):
                return hbuf[:, ci * (TC + 1) + off: ci * (TC + 1) + off + w]

            for ch in range(NCH):
                t0, t1 = ch * TC, (ch + 1) * TC
                lo = max(t0 - 1, 0)
                off = 1 - (t0 - lo)
                with nc.Block() as blk:
                    @blk.sync
                    def _(e):
                        ins = load_cm_chunk(e, p, hbuf, off, h2b, lo, t1,
                                            stride=TC + 1)
                        p.inc(ins, "dsp", 16)
                        e.wait_ge(p.S["dsp"], p.c["dsp"])

                    if ch == 0:
                        @blk.vector
                        def _(v):
                            vi = vch(v, p)
                            for ci in range(NCT):
                                vi(v.memset(hv(ci, 0, 1).bitcast(F32), 0.0))

                with nc.Block() as blk:
                    @blk.vector
                    def _(v):
                        vi = vch(v, p)
                        for ci in range(NCT):
                            vi(v.tensor_sub(xk[:, ci * TC:(ci + 1) * TC],
                                            hv(ci), hv(ci, 0)))
                            vi(v.scalar_tensor_tensor(
                                xk[:, ci * TC:(ci + 1) * TC],
                                xk[:, ci * TC:(ci + 1) * TC],
                                vc(mu_name)[:, ci:ci + 1], hv(ci, 0),
                                ALU.mult, ALU.add))

                if w_res:
                    nt = Nd // 128
                    with nc.Block() as blk:
                        pe0 = p.c["pe"]
                        e0 = p.c[ep_eng]
                        g0 = p.c["dpool"]

                        @blk.tensor
                        def _(t):
                            for ot in range(nt):
                                if ot >= 2:
                                    t.wait_ge(p.S[ep_eng],
                                              e0 + ep_nops * (ot - 1))
                                for ci in range(NCT):
                                    ins = t.matmul(
                                        psms[ot % 2][:],
                                        w_sb[:, ci * Nd + ot * 128:
                                             ci * Nd + ot * 128 + 128],
                                        xk[:, ci * TC:(ci + 1) * TC],
                                        start=(ci == 0), stop=(ci == NCT - 1))
                                p.inc(ins, "pe")

                        def _ep(eng):
                            issue = (sch if ep_eng == "s" else vch)(eng, p)
                            for ot in range(nt):
                                eng.wait_ge(p.S["pe"], pe0 + ot + 1)
                                if ot >= 2:
                                    eng.wait_ge(p.S["dpool"],
                                                g0 + 16 * (ot - 1))
                                ep_fn(eng, issue, ot, psms[ot % 2][:],
                                      stg[:, (ot % 2) * TC:(ot % 2 + 1) * TC])

                        if ep_eng == "s":
                            @blk.scalar
                            def _(s):
                                _ep(s)
                        else:
                            @blk.vector
                            def _(v):
                                _ep(v)

                        @blk.gpsimd
                        def _(g):
                            for ot in range(nt):
                                g.wait_ge(p.S[ep_eng], e0 + ep_nops * (ot + 1))
                                ins = g.dma_start(
                                    dram_cm(dstd)[:, ot, t0:t1],
                                    stg[:, (ot % 2) * TC:(ot % 2 + 1) * TC])
                                p.inc(ins, "dpool", 16)
                else:
                    nt = Nd // 128
                    with nc.Block() as blk:
                        pass
                with nc.Block() as blk:
                    @blk.gpsimd
                    def _(g):
                        g.wait_ge(p.S["dpool"], p.c["dpool"])

    def ep_kk(eng, issue, ot, ps, stage):
        issue(eng.activation(stage, ps, AF.Relu))
        issue(eng.activation(stage, stage, AF.Square))
    mixed_pass(W["fkwT"], 4096, "ftmk", kkb, ep_kk, "s", 2, True)

    def ep_sr(eng, issue, ot, ps, stage):
        issue(eng.activation(stage, ps, AF.Sigmoid))
    mixed_pass(W["frwT"], C, "ftmr", srb, ep_sr, "s", 1, True)

    with p.frame():
        lns = make_ln_scratch(p)
        xbuf = p.sbuf("l0c_xres", [128, NCT * TC], F32R)
        aux2 = p.sbuf("l0c_aux", [128, 2 * NCT * TC], F32R)

        def epc(eng, issue, ot, ps, stage, ch, aux):
            issue(eng.tensor_copy(stage, ps))

        def fin(ch, out_sb, srcb):
            t0, t1 = ch * TC, (ch + 1) * TC
            with nc.Block() as blk:
                @blk.sync
                def _(e):
                    ins = load_cm_chunk(e, p, aux2, 0, srb, t0, t1,
                                        stride=2 * TC)
                    p.inc(ins, "dsp", 16)
                    ins = load_cm_chunk(e, p, aux2, TC, zb, t0, t1,
                                        stride=2 * TC)
                    p.inc(ins, "dsp", 16)
                    e.wait_ge(p.S["dsp"], p.c["dsp"])
            with nc.Block() as blk:
                @blk.vector
                def _(v):
                    vi = vch(v, p)
                    for ci in range(NCT):
                        sl = out_sb[:, ci * TC:(ci + 1) * TC]
                        vi(v.tensor_mul(sl, sl,
                                        aux2[:, 2 * ci * TC:
                                             (2 * ci + 1) * TC]))
                        vi(v.tensor_add(sl, sl,
                                        aux2[:, (2 * ci + 1) * TC:
                                             (2 * ci + 2) * TC]))
            emit_resid_ln(p, cn, lns, ch=ch,
                          y_tiles=lambda ci: out_sb[:, ci * TC:(ci + 1) * TC],
                          xres=x0, out_dram=xout, xbuf=xbuf)

        lin_pass(p, cn, wT=W["fvwT"], Kd=4096, Nd=C, src=kkb, dst=None,
                 ep=Ep("v", 1, epc), finalize=fin)



def emit_l3_attn(p, cn, xin, xout):
    nc = p.nc
    ptb = p.dram("l3a_pt", [C, T], BF16)
    xtokb = p.dram("l3a_xtok", [T, C], BF16)
    IB = 512

    logpT = p.sbuf("l3a_logpT", [128, NCT * T], BF16)
    ne_row = p.sbuf("l3a_ne", [1, T], F32)
    ne_bf = p.sbuf("l3a_nebf", [1, T], BF16)
    ones_row_bf = p.sbuf("l3a_ones_bf", [1, 128], BF16)

    with nc.Block() as blk:
        @blk.sync
        def _(e):
            ins = e.dma_start(ones_row_bf[:],
                              cn.c["ones_bf_col"].rearrange("a b -> b a"))
            p.inc(ins, "dsp", 16)
            e.wait_ge(p.S["dsp"], p.c["dsp"])

    with p.frame():
        xch = p.sbuf("l3a_x", [128, NCT * TC], F32R)
        Ech = p.sbuf("l3a_E", [128, NCT * TC], F32R)
        t1b = p.sbuf("l3a_t1", [128, TC], F32)
        t2b = p.sbuf("l3a_t2", [128, TC], F32)
        ptst = p.sbuf("l3a_pst", [128, NCT * TC], BF16)
        xbf = p.sbuf("l3a_xbf", [128, NCT * TC], BF16)
        xtt = p.sbuf("l3a_xtt", [128, NCT * TC], BF16)
        srow = p.sbuf("l3a_srow", [1, 4 * TC], F32R)
        ps_sum = p.psum("l3a_psum", [1, 2 * TC], F32)
        ps_b = p.psum("l3a_pb", [128, 2 * TC], F32)
        ps_tr = p.psum("l3a_ptr", [128, 128], BF16)

        s_row = srow[0:1, 0:TC]
        lse_row = srow[0:1, TC:2 * TC]
        rs_row = srow[0:1, 2 * TC:3 * TC]

        for ch in range(NCH):
            t0, t1 = ch * TC, (ch + 1) * TC
            with nc.Block() as blk:
                @blk.sync
                def _(e):
                    ins = load_cm_chunk(e, p, xch, 0, xin, t0, t1)
                    p.inc(ins, "dsp", 16)
                    e.wait_ge(p.S["dsp"], p.c["dsp"])

            with nc.Block() as blk:
                @blk.scalar
                def _(s):
                    si = sch(s, p)
                    for ci in range(NCT):
                        si(s.activation(Ech[:, ci * TC:(ci + 1) * TC],
                                        xch[:, ci * TC:(ci + 1) * TC], AF.Exp))

                @blk.vector
                def _(v):
                    vi = vch(v, p)
                    vi(v.tensor_copy(xbf[:], xch[:]))

            with nc.Block() as blk:
                pe0 = p.c["pe"]

                @blk.tensor
                def _(t):
                    for ci in range(NCT):
                        ins = t.matmul(ps_sum[0:1, 0:TC], cn.ones_col[:],
                                       Ech[:, ci * TC:(ci + 1) * TC],
                                       start=(ci == 0), stop=(ci == NCT - 1))
                    p.inc(ins, "pe")

                pm = p.c["pe"]

                @blk.vector
                def _(v):
                    vi = vch(v, p)
                    v.wait_ge(p.S["pe"], pm)
                    vi(v.tensor_copy(s_row, ps_sum[0:1, 0:TC]))
                    with nc.allow_low_precision("f32r bits are fp32"):
                        vi(v.reciprocal(rs_row, s_row))
                vm = p.c["v"]

                @blk.scalar
                def _(s):
                    si = sch(s, p)
                    s.wait_ge(p.S["v"], vm)
                    si(s.activation(lse_row, s_row, AF.Ln))

            with nc.Block() as blk:
                pe0 = p.c["pe"]

                @blk.tensor
                def _(t):
                    ins = t.matmul(ps_b[:, 0:TC], cn.ones_row[:], lse_row,
                                   start=True, stop=True)
                    p.inc(ins, "pe")
                    ins = t.matmul(ps_b[:, TC:2 * TC], cn.ones_row[:], rs_row,
                                   start=True, stop=True)
                    p.inc(ins, "pe")

                pm = p.c["pe"]
                v0 = p.c["v"]

                @blk.vector
                def _(v):
                    vi = vch(v, p)
                    v.wait_ge(p.S["pe"], pm)
                    for ci in range(NCT):
                        vi(v.tensor_sub(t1b[:], xch[:, ci * TC:(ci + 1) * TC],
                                        ps_b[:, 0:TC]))
                        vi(v.tensor_mul(t2b[:], Ech[:, ci * TC:(ci + 1) * TC],
                                        ps_b[:, TC:2 * TC]))
                        vi(v.tensor_copy(
                            logpT[:, ci * T + t0:ci * T + t1], t1b[:]))
                        vi(v.tensor_copy(ptst[:, ci * TC:(ci + 1) * TC],
                                         t2b[:]))
                        vi(v.tensor_mul(t1b[:], t1b[:], t2b[:]))
                        vi(v.tensor_copy(Ech[:, ci * TC:(ci + 1) * TC],
                                         t1b[:]))

            with nc.Block() as blk:
                @blk.tensor
                def _(t):
                    for ci in range(NCT):
                        ins = t.matmul(ps_sum[0:1, TC:2 * TC], cn.ones_col[:],
                                       Ech[:, ci * TC:(ci + 1) * TC],
                                       start=(ci == 0), stop=(ci == NCT - 1))
                    p.inc(ins, "pe")

                pm = p.c["pe"]

                @blk.vector
                def _(v):
                    vi = vch(v, p)
                    v.wait_ge(p.S["pe"], pm)
                    vi(v.tensor_copy(ne_row[0:1, t0:t1], ps_sum[0:1, TC:2 * TC]))

            with nc.Block() as blk:
                pe0 = p.c["pe"]
                v0 = p.c["v"]

                @blk.tensor
                def _(t):
                    n = 0
                    for ci in range(NCT):
                        for st in range(4):
                            if n >= 1:
                                t.wait_ge(p.S["v"], v0 + n)
                            ins = t.transpose(
                                ps_tr[:],
                                xbf[:, ci * TC + st * 128:
                                    ci * TC + (st + 1) * 128],
                                cn.ident_bf[:])
                            p.inc(ins, "pe")
                            n += 1

                @blk.vector
                def _(v):
                    vi = vch(v, p)
                    n = 0
                    for ci in range(NCT):
                        for st in range(4):
                            v.wait_ge(p.S["pe"], pe0 + n + 1)
                            vi(v.tensor_copy(
                                xtt[:, st * C + ci * 128:
                                    st * C + (ci + 1) * 128], ps_tr[:]))
                            n += 1

            with nc.Block() as blk:
                @blk.gpsimd
                def _(g):
                    ins = g.dma_start(
                        ptb.rearrange("(ci p) t -> p ci t", p=128)[:, :, t0:t1],
                        ptst[:].rearrange("p (ci t) -> p ci t", ci=NCT))
                    p.inc(ins, "dpool", 16)
                    ins = g.dma_start(
                        xtokb.rearrange("(st pp) c -> pp st c", pp=128)
                        [:, 4 * ch:4 * (ch + 1), :],
                        xtt[:].rearrange("p (st c) -> p st c", st=4))
                    p.inc(ins, "dpool", 16)
                    g.wait_ge(p.S["dpool"], p.c["dpool"])

        with nc.Block() as blk:
            @blk.vector
            def _(v):
                vi = vch(v, p)
                vi(v.tensor_copy(ne_bf[:], ne_row[:]))

    with p.frame():
        pt_sb = p.sbuf("l3a_pti", [128, NCT * IB], BF16)
        expT = p.sbuf("l3a_expT", [128, 32 * IB], BF16)
        negent_sb = p.sbuf("l3a_neg", [128, IB], F32)
        crs = p.sbuf("l3a_crs", [128, IB], F32)
        zpad = p.sbuf("l3a_zpad", [128, IB], F32)
        zr = p.sbuf("l3a_zr", [128, 4], F32)
        xtj = [p.sbuf("l3a_xtj", [128, TC], BF16) for _ in range(2)]
        otok = p.sbuf("l3a_otok", [128, TC], F32)
        y_sb = p.sbuf("l3a_y", [128, NCT * IB], F32R)
        xbuf = p.sbuf("l3a_xres", [128, NCT * IB], F32R)

        for ib in range(T // IB):
            i0, i1 = ib * IB, (ib + 1) * IB
            p.frames.append(ExitStack())
            ps_cr = p.psum("l3a_pcr", [128, IB], F32)
            ps_z = p.psum("l3a_pz", [1, IB], F32)
            ps_nb = p.psum("l3a_pnb", [128, IB], F32)

            with nc.Block() as blk:
                @blk.sync
                def _(e):
                    ins = e.dma_start(
                        pt_sb[:].rearrange("p (ci i) -> p ci i", ci=NCT),
                        ptb.rearrange("(ci p) t -> p ci t", p=128)[:, :, i0:i1])
                    p.inc(ins, "dsp", 16)
                    e.wait_ge(p.S["dsp"], p.c["dsp"])

            with nc.Block() as blk:
                pe0 = p.c["pe"]

                @blk.tensor
                def _(t):
                    ins = t.matmul(ps_nb[:], ones_row_bf[:],
                                   ne_bf[0:1, i0:i1], start=True, stop=True)
                    p.inc(ins, "pe")

                pm = p.c["pe"]

                @blk.vector
                def _(v):
                    vi = vch(v, p)
                    v.wait_ge(p.S["pe"], pm)
                    vi(v.tensor_copy(negent_sb[:], ps_nb[:]))
                    vi(v.memset(zpad[:], 0.0))

            with nc.Block() as blk:
                pe0 = p.c["pe"]
                v0 = p.c["v"]
                s0 = p.c["s"]

                @blk.tensor
                def _(t):
                    for jt in range(32):
                        if jt >= 1:
                            t.wait_ge(p.S["v"], v0 + jt)
                        for ci in range(NCT):
                            ins = t.matmul(
                                ps_cr[:],
                                logpT[:, ci * T + jt * 128:
                                      ci * T + (jt + 1) * 128],
                                pt_sb[:, ci * IB:(ci + 1) * IB],
                                start=(ci == 0), stop=(ci == NCT - 1))
                        p.inc(ins, "pe")
                        t.wait_ge(p.S["s"], s0 + jt + 1)
                        ins = t.matmul(ps_z[:], cn.ones_bf_col[:],
                                       expT[:, jt * IB:(jt + 1) * IB],
                                       start=(jt == 0), stop=(jt == 31))
                        p.inc(ins, "pe")

                @blk.vector
                def _(v):
                    vi = vch(v, p)
                    for jt in range(32):
                        v.wait_ge(p.S["pe"], pe0 + 2 * jt + 1)
                        vi(v.tensor_sub(crs[:], ps_cr[:], negent_sb[:]))

                @blk.scalar
                def _(s):
                    si = sch(s, p)
                    for jt in range(32):
                        s.wait_ge(p.S["v"], v0 + jt + 1)
                        si(s.activation(expT[:, jt * IB:(jt + 1) * IB],
                                        crs[:], AF.Exp))

            with nc.Block() as blk:
                @blk.vector
                def _(v):
                    vi = vch(v, p)
                    vi(v.tensor_copy(zpad[0:1, :], ps_z[0:1, :]))
                    vi(v.reciprocal(zpad[0:1, :], zpad[0:1, :]))

            with nc.Block() as blk:
                pe0 = p.c["pe"]
                v0 = p.c["v"]

                @blk.tensor
                def _(t):
                    for k in range(4):
                        if k >= 1:
                            t.wait_ge(p.S["v"], v0 + k)
                        ins = t.transpose(ps_nb[:, 0:128],
                                          zpad[:, k * 128:(k + 1) * 128],
                                          cn.ident[:])
                        p.inc(ins, "pe")

                @blk.vector
                def _(v):
                    vi = vch(v, p)
                    for k in range(4):
                        v.wait_ge(p.S["pe"], pe0 + k + 1)
                        vi(v.tensor_copy(zr[:, k:k + 1], ps_nb[:, 0:1]))

            p.frames.pop().close()

            p.frames.append(ExitStack())
            ps_o = [p.psum("l3a_po", [128, TC], F32) for _ in range(4)]
            ps_tr2 = p.psum("l3a_ptr2", [128, 128], F32)
            for dh in range(2):
                with nc.Block() as blk:
                    dready = [0] * 32
                    pe_base = p.c["pe"]

                    @blk.sync
                    def _(e):
                        for jt in range(32):
                            if jt >= 2:
                                e.wait_ge(p.S["pe"], pe_base + 4 * (jt - 1))
                            ins = e.dma_start(
                                xtj[jt % 2][:],
                                xtokb[jt * 128:(jt + 1) * 128,
                                      dh * TC:(dh + 1) * TC])
                            p.inc(ins, "dsp", 16)
                            dready[jt] = p.c["dsp"]

                    @blk.tensor
                    def _(t):
                        for jt in range(32):
                            t.wait_ge(p.S["dsp"], dready[jt])
                            for isub in range(4):
                                ins = t.matmul(
                                    ps_o[isub][:],
                                    expT[:, jt * IB + isub * 128:
                                         jt * IB + (isub + 1) * 128],
                                    xtj[jt % 2][:],
                                    start=(jt == 0), stop=(jt == 31))
                                p.inc(ins, "pe")

                for isub in range(4):
                    with nc.Block() as blk:
                        @blk.vector
                        def _(v):
                            vi = vch(v, p)
                            vi(v.tensor_scalar(otok[:], ps_o[isub][:],
                                               zr[:, isub:isub + 1], None,
                                               ALU.mult))
                    with nc.Block() as blk:
                        pe0t = p.c["pe"]
                        v0t = p.c["v"]

                        @blk.tensor
                        def _(t):
                            for ds in range(4):
                                if ds >= 1:
                                    t.wait_ge(p.S["v"], v0t + ds)
                                ins = t.transpose(
                                    ps_tr2[:],
                                    otok[:, ds * 128:(ds + 1) * 128],
                                    cn.ident[:])
                                p.inc(ins, "pe")

                        @blk.vector
                        def _(v):
                            vi = vch(v, p)
                            for ds in range(4):
                                v.wait_ge(p.S["pe"], pe0t + ds + 1)
                                vi(v.tensor_copy(
                                    y_sb[:, (dh * 4 + ds) * IB + isub * 128:
                                         (dh * 4 + ds) * IB + (isub + 1) * 128],
                                    ps_tr2[:]))
            p.frames.pop().close()

            p.frames.append(ExitStack())
            lns = make_ln_scratch(p, width=IB)
            emit_resid_ln(p, cn, lns, ch=ib,
                          y_tiles=lambda ci: y_sb[:, ci * IB:(ci + 1) * IB],
                          xres=xin, out_dram=xout, xbuf=xbuf)
            p.frames.pop().close()



ECAP = 1280
NEXP = 32
BIGSLOT = 1.0e9


def emit_l3_ffn(p, cn, W, xin, out_final):
    nc = p.nc
    NT = 32
    xtok = p.dram("l3f_xtok", [T, C], F32R)
    cc_cnt_in = p.dram("l3f_cci", [1, NEXP], F32)
    cc_cnt_out = p.dram("l3f_cco", [8, NEXP], F32)
    idx_in = p.dram("l3f_idxi", [NEXP * ECAP, 1], F32)
    idx_out = p.dram("l3f_idxo", [NEXP * ECAP, 1], F32)
    xg = p.dram("l3f_xg", [8 * T, C], F32R)
    y_own = p.dram("l3f_yown", [4 * ECAP, C], F32R)
    yg = p.dram("l3f_yg", [NEXP * ECAP, C], F32R)

    wr_sb = p.sbuf("l3f_wr", [128, NCT * NEXP], F32R)
    cmask = p.sbuf("l3f_cm", [8, 1], F32R)
    cb = p.sbuf("l3f_cb", [128, 1], F32)
    cb2 = p.sbuf("l3f_cb2", [128, 1], F32)
    gate_sb = p.sbuf("l3f_gate", [128, NT], F32)
    acc_sb = p.sbuf("l3f_acc", [128, NT], F32)
    slots_sb = p.sbuf("l3f_slots", [128, NT], F32)
    idxf_sb = p.sbuf("l3f_idxf", [128, NT], F32)
    oh_sb = p.sbuf("l3f_oh", [128, NT * NEXP], F32R)
    pos_sb = p.sbuf("l3f_pos", [128, NT * NEXP], F32)
    cnts_sb = p.sbuf("l3f_cnts", [1, NT * NEXP], F32)
    offs_sb = p.sbuf("l3f_offs", [1, (NT + 1) * NEXP], F32R)
    coff_sb = p.sbuf("l3f_coff", [1, NEXP], F32R)

    with nc.Block() as blk:
        @blk.sync
        def _(e):
            ins = e.dma_start(wr_sb[:].rearrange("p (ci o) -> p ci o", ci=NCT),
                              W["w_router"].rearrange("(ci p) o -> p ci o", p=128))
            p.inc(ins, "dsp", 16)
            for tile, nmw in ((cmask, "coremask"), (cb, "cb"), (cb2, "cb2")):
                ins = e.dma_start(tile[:], W[nmw])
                p.inc(ins, "dsp", 16)
            e.wait_ge(p.S["dsp"], p.c["dsp"])

    with p.frame():
        xch = p.sbuf("l3f_x", [128, NCT * TC], F32R)
        xtt = p.sbuf("l3f_xtt", [128, 4 * C], F32R)
        e_sb = p.sbuf("l3f_e", [128, NEXP], F32)
        prob = p.sbuf("l3f_prob", [128, NEXP], F32)
        max8 = p.sbuf("l3f_max8", [128, 8], F32)
        idxu = p.sbuf("l3f_idxu", [128, 8], U32)
        stats = p.sbuf("l3f_st", [128, 4], F32)
        ps_tr = p.psum("l3f_ptr", [128, 128], F32)
        ps_lg = p.psum("l3f_plg", [128, NEXP], F32)
        ps_pos = p.psum("l3f_ppos", [128, NEXP], F32)
        ps_cnt = p.psum("l3f_pcnt", [1, NEXP], F32)

        for ch in range(NCH):
            t0, t1 = ch * TC, (ch + 1) * TC
            with nc.Block() as blk:
                @blk.sync
                def _(e):
                    ins = load_cm_chunk(e, p, xch, 0, xin, t0, t1)
                    p.inc(ins, "dsp", 16)
                    e.wait_ge(p.S["dsp"], p.c["dsp"])

            with nc.Block() as blk:
                pe0 = p.c["pe"]
                v0 = p.c["v"]

                @blk.tensor
                def _(t):
                    n = 0
                    for ci in range(NCT):
                        for st in range(4):
                            if n >= 1:
                                t.wait_ge(p.S["v"], v0 + n)
                            ins = t.transpose(
                                ps_tr[:].bitcast(F32R),
                                xch[:, ci * TC + st * 128:
                                    ci * TC + (st + 1) * 128],
                                cn.ident_r[:])
                            p.inc(ins, "pe")
                            n += 1

                @blk.vector
                def _(v):
                    vi = vch(v, p)
                    n = 0
                    for ci in range(NCT):
                        for st in range(4):
                            v.wait_ge(p.S["pe"], pe0 + n + 1)
                            vi(v.tensor_copy(
                                xtt[:, st * C + ci * 128:
                                    st * C + (ci + 1) * 128],
                                ps_tr[:].bitcast(F32R)))
                            n += 1

            with nc.Block() as blk:
                @blk.gpsimd
                def _(g):
                    ins = g.dma_start(
                        xtok.rearrange("(st pp) c -> pp st c", pp=128)
                        [:, 4 * ch:4 * (ch + 1), :],
                        xtt[:].rearrange("p (st c) -> p st c", st=4))
                    p.inc(ins, "dpool", 16)
                    g.wait_ge(p.S["dpool"], p.c["dpool"])

            for st in range(4):
                tile = 4 * ch + st
                with nc.Block() as blk:
                    pe0 = p.c["pe"]

                    @blk.tensor
                    def _(t):
                        for ci in range(NCT):
                            ins = t.matmul(
                                ps_lg[:],
                                xch[:, ci * TC + st * 128:
                                    ci * TC + (st + 1) * 128],
                                wr_sb[:, ci * NEXP:(ci + 1) * NEXP],
                                start=(ci == 0), stop=(ci == NCT - 1))
                        p.inc(ins, "pe")

                    pm = p.c["pe"]
                    v0 = p.c["v"]
                    s0 = p.c["s"]

                    @blk.vector
                    def _(v):
                        vi = vch(v, p)
                        v.wait_ge(p.S["pe"], pm)
                        vi(v.reduce_max(stats[:, 0:1], ps_lg[:], axis=AX.X))
                        vi(v.tensor_scalar(stats[:, 0:1], stats[:, 0:1], -1.0,
                                           None, ALU.mult))
                        v.wait_ge(p.S["s"], s0 + 1)
                        vi(v.reduce_sum(stats[:, 1:2], e_sb[:], axis=AX.X))
                        vi(v.reciprocal(stats[:, 2:3], stats[:, 1:2]))
                        vi(v.tensor_scalar(prob[:], e_sb[:], stats[:, 2:3],
                                           None, ALU.mult))
                        vi(v.max(max8[:], prob[:]))
                        vi(v.max_index(idxu[:], max8[:], prob[:]))
                        vi(v.tensor_copy(gate_sb[:, tile:tile + 1],
                                         max8[:, 0:1]))
                        vi(v.tensor_copy(idxf_sb[:, tile:tile + 1],
                                         idxu[:, 0:1]))
                        vi(v.tensor_scalar(prob[:], cn.iota32[:],
                                           idxf_sb[:, tile:tile + 1], None,
                                           ALU.subtract))
                    vm = p.c["v"]

                    @blk.scalar
                    def _(s):
                        si = sch(s, p)
                        s.wait_ge(p.S["v"], v0 + 2)
                        si(s.activation(e_sb[:], ps_lg[:], AF.Exp,
                                        bias=stats[:, 0:1]))
                        s.wait_ge(p.S["v"], vm)
                        si(s.activation(prob[:], prob[:], AF.Square))
                    sm = p.c["s"]

                    @blk.vector
                    def _(v):
                        vi = vch(v, p)
                        v.wait_ge(p.S["s"], sm)
                        vi(v.tensor_scalar(
                            oh_sb[:, tile * NEXP:(tile + 1) * NEXP],
                            prob[:], -1.0, 1.0, ALU.mult, ALU.add))
                        vi(v.tensor_scalar_max(
                            oh_sb[:, tile * NEXP:(tile + 1) * NEXP],
                            oh_sb[:, tile * NEXP:(tile + 1) * NEXP], 0.0))
                    vm2 = p.c["v"]

                    @blk.tensor
                    def _(t):
                        t.wait_ge(p.S["v"], vm2)
                        ins = t.matmul(ps_pos[:], cn.ustrict[:],
                                       oh_sb[:, tile * NEXP:(tile + 1) * NEXP],
                                       start=True, stop=True)
                        p.inc(ins, "pe")
                        ins = t.matmul(ps_cnt[:], cn.ones_col[:],
                                       oh_sb[:, tile * NEXP:(tile + 1) * NEXP],
                                       start=True, stop=True)
                        p.inc(ins, "pe")

                    pm2 = p.c["pe"]

                    @blk.vector
                    def _(v):
                        vi = vch(v, p)
                        v.wait_ge(p.S["pe"], pm2)
                        vi(v.tensor_copy(
                            pos_sb[:, tile * NEXP:(tile + 1) * NEXP],
                            ps_pos[:]))
                        vi(v.tensor_copy(cnts_sb[0:1, tile * NEXP:
                                                 (tile + 1) * NEXP],
                                         ps_cnt[0:1, :]))

    with nc.Block() as blk:
        @blk.vector
        def _(v):
            vi = vch(v, p)
            vi(v.memset(offs_sb[0:1, 0:NEXP].bitcast(F32), 0.0))
            for k in range(NT):
                vi(v.tensor_add(offs_sb[0:1, (k + 1) * NEXP:(k + 2) * NEXP]
                                .bitcast(F32),
                                offs_sb[0:1, k * NEXP:(k + 1) * NEXP]
                                .bitcast(F32),
                                cnts_sb[0:1, k * NEXP:(k + 1) * NEXP]))

    with nc.Block() as blk:
        @blk.gpsimd
        def _(g):
            ins = g.dma_start(cc_cnt_in,
                              offs_sb[0:1, NT * NEXP:(NT + 1) * NEXP]
                              .bitcast(F32))
            p.inc(ins, "dpool", 16)
            g.wait_ge(p.S["dpool"], p.c["dpool"])
            g.collective_compute(
                "AllGather", ALU.bypass, replica_groups=[list(range(8))],
                ins=[cc_cnt_in.opt()], outs=[cc_cnt_out.opt()],
            ).then_inc(p.S["cc"])
            p.c["cc"] += 1
            g.wait_ge(p.S["cc"], p.c["cc"])

    agc = p.sbuf("l3f_agc", [8, NEXP], F32R)
    agc0 = p.sbuf("l3f_agc0", [8, NEXP], F32)
    ps_coff = p.psum("l3f_pcoff", [1, NEXP], F32)
    with nc.Block() as blk:
        @blk.sync
        def _(e):
            ins = e.dma_start(agc0[:], cc_cnt_out)
            p.inc(ins, "dsp", 16)
            e.wait_ge(p.S["dsp"], p.c["dsp"])
    with nc.Block() as blk:
        @blk.vector
        def _(v):
            vi = vch(v, p)
            vi(v.tensor_copy(agc[:], agc0[:]))

    with nc.Block() as blk:
        @blk.tensor
        def _(t):
            ins = t.matmul(ps_coff[:], cmask[:], agc[:], start=True, stop=True)
            p.inc(ins, "pe")

        pm = p.c["pe"]

        @blk.vector
        def _(v):
            vi = vch(v, p)
            v.wait_ge(p.S["pe"], pm)
            vi(v.tensor_copy(coff_sb[:], ps_coff[:]))

    with p.frame():
        combo = p.sbuf("l3f_combo", [1, NEXP], F32R)
        rep = p.psum("l3f_rep", [128, NEXP], F32)
        tposf = p.sbuf("l3f_tposf", [128, NEXP], F32)
        tok_sb = p.sbuf("l3f_tok", [128, 2], F32)
        slotu = p.sbuf("l3f_slotu", [128, 1], U32)
        zrow = p.sbuf("l3f_zrow", [128, 320], F32)

        with nc.Block() as blk:
            @blk.vector
            def _(v):
                vi = vch(v, p)
                vi(v.memset(zrow[:], 0.0))

        with nc.Block() as blk:
            @blk.gpsimd
            def _(g):
                ins = g.dma_start(
                    idx_in.rearrange("(pp a) c -> pp (a c)", pp=128),
                    zrow[:])
                p.inc(ins, "dpool", 16)
                g.wait_ge(p.S["dpool"], p.c["dpool"])

        for tile in range(NT):
            with nc.Block() as blk:
                @blk.vector
                def _(v):
                    vi = vch(v, p)
                    vi(v.tensor_add(combo[:], coff_sb[:],
                                    offs_sb[0:1, tile * NEXP:(tile + 1) * NEXP]))
                vm = p.c["v"]

                @blk.tensor
                def _(t):
                    t.wait_ge(p.S["v"], vm)
                    ins = t.matmul(rep[:], cn.ones_row[:], combo[:],
                                   start=True, stop=True)
                    p.inc(ins, "pe")

                pm = p.c["pe"]

                @blk.vector
                def _(v):
                    vi = vch(v, p)
                    v.wait_ge(p.S["pe"], pm)
                    vi(v.tensor_add(tposf[:],
                                    pos_sb[:, tile * NEXP:(tile + 1) * NEXP],
                                    rep[:]))
                    vi(v.tensor_mul(tposf[:], tposf[:],
                                    oh_sb[:, tile * NEXP:(tile + 1) * NEXP]))
                    vi(v.reduce_sum(tok_sb[:, 0:1], tposf[:], axis=AX.X))
                    vi(v.tensor_scalar(tok_sb[:, 1:2], tok_sb[:, 0:1], -1.0,
                                       float(ECAP), ALU.mult, ALU.add))
                    vi(v.tensor_scalar_min(tok_sb[:, 1:2], tok_sb[:, 1:2], 1.0))
                    vi(v.tensor_scalar_max(acc_sb[:, tile:tile + 1],
                                           tok_sb[:, 1:2], 0.0))
                    vi(v.scalar_tensor_tensor(
                        slots_sb[:, tile:tile + 1],
                        idxf_sb[:, tile:tile + 1], float(ECAP),
                        tok_sb[:, 0:1], ALU.mult, ALU.add))
                    vi(v.tensor_scalar(tok_sb[:, 1:2],
                                       acc_sb[:, tile:tile + 1],
                                       -BIGSLOT, BIGSLOT, ALU.mult, ALU.add))
                    vi(v.scalar_tensor_tensor(
                        slots_sb[:, tile:tile + 1],
                        slots_sb[:, tile:tile + 1],
                        acc_sb[:, tile:tile + 1],
                        tok_sb[:, 1:2], ALU.mult, ALU.add))
                    vi(v.tensor_copy(slotu[:], slots_sb[:, tile:tile + 1]))
                    vi(v.tensor_scalar(tok_sb[:, 0:1], cn.iota_col[:], 1.0,
                                       float(tile * 128), ALU.mult, ALU.add))
                    vi(v.tensor_add(tok_sb[:, 0:1], tok_sb[:, 0:1], cb[:]))

            with nc.Block() as blk:
                @blk.gpsimd
                def _(g):
                    ins = g.indirect_dma_start(
                        idx_in, IndirectOffsetOnAxis(slotu[:], 0),
                        tok_sb[:, 0:1], None,
                        bounds_check=NEXP * ECAP, oob_is_err=False)
                    p.inc(ins, "dpool", 16)
                    g.wait_ge(p.S["dpool"], p.c["dpool"])

        with nc.Block() as blk:
            @blk.gpsimd
            def _(g):
                g.collective_compute(
                    "AllReduce", ALU.add, replica_groups=[list(range(8))],
                    ins=[idx_in.opt()], outs=[idx_out.opt()],
                ).then_inc(p.S["cc"])
                p.c["cc"] += 1
                g.wait_ge(p.S["cc"], p.c["cc"])
                g.collective_compute(
                    "AllGather", ALU.bypass, replica_groups=[list(range(8))],
                    ins=[xtok.opt()], outs=[xg.opt()],
                ).then_inc(p.S["cc"])
                p.c["cc"] += 1
                g.wait_ge(p.S["cc"], p.c["cc"])

    with p.frame():
        NTOK = 4 * ECAP // 128
        idxl = p.sbuf("l3f_idxl", [128, 1], F32)
        idxlu = p.sbuf("l3f_idxlu", [128, 1], U32)
        offu = p.sbuf("l3f_offu", [128, 1], U32)
        offf = p.sbuf("l3f_offf", [128, 1], F32)
        xrow = p.sbuf("l3f_xrow", [128, C], F32R)
        xeT = p.sbuf("l3f_xeT", [128, NCT * ECAP], F32R)
        hT = p.sbuf("l3f_hT", [128, NCT * ECAP], F32R)
        w_sb = p.sbuf("l3f_we", [128, NCT * C], F32R)
        yrow = p.sbuf("l3f_yrow", [128, C], F32)
        ps_tr = p.psum("l3f_ptr2", [128, 128], F32)
        psm = [p.psum("l3f_pse", [128, TC], F32) for _ in range(2)]
        tchunks = [(0, 512), (512, 1024), (1024, 1280)]

        for eo in range(4):
            for gt in range(10):
                with nc.Block() as blk:
                    @blk.vector
                    def _(v):
                        vi = vch(v, p)
                        vi(v.tensor_scalar(offf[:], cn.iota_col[:], 1.0,
                                           float((eo * 10 + gt) * 128),
                                           ALU.mult, ALU.add))
                        vi(v.tensor_add(offf[:], offf[:], cb2[:]))
                        vi(v.tensor_copy(offu[:], offf[:]))

                with nc.Block() as blk:
                    @blk.gpsimd
                    def _(g):
                        ins = g.indirect_dma_start(
                            idxl[:], None,
                            idx_out, IndirectOffsetOnAxis(offu[:], 0))
                        p.inc(ins, "dpool", 16)
                        g.wait_ge(p.S["dpool"], p.c["dpool"])

                with nc.Block() as blk:
                    @blk.vector
                    def _(v):
                        vi = vch(v, p)
                        vi(v.tensor_copy(idxlu[:], idxl[:]))

                with nc.Block() as blk:
                    @blk.gpsimd
                    def _(g):
                        ins = g.indirect_dma_start(
                            xrow[:], None,
                            xg, IndirectOffsetOnAxis(idxlu[:], 0))
                        p.inc(ins, "dpool", 16)
                        g.wait_ge(p.S["dpool"], p.c["dpool"])

                with nc.Block() as blk:
                    pe0 = p.c["pe"]
                    v0 = p.c["v"]

                    @blk.tensor
                    def _(t):
                        for ci in range(NCT):
                            if ci >= 1:
                                t.wait_ge(p.S["v"], v0 + ci)
                            ins = t.transpose(
                                ps_tr[:].bitcast(F32R),
                                xrow[:, ci * 128:(ci + 1) * 128],
                                cn.ident_r[:])
                            p.inc(ins, "pe")

                    @blk.vector
                    def _(v):
                        vi = vch(v, p)
                        for ci in range(NCT):
                            v.wait_ge(p.S["pe"], pe0 + ci + 1)
                            vi(v.tensor_copy(
                                xeT[:, ci * ECAP + gt * 128:
                                    ci * ECAP + (gt + 1) * 128],
                                ps_tr[:].bitcast(F32R)))

            def stage_mm(wname, src_sb, ep_fn, ep_eng, ep_nops):
                with nc.Block() as blk:
                    @blk.sync
                    def _(e):
                        ins = e.dma_start(
                            w_sb[:].rearrange("p (ci o) -> p ci o", ci=NCT),
                            W[wname][eo].rearrange("(ci p) o -> p ci o", p=128))
                        p.inc(ins, "dsp", 16)
                        e.wait_ge(p.S["dsp"], p.c["dsp"])
                with nc.Block() as blk:
                    pe0 = p.c["pe"]
                    e0 = p.c[ep_eng]

                    @blk.tensor
                    def _(t):
                        n = 0
                        for ot in range(NCT):
                            for tci, (c0, c1) in enumerate(tchunks):
                                if n >= 2:
                                    t.wait_ge(p.S[ep_eng],
                                              e0 + ep_nops * (n - 1))
                                for ci in range(NCT):
                                    ins = t.matmul(
                                        psm[n % 2][:, 0:c1 - c0],
                                        w_sb[:, ci * C + ot * 128:
                                             ci * C + ot * 128 + 128],
                                        src_sb[:, ci * ECAP + c0:
                                               ci * ECAP + c1],
                                        start=(ci == 0), stop=(ci == NCT - 1))
                                p.inc(ins, "pe")
                                n += 1

                    def _ep(eng):
                        issue = (sch if ep_eng == "s" else vch)(eng, p)
                        n = 0
                        for ot in range(NCT):
                            for tci, (c0, c1) in enumerate(tchunks):
                                eng.wait_ge(p.S["pe"], pe0 + n + 1)
                                ep_fn(eng, issue, ot, c0, c1,
                                      psm[n % 2][:, 0:c1 - c0])
                                n += 1

                    if ep_eng == "s":
                        @blk.scalar
                        def _(s):
                            _ep(s)
                    else:
                        @blk.vector
                        def _(v):
                            _ep(v)

            def ep_g(eng, issue, ot, c0, c1, ps):
                issue(eng.activation(hT[:, ot * ECAP + c0:ot * ECAP + c1],
                                     ps, AF.Silu))
            stage_mm("wg", xeT, ep_g, "s", 1)

            def ep_u(eng, issue, ot, c0, c1, ps):
                issue(eng.tensor_mul(hT[:, ot * ECAP + c0:ot * ECAP + c1],
                                     hT[:, ot * ECAP + c0:ot * ECAP + c1], ps))
            stage_mm("wu", xeT, ep_u, "v", 1)

            def ep_y(eng, issue, ot, c0, c1, ps):
                issue(eng.tensor_copy(xeT[:, ot * ECAP + c0:ot * ECAP + c1],
                                      ps))
            stage_mm("wd", hT, ep_y, "v", 1)

            for gt in range(10):
                with nc.Block() as blk:
                    pe0 = p.c["pe"]
                    v0 = p.c["v"]

                    @blk.tensor
                    def _(t):
                        for ci in range(NCT):
                            if ci >= 1:
                                t.wait_ge(p.S["v"], v0 + ci)
                            ins = t.transpose(
                                ps_tr[:].bitcast(F32R),
                                xeT[:, ci * ECAP + gt * 128:
                                    ci * ECAP + (gt + 1) * 128],
                                cn.ident_r[:])
                            p.inc(ins, "pe")

                    @blk.vector
                    def _(v):
                        vi = vch(v, p)
                        for ci in range(NCT):
                            v.wait_ge(p.S["pe"], pe0 + ci + 1)
                            vi(v.tensor_copy(yrow[:, ci * 128:(ci + 1) * 128],
                                             ps_tr[:]))

                with nc.Block() as blk:
                    @blk.gpsimd
                    def _(g):
                        ins = g.dma_start(
                            y_own[(eo * 10 + gt) * 128:
                                  (eo * 10 + gt + 1) * 128, :],
                            yrow[:].bitcast(F32R))
                        p.inc(ins, "dpool", 16)
                        g.wait_ge(p.S["dpool"], p.c["dpool"])

    with nc.Block() as blk:
        @blk.gpsimd
        def _(g):
            g.collective_compute(
                "AllGather", ALU.bypass, replica_groups=[list(range(8))],
                ins=[y_own.opt()], outs=[yg.opt()],
            ).then_inc(p.S["cc"])
            p.c["cc"] += 1
            g.wait_ge(p.S["cc"], p.c["cc"])

    with p.frame():
        slotu2 = p.sbuf("l3f_slotu2", [128, 1], U32)
        ytok = p.sbuf("l3f_ytok", [128, C], F32)
        xtrow = p.sbuf("l3f_xtrow", [128, C], F32R)
        sq = p.sbuf("l3f_sq", [128, C], F32)
        st4 = p.sbuf("l3f_st4", [128, 6], F32)

        for tile in range(NT):
            with nc.Block() as blk:
                @blk.vector
                def _(v):
                    vi = vch(v, p)
                    vi(v.tensor_scalar_min(ytok[:, 0:1], slots_sb[:, tile:tile + 1],
                                           float(NEXP * ECAP - 1)))
                    vi(v.tensor_copy(slotu2[:], ytok[:, 0:1]))
                    vi(v.memset(ytok[:], 0.0))

                @blk.sync
                def _(e):
                    ins = e.dma_start(xtrow[:],
                                      xtok[tile * 128:(tile + 1) * 128, :])
                    p.inc(ins, "dsp", 16)
                    e.wait_ge(p.S["dsp"], p.c["dsp"])

            with nc.Block() as blk:
                @blk.gpsimd
                def _(g):
                    ins = g.indirect_dma_start(
                        ytok[:].bitcast(F32R), None,
                        yg, IndirectOffsetOnAxis(slotu2[:], 0))
                    p.inc(ins, "dpool", 16)
                    g.wait_ge(p.S["dpool"], p.c["dpool"])

            with nc.Block() as blk:
                s0 = p.c["s"]
                v0 = p.c["v"]

                @blk.vector
                def _(v):
                    vi = vch(v, p)
                    vi(v.tensor_scalar(ytok[:], ytok[:],
                                       acc_sb[:, tile:tile + 1],
                                       gate_sb[:, tile:tile + 1],
                                       ALU.mult, ALU.mult))
                    vi(v.scalar_tensor_tensor(ytok[:], xtrow[:], 2.0, ytok[:],
                                              ALU.mult, ALU.add))
                    vi(v.reduce_sum(st4[:, 0:1], ytok[:], axis=AX.X))
                    vi(v.tensor_scalar(st4[:, 1:2], st4[:, 0:1], 1.0 / C,
                                       None, ALU.mult))
                    v.wait_ge(p.S["s"], s0 + 1)
                    vi(v.reduce_sum(st4[:, 2:3], sq[:], axis=AX.X))
                    vi(v.tensor_scalar(st4[:, 2:3], st4[:, 2:3], 1.0 / C,
                                       None, ALU.mult))
                    vi(v.scalar_tensor_tensor(st4[:, 3:4], st4[:, 1:2],
                                              st4[:, 1:2], st4[:, 2:3],
                                              ALU.mult, ALU.subtract))
                    vi(v.tensor_scalar(st4[:, 3:4], st4[:, 3:4], -1.0,
                                       LN_EPS, ALU.mult, ALU.add))
                    v.wait_ge(p.S["s"], s0 + 2)
                    vi(v.reciprocal(st4[:, 4:5], st4[:, 4:5]))
                    vi(v.tensor_scalar(ytok[:], ytok[:], st4[:, 1:2],
                                       st4[:, 4:5], ALU.subtract, ALU.mult))

                @blk.scalar
                def _(s):
                    si = sch(s, p)
                    s.wait_ge(p.S["v"], v0 + 2)
                    si(s.activation(sq[:], ytok[:], AF.Square))
                    s.wait_ge(p.S["v"], v0 + 8)
                    si(s.activation(st4[:, 4:5], st4[:, 3:4], AF.Sqrt))

            with nc.Block() as blk:
                @blk.gpsimd
                def _(g):
                    ins = g.dma_start(
                        out_final[tile * 128:(tile + 1) * 128, :], ytok[:])
                    p.inc(ins, "dpool", 16)
                    g.wait_ge(p.S["dpool"], p.c["dpool"])



def msel_const():
    M = np.zeros((80, 48), np.float32)
    for n in range(16):
        M[n, n] = 1.0
        M[16 + 2 * n, 16 + n] = 0.5
        M[17 + 2 * n, 16 + n] = 0.5
        M[48 + 2 * n, 32 + n] = 0.5
        M[49 + 2 * n, 32 + n] = 0.5
    return M


def build_full():
    nc = bass.Bass("TRN2", target_bir_lowering=False)

    def din(name, shape, dt=F32R):
        return nc.dram_tensor(name, list(shape), dt, kind="ExternalInput").ap()

    x0 = din("x0", [C, T])
    out = nc.dram_tensor("out", [T, C], F32, kind="ExternalOutput").ap()

    W0a = dict(
        kwT=din("l0_kwT", [C, C]), vwT=din("l0_vwT", [C, C]),
        rwT=din("l0_rwT", [C, C]), owT=din("l0_owT", [C, C]),
        fkwT=din("l0_fkwT", [C, 4096]), frwT=din("l0_frwT", [C, C]),
        fvwT=din("l0_fvwT", [4096, C]),
        **{k: din("l0_" + k, [128, 8], F32) for k in
           ["ln1w", "ln1b", "ln2w", "ln2b", "decay", "tfirst", "tmk", "tmv",
            "tmr", "ftmk", "ftmr"]},
    )
    W0f = dict(
        w1T=din("l0f_w1T", [C, 4096]), w2T=din("l0f_w2T", [C, 4096]),
        h1T=din("l0f_h1T", [4096, 4096]), h2T=din("l0f_h2T", [4096, 4096]),
        w3T=din("l0f_w3T", [4096, C]),
        s1=din("l0f_s1", [128, 32], F32), s2=din("l0f_s2", [128, 32], F32),
        sh1=din("l0f_sh1", [128, 32], F32), sh2=din("l0f_sh2", [128, 32], F32),
        s3=din("l0f_s3", [128, 8], F32),
    )
    W1a = dict(
        wq=din("l1_wq", [C, C]), wk=din("l1_wk", [C, C]), wv=din("l1_wv", [C, C]),
        wo=din("l1_wo", [C, C]), w1=din("l1_w1", [C, 4096]),
        w2=din("l1_w2", [4096, C]),
        b1=din("l1_b1", [128, 32], F32),
        **{k: din("l1_" + k, [128, 8], F32) for k in
           ["bq", "bk", "bv", "bo", "b2", "g1", "be1", "g2", "be2"]},
    )
    W1f = dict(
        wrT=din("l1f_wrT", [C, C]), wkT=din("l1f_wkT", [C, C]),
        wvT=din("l1f_wvT", [C, C]),
        mu_r=din("l1f_mur", [128, 8], F32), mu_k=din("l1f_muk", [128, 8], F32),
    )
    W2a = dict(
        convw=din("l2_convw", [128, 32], F32), convb=din("l2_convb", [128, 8], F32),
        win=din("l2_win", [C, 80]), b_in=din("l2_bin", [80, 16]),
        msel=din("l2_msel", [80, 48]), wout=din("l2_wout", [16, C]),
        b_out=din("l2_bout", [128, 8], F32), alog=din("l2_alog", [16, 1], F32),
    )
    W2f = dict(
        g1r=din("l2f_g1r", [32, 128]), g1i=din("l2f_g1i", [32, 128]),
        g2r=din("l2f_g2r", [16, 64]), g2i=din("l2f_g2i", [16, 64]),
        bias=din("l2f_bias", [128, 8], F32),
    )
    W3f = dict(
        w_router=din("l3_router", [C, 32]),
        wg=din("l3_wg", [4, C, C]), wu=din("l3_wu", [4, C, C]),
        wd=din("l3_wd", [4, C, C]),
        coremask=din("l3_cmask", [8, 1]),
        cb=din("l3_cb", [128, 1], F32), cb2=din("l3_cb2", [128, 1], F32),
    )
    cd = {k: nc.dram_tensor("c_" + k, list(v.shape), const_dtype(k),
                            kind="ExternalInput").ap()
          for k, v in host_consts().items()}

    xA = nc.dram_tensor("xA", [C, T], F32R, kind="Internal").ap()
    xB = nc.dram_tensor("xB", [C, T], F32R, kind="Internal").ap()

    with ExitStack() as stack:
        p = P(nc, stack)
        cn = Consts(p, cd)
        cn.emit_load(p)
        with p.frame():
            emit_l0_attn(p, cn, W0a, x0, xA)
        with p.frame():
            emit_l0_ffn(p, cn, W0f, xA, xB)
        with p.frame():
            emit_l1_attn(p, cn, W1a, xB, xA)
        with p.frame():
            emit_l1_ffn(p, cn, W1f, xA, xB)
        with p.frame():
            emit_l2_attn(p, cn, W2a, xB, xA)
        with p.frame():
            emit_l2_ffn(p, cn, W2f, xA, xB)
        with p.frame():
            emit_l3_attn(p, cn, xB, xA)
        with p.frame():
            emit_l3_ffn(p, cn, W3f, xA, out)
    return nc


def _ct(a):
    return np.ascontiguousarray(np.asarray(a, np.float32))


def kernel(**inputs):
    g = lambda k: np.asarray(inputs[k], np.float32)
    consts = {"c_" + k: v for k, v in host_consts().items()}

    shared = dict(
        l0_kwT=_ct(g("L0_attn_att_kw").T), l0_vwT=_ct(g("L0_attn_att_vw").T),
        l0_rwT=_ct(g("L0_attn_att_rw").T), l0_owT=_ct(g("L0_attn_att_ow").T),
        l0_fkwT=_ct(g("L0_attn_f_kw").T), l0_frwT=_ct(g("L0_attn_f_rw").T),
        l0_fvwT=_ct(g("L0_attn_f_vw").T),
        l0_ln1w=vec8(g("L0_attn_ln1_w")), l0_ln1b=vec8(g("L0_attn_ln1_b")),
        l0_ln2w=vec8(g("L0_attn_ln2_w")), l0_ln2b=vec8(g("L0_attn_ln2_b")),
        l0_decay=vec8(g("L0_attn_time_decay")),
        l0_tfirst=vec8(g("L0_attn_time_first")),
        l0_tmk=vec8(g("L0_attn_tmk")), l0_tmv=vec8(g("L0_attn_tmv")),
        l0_tmr=vec8(g("L0_attn_tmr")),
        l0_ftmk=vec8(g("L0_attn_f_tmk")), l0_ftmr=vec8(g("L0_attn_f_tmr")),
        l0f_w1T=_ct(g("L0_ffn_w1_w").T), l0f_w2T=_ct(g("L0_ffn_w2_w").T),
        l0f_h1T=_ct(g("L0_ffn_h1_w").T), l0f_h2T=_ct(g("L0_ffn_h2_w").T),
        l0f_w3T=_ct(g("L0_ffn_w3_w").T),
        l0f_s1=_ct(g("L0_ffn_w1_s").reshape(-1, 128).T),
        l0f_s2=_ct(g("L0_ffn_w2_s").reshape(-1, 128).T),
        l0f_sh1=_ct(g("L0_ffn_h1_s").reshape(-1, 128).T),
        l0f_sh2=_ct(g("L0_ffn_h2_s").reshape(-1, 128).T),
        l0f_s3=vec8(g("L0_ffn_w3_s")),
        l1_wq=_ct(g("L1_attn_Wq")), l1_wk=_ct(g("L1_attn_Wk")),
        l1_wv=_ct(g("L1_attn_Wv")), l1_wo=_ct(g("L1_attn_Wo")),
        l1_w1=_ct(g("L1_attn_W1")), l1_w2=_ct(g("L1_attn_W2")),
        l1_bq=vec8(g("L1_attn_bq")), l1_bk=vec8(g("L1_attn_bk")),
        l1_bv=vec8(g("L1_attn_bv")), l1_bo=vec8(g("L1_attn_bo")),
        l1_b1=_ct(g("L1_attn_b1").reshape(-1, 128).T),
        l1_b2=vec8(g("L1_attn_b2")),
        l1_g1=vec8(g("L1_attn_g1")), l1_be1=vec8(g("L1_attn_be1")),
        l1_g2=vec8(g("L1_attn_g2")), l1_be2=vec8(g("L1_attn_be2")),
        l1f_wrT=_ct(g("L1_ffn_W_r").T), l1f_wkT=_ct(g("L1_ffn_W_k").T),
        l1f_wvT=_ct(g("L1_ffn_W_v").T),
        l1f_mur=vec8(g("L1_ffn_mu_r")), l1f_muk=vec8(g("L1_ffn_mu_k")),
        l2_convw=_ct(g("L2_attn_conv_w")[:, 0, :].reshape(8, 128, 4)
                     .transpose(1, 0, 2).reshape(128, 32)),
        l2_convb=vec8(g("L2_attn_conv_b")),
        l2_win=_ct(g("L2_attn_Win")),
        l2_bin=_ct(np.repeat(g("L2_attn_b_in").reshape(80, 1), 16, 1)),
        l2_msel=msel_const(),
        l2_wout=_ct(g("L2_attn_Wout")),
        l2_bout=vec8(g("L2_attn_b_out")),
        l2_alog=_ct(g("L2_attn_A_log").reshape(16, 1)),
        l2f_g1r=_ct(g("L2_ffn_G1_R")[0].transpose(2, 1, 0).reshape(32, 128)),
        l2f_g1i=_ct(g("L2_ffn_G1_I")[0].transpose(2, 1, 0).reshape(32, 128)),
        l2f_g2r=_ct(g("L2_ffn_G2_R")[:, :, 0, :].transpose(2, 0, 1).reshape(16, 64)),
        l2f_g2i=_ct(g("L2_ffn_G2_I")[:, :, 0, :].transpose(2, 0, 1).reshape(16, 64)),
        l2f_bias=vec8(np.concatenate([g("L2_ffn_bias_real"),
                                      g("L2_ffn_bias_imag")])),
        l3_router=_ct(g("L3_ffn_w_router")),
    )
    shared.update(consts)

    x = g("L0_attn_x")
    wg = g("L3_ffn_w_gate")
    wu = g("L3_ffn_w_up")
    wd = g("L3_ffn_w_down")
    in_maps = []
    for c in range(NCORE):
        m = dict(shared)
        m["x0"] = _ct(x[c].T)
        m["l3_wg"] = _ct(wg[c * 4:(c + 1) * 4])
        m["l3_wu"] = _ct(wu[c * 4:(c + 1) * 4])
        m["l3_wd"] = _ct(wd[c * 4:(c + 1) * 4])
        m["l3_cmask"] = _ct((np.arange(8) < c).reshape(8, 1))
        m["l3_cb"] = np.full((128, 1), float(c * T), np.float32)
        m["l3_cb2"] = np.full((128, 1), float(c * 4 * ECAP), np.float32)
        in_maps.append(m)

    nc = build_full()
    res = run_bass_kernel_spmd(nc, in_maps, core_ids=list(range(NCORE)))
    return np.stack([np.asarray(res.results[c]["out"]) for c in range(NCORE)])


# version 05
# speedup vs baseline: 1.0039x; 1.0039x over previous
import sys
if "/opt/trn_rl_repo" not in sys.path:
    sys.path.insert(0, "/opt/trn_rl_repo")
import numpy as np
from contextlib import ExitStack

import concourse.bass as bass
import concourse.mybir as mybir
from concourse.bass_utils import run_bass_kernel_spmd
from concourse.bass_core_rust import IndirectOffsetOnAxis

F32 = mybir.dt.float32
F32R = mybir.dt.float32r
BF16 = mybir.dt.bfloat16
U32 = mybir.dt.uint32
AF = mybir.ActivationFunctionType
ALU = mybir.AluOpType
AX = mybir.AxisListType

T = 4096
C = 1024
TC = 512
NCH = T // TC
NCT = C // 128
NCORE = 8

LN_EPS = 1e-5


class P:

    def __init__(self, nc, stack):
        self.nc = nc
        self.stack = stack
        self.frames = [stack]
        self.c = {}
        self.S = {}
        self.uid = 0
        for name in ["pe", "v", "s", "g", "dsp", "dact", "dpool", "cc"]:
            self.S[name] = stack.enter_context(nc.semaphore("sem_" + name))
            self.c[name] = 0

    def nm(self, base):
        self.uid += 1
        return f"{base}_{self.uid}"

    def sbuf(self, base, shape, dtype):
        return self.frames[-1].enter_context(self.nc.sbuf_tensor(self.nm(base), shape, dtype))

    def psum(self, base, shape, dtype=F32):
        return self.frames[-1].enter_context(self.nc.psum_tensor(self.nm(base), shape, dtype))

    from contextlib import contextmanager as _cm

    @_cm
    def frame(self):
        st = ExitStack()
        self.frames.append(st)
        try:
            with st:
                yield
        finally:
            self.frames.pop()

    def dram(self, base, shape, dtype):
        return self.nc.dram_tensor(self.nm(base), shape, dtype, kind="Internal").ap()

    def inc(self, ins, sem, n=1):
        ins.then_inc(self.S[sem], n)
        self.c[sem] += n
        return self.c[sem]

    def wait(self, eng, sem, cnt=None):
        eng.wait_ge(self.S[sem], self.c[sem] if cnt is None else cnt)

    def vdo(self, ins):
        return self.inc(ins, "v")

    def sdo(self, ins):
        return self.inc(ins, "s")

    def gdo(self, ins):
        return self.inc(ins, "g")


def vch(v, p):
    def issue(ins):
        p.inc(ins, "v")
        v.wait_ge(p.S["v"], p.c["v"])
        return ins
    return issue


def sch(s, p):
    def issue(ins):
        p.inc(ins, "s")
        s.wait_ge(p.S["s"], p.c["s"])
        return ins
    return issue


def gch(g, p):
    def issue(ins):
        p.inc(ins, "g")
        g.wait_ge(p.S["g"], p.c["g"])
        return ins
    return issue


def dram_cm(x):
    return x.rearrange("(ct p) t -> p ct t", p=128)


def load_cm_chunk(e, p, dst, dst_off, x_dram, t0, t1, nct=NCT, stride=None):
    w = t1 - t0
    st = stride if stride is not None else w
    dview = dst[:].rearrange("p (ct t) -> p ct t", t=st)[:, 0:nct, dst_off:dst_off + w]
    sview = dram_cm(x_dram)[:, 0:nct, t0:t1]
    return e.dma_start(dview, sview)


def store_cm_chunk(e, p, src, src_off, x_dram, t0, t1, nct=NCT, stride=None):
    w = t1 - t0
    st = stride if stride is not None else w
    sview = src[:].rearrange("p (ct t) -> p ct t", t=st)[:, 0:nct, src_off:src_off + w]
    dview = dram_cm(x_dram)[:, 0:nct, t0:t1]
    return e.dma_start(dview, sview)


class Consts:

    def __init__(self, p, consts_dram):
        nc = p.nc
        self.tt_e1 = p.sbuf("tt_e1", [32, 512], F32R)
        self.tt_e2 = p.sbuf("tt_e2", [16, 128], F32R)
        self.sel_lo = p.sbuf("sel_lo", [1, 128], F32R)
        self.sel_hi = p.sbuf("sel_hi", [1, 128], F32R)
        self.ones_col = p.sbuf("ones_col", [128, 1], F32R)
        self.ones_row = p.sbuf("ones_row", [1, 128], F32R)
        self.ident = p.sbuf("ident", [128, 128], F32)
        self.ident_r = p.sbuf("ident_r", [128, 128], F32R)
        self.ident_bf = p.sbuf("ident_bf", [128, 128], BF16)
        self.ustrict = p.sbuf("ustrict", [128, 128], F32R)
        self.iota32 = p.sbuf("iota32", [128, 32], F32)
        self.iota_col = p.sbuf("iota_col", [128, 1], F32)
        self.ones_bf_col = p.sbuf("ones_bf_col", [128, 1], BF16)
        self.c = consts_dram

    def emit_load(self, p):
        nc = p.nc
        with nc.Block() as blk:
            @blk.sync
            def _(e):
                e.dma_start(self.ident[:], self.c["ident"]).then_inc(p.S["dsp"], 16)
                e.dma_start(self.ident_r[:], self.c["ident_r"]).then_inc(p.S["dsp"], 16)
                p.c["dsp"] += 16
                e.dma_start(self.ident_bf[:], self.c["ident_bf"]).then_inc(p.S["dsp"], 16)
                e.dma_start(self.ustrict[:], self.c["ustrict"]).then_inc(p.S["dsp"], 16)
                e.dma_start(self.ones_col[:], self.c["ones_col"]).then_inc(p.S["dsp"], 16)
                e.dma_start(self.ones_row[:], self.c["ones_col"].rearrange("a b -> b a")).then_inc(p.S["dsp"], 16)
                e.dma_start(self.ones_bf_col[:], self.c["ones_bf_col"]).then_inc(p.S["dsp"], 16)
                e.dma_start(self.tt_e1[:], self.c["tt_e1"]).then_inc(p.S["dsp"], 16)
                e.dma_start(self.tt_e2[:], self.c["tt_e2"]).then_inc(p.S["dsp"], 16)
                e.dma_start(self.sel_lo[:], self.c["selhalf"][0:1, :]).then_inc(p.S["dsp"], 16)
                e.dma_start(self.sel_hi[:], self.c["selhalf"][1:2, :]).then_inc(p.S["dsp"], 16)
                p.c["dsp"] += 160
                e.wait_ge(p.S["dsp"], p.c["dsp"])

            @blk.gpsimd
            def _(g):
                gi = gch(g, p)
                gi(g.iota(self.iota32[:], [[1, 32]], base=0, channel_multiplier=0,
                          allow_small_or_imprecise_dtypes=True))
                gi(g.iota(self.iota_col[:], [[0, 1]], base=0, channel_multiplier=1,
                          allow_small_or_imprecise_dtypes=True))


def const_dtype(k):
    if k in ("ustrict", "ones_col", "ones_row", "tt_e1", "tt_e2", "ident_r",
             "selhalf"):
        return F32R
    if k.endswith("_bf") or k == "ones_bf_col":
        return BF16
    return F32


def host_consts():
    import ml_dtypes
    ident = np.eye(128, dtype=np.float32)
    ident_bf = np.eye(128, dtype=np.float32).astype(ml_dtypes.bfloat16)
    iu = (np.arange(128)[:, None] < np.arange(128)[None, :]).astype(np.float32)
    e1 = np.zeros((32, 512), np.float32)
    for j in range(512):
        e1[j // 16, j] = 1.0
    e2 = np.zeros((16, 128), np.float32)
    for j in range(128):
        e2[j % 16, j] = 1.0
    sh = np.zeros((2, 128), np.float32)
    sh[0, :64] = 1.0
    sh[1, 64:] = 1.0
    return {"ident": ident, "ident_r": ident, "ident_bf": ident_bf, "ustrict": iu,
            "selhalf": sh,
            "tt_e1": e1, "tt_e2": e2,
            "ones_col": np.ones((128, 1), np.float32),
            "ones_bf_col": np.ones((128, 1), np.float32).astype(ml_dtypes.bfloat16)}



def ln_chunk(p, cn, *, z_tiles, out_tiles, g_ap=None, b_ap=None, eps=LN_EPS,
             nct=NCT, width=TC, sq=None, stats=None, ps_sum=None, ps_b=None):
    nc = p.nc
    s_sum = stats[0:1, 0 * width:1 * width]
    s_ssq = stats[0:1, 1 * width:2 * width]
    s_nmean = stats[0:1, 2 * width:3 * width]
    s_rstd = stats[0:1, 3 * width:4 * width]
    mb = ps_b[:, 0:width]
    rb = ps_b[:, width:2 * width]

    sc_marks = []
    with nc.Block() as blk:
        pe0 = p.c["pe"]

        @blk.scalar
        def _(s):
            si = sch(s, p)
            for ci in range(nct):
                if ci > 0:
                    s.wait_ge(p.S["pe"], pe0 + 2 * ci)
                si(s.activation(sq[:], z_tiles(ci), AF.Square))
                sc_marks.append(p.c["s"])

        @blk.tensor
        def _(t):
            for ci in range(nct):
                t.wait_ge(p.S["s"], sc_marks[ci])
                ins = t.matmul(ps_sum[0:1, 0:width], cn.ones_col[:], z_tiles(ci),
                               start=(ci == 0), stop=(ci == nct - 1))
                p.inc(ins, "pe")
                ins = t.matmul(ps_sum[0:1, width:2 * width], cn.ones_col[:], sq[:],
                               start=(ci == 0), stop=(ci == nct - 1))
                p.inc(ins, "pe")

        pe_done = p.c["pe"]

        @blk.vector
        def _(v):
            vi = vch(v, p)
            v.wait_ge(p.S["pe"], pe_done)
            vi(v.tensor_scalar(s_nmean, ps_sum[0:1, 0:width], -1.0 / float(C), None, ALU.mult))
            vi(v.tensor_scalar(s_rstd, ps_sum[0:1, width:2 * width], 1.0 / float(C), None, ALU.mult))
            vi(v.scalar_tensor_tensor(s_ssq, s_nmean, 1.0,
                                      s_nmean, ALU.mult, ALU.mult))
            vi(v.tensor_sub(s_rstd, s_rstd, s_ssq))
            vi(v.tensor_scalar(s_rstd, s_rstd, float(eps),
                               None, ALU.add))

        vmark = p.c["v"]

        @blk.scalar
        def _(s):
            si = sch(s, p)
            s.wait_ge(p.S["v"], vmark)
            si(s.activation(s_rstd, s_rstd, AF.Ln))
            si(s.activation(s_rstd, s_rstd, AF.Exp, scale=-0.5))

    with nc.Block() as blk:
        pe0 = p.c["pe"]

        @blk.tensor
        def _(t):
            ins = t.matmul(mb, cn.ones_row[:], s_nmean, start=True, stop=True)
            p.inc(ins, "pe")
            ins = t.matmul(rb, cn.ones_row[:], s_rstd, start=True, stop=True)
            p.inc(ins, "pe")

        pe_done = p.c["pe"]

        @blk.vector
        def _(v):
            vi = vch(v, p)
            v.wait_ge(p.S["pe"], pe_done)
            for ci in range(nct):
                vi(v.tensor_add(out_tiles(ci), z_tiles(ci), mb))
                vi(v.tensor_mul(out_tiles(ci), out_tiles(ci), rb))
                if g_ap is not None:
                    vi(v.tensor_scalar(out_tiles(ci),
                                       out_tiles(ci),
                                       g_ap(ci), b_ap(ci), ALU.mult, ALU.add))


def make_ln_scratch(p, width=TC):
    return dict(
        sq=p.sbuf("lns_sq", [128, width], F32R),
        stats=p.sbuf("lns_stats", [1, 4 * width], F32R),
        ps_sum=p.psum("lns_psum", [1, 2 * width], F32),
        ps_b=p.psum("lns_pb", [128, 2 * width], F32),
    )



def emit_l1_ffn(p, cn, W, xin, xout):
    nc = p.nc
    w_sb = p.sbuf("l1f_w", [128, 2 * NCT * C], F32R)
    wcols = [p.sbuf("l1f_wc", [128, NCT * 128], F32R) for _ in range(2)]
    mu = p.sbuf("l1f_mu", [128, 16], F32)
    xch = p.sbuf("l1f_x", [128, NCT * (TC + 1)], F32R)
    xr = p.sbuf("l1f_xr", [128, NCT * TC], F32R)
    xk = p.sbuf("l1f_xk", [128, NCT * TC], F32R)
    rsig = p.sbuf("l1f_r", [128, NCT * TC], F32)
    k2 = p.sbuf("l1f_k2", [128, NCT * TC], F32R)
    yv = p.sbuf("l1f_y", [128, NCT * TC], F32R)
    lns = make_ln_scratch(p)
    psm = [p.psum("l1f_ps", [128, TC], F32) for _ in range(4)]

    def wtile(which, ot, ci):
        base = which * NCT * C
        return w_sb[:, base + ci * C + ot * 128: base + ci * C + ot * 128 + 128]

    with nc.Block() as blk:
        @blk.sync
        def _(e):
            for wi, nmw in enumerate(["wrT", "wkT"]):
                e.dma_start(
                    w_sb[:, wi * NCT * C:(wi + 1) * NCT * C]
                    .rearrange("p (ci o) -> p ci o", ci=NCT),
                    W[nmw].rearrange("(ci p) o -> p ci o", p=128),
                ).then_inc(p.S["dsp"], 16)
            e.dma_start(mu[:, 0:8], W["mu_r"]).then_inc(p.S["dsp"], 16)
            e.dma_start(mu[:, 8:16], W["mu_k"]).then_inc(p.S["dsp"], 16)
            p.c["dsp"] += 4 * 16
            e.wait_ge(p.S["dsp"], p.c["dsp"])

    for ch in range(NCH):
        t0, t1 = ch * TC, (ch + 1) * TC

        def xt(ci, off=1, w=TC):
            return xch[:, ci * (TC + 1) + off: ci * (TC + 1) + off + w]

        with nc.Block() as blk:
            @blk.sync
            def _(e):
                if ch == 0:
                    ins = load_cm_chunk(e, p, xch, 1, xin, 0, TC, stride=TC + 1)
                    p.inc(ins, "dsp", 16)
                else:
                    ins = load_cm_chunk(e, p, xch, 0, xin, t0 - 1, t1, stride=TC + 1)
                    p.inc(ins, "dsp", 16)
                e.wait_ge(p.S["dsp"], p.c["dsp"])

            if ch == 0:
                @blk.vector
                def _(v):
                    vi = vch(v, p)
                    for ci in range(NCT):
                        vi(v.memset(xt(ci, 0, 1).bitcast(F32), 0.0))

        with nc.Block() as blk:
            @blk.vector
            def _(v):
                vi = vch(v, p)
                for ci in range(NCT):
                    dd = xr[:, ci * TC:(ci + 1) * TC]
                    vi(v.tensor_sub(dd, xt(ci, 1), xt(ci, 0)))
                    vi(v.scalar_tensor_tensor(
                        xk[:, ci * TC:(ci + 1) * TC], dd,
                        mu[:, 8 + ci:9 + ci], xt(ci, 0), ALU.mult, ALU.add))
                    vi(v.scalar_tensor_tensor(
                        xr[:, ci * TC:(ci + 1) * TC], dd,
                        mu[:, ci:ci + 1], xt(ci, 0), ALU.mult, ALU.add))

        with nc.Block() as blk:
            pe0 = p.c["pe"]
            sc0 = p.c["s"]

            @blk.tensor
            def _(t):
                for ot in range(NCT):
                    for ci in range(NCT):
                        ins = t.matmul(psm[ot % 2][:], wtile(0, ot, ci),
                                       xr[:, ci * TC:(ci + 1) * TC],
                                       start=(ci == 0), stop=(ci == NCT - 1))
                    p.inc(ins, "pe")
                    for ci in range(NCT):
                        ins = t.matmul(psm[2 + ot % 2][:], wtile(1, ot, ci),
                                       xk[:, ci * TC:(ci + 1) * TC],
                                       start=(ci == 0), stop=(ci == NCT - 1))
                    p.inc(ins, "pe")
                    if ot < NCT - 1:
                        t.wait_ge(p.S["s"], sc0 + 3 * ot + 1)

            @blk.scalar
            def _(s):
                si = sch(s, p)
                for ot in range(NCT):
                    s.wait_ge(p.S["pe"], pe0 + 2 * ot + 1)
                    si(s.activation(rsig[:, ot * TC:(ot + 1) * TC], psm[ot % 2][:],
                                    AF.Sigmoid))
                    s.wait_ge(p.S["pe"], pe0 + 2 * ot + 2)
                    si(s.activation(k2[:, ot * TC:(ot + 1) * TC],
                                    psm[2 + ot % 2][:], AF.Relu))
                    si(s.activation(k2[:, ot * TC:(ot + 1) * TC],
                                    k2[:, ot * TC:(ot + 1) * TC], AF.Square))

        def ep_kv(eng, issue, ot, ps):
            issue(eng.tensor_mul(yv[:, ot * TC:(ot + 1) * TC],
                                 rsig[:, ot * TC:(ot + 1) * TC], ps))
        _proj_stream(p, cn, wT=W["wvT"], Kd=C, Nd=C,
                     x_tiles=lambda ci: k2[:, ci * TC:(ci + 1) * TC],
                     ep_fn=ep_kv, ep_eng="v", ep_nops=1, psms=psm[:2],
                     wcols=wcols)

        with nc.Block() as blk:
            @blk.vector
            def _(v):
                vi = vch(v, p)
                for ci in range(NCT):
                    vi(v.tensor_add(yv[:, ci * TC:(ci + 1) * TC],
                                    yv[:, ci * TC:(ci + 1) * TC],
                                    xt(ci, 1)))

        ln_chunk(p, cn,
                 z_tiles=lambda ci: yv[:, ci * TC:(ci + 1) * TC],
                 out_tiles=lambda ci: xr[:, ci * TC:(ci + 1) * TC],
                 **{k: v for k, v in lns.items()})

        with nc.Block() as blk:
            @blk.sync
            def _(e):
                ins = store_cm_chunk(e, p, xr, 0, xout, t0, t1)
                p.inc(ins, "dsp", 16)
                e.wait_ge(p.S["dsp"], p.c["dsp"])



def vec8(v):
    return np.ascontiguousarray(np.asarray(v, np.float32).reshape(NCT, 128).T)





class Ep:

    def __init__(self, eng, nops, fn):
        self.eng = eng
        self.nops = nops
        self.fn = fn


class Ep3:

    def __init__(self, nv_pre, ns, nv_post, fn_vpre, fn_s, fn_vpost):
        self.eng = "sv3"
        self.nv_pre = nv_pre
        self.ns = ns
        self.nv_post = nv_post
        self.fn_vpre = fn_vpre
        self.fn_s = fn_s
        self.fn_vpost = fn_vpost


def lin_pass(p, cn, *, wT=None, w_sb=None, Kd, Nd, src=None, src_sb=None,
             dst=None, ep=None, aux=None, finalize=None):
    with p.frame():
        _lin_pass(p, cn, wT=wT, w_sb=w_sb, Kd=Kd, Nd=Nd, src=src, src_sb=src_sb,
                  dst=dst, ep=ep, aux=aux, finalize=finalize)


def _lin_pass(p, cn, *, wT=None, w_sb=None, Kd, Nd, src=None, src_sb=None,
              dst=None, ep=None, aux=None, finalize=None):
    nc = p.nc
    kt, nt = Kd // 128, Nd // 128
    need_pp = (Kd * Nd * 4) // 128 + 2 * (Kd * TC * 4) // 128 + 40960
    w_res = w_sb is not None or (wT is not None and need_pp <= nc.sbuf_bytes_remaining)
    if w_sb is None and w_res:
        w_sb = p.sbuf("lp_w", [128, kt * Nd], F32R)
        with nc.Block() as blk:
            @blk.sync
            def _(e):
                ins = e.dma_start(
                    w_sb[:].rearrange("p (ci o) -> p ci o", ci=kt),
                    wT.rearrange("(ci p) o -> p ci o", p=128))
                p.inc(ins, "dsp", 16)
                e.wait_ge(p.S["dsp"], p.c["dsp"])
    wcols = None if w_res else [p.sbuf("lp_wc", [128, kt * 128], F32R)
                                for _ in range(2)]
    collect = dst is None
    nsrc = 1 if (collect or src_sb is not None) else 2
    srcbs = [src_sb] if src_sb is not None else         [p.sbuf("lp_src", [128, kt * TC], F32R) for _ in range(nsrc)]
    stages = None if collect else [p.sbuf("lp_stg", [128, TC], F32R) for _ in range(2)]
    out_sb = p.sbuf("lp_out", [128, nt * TC], F32R) if collect else None
    auxb = [p.sbuf("lp_aux", [128, TC], F32R) for _ in range(2)] if aux else None
    collect_pre = dst is None
    NB = 2 if collect_pre else 4
    psm = [p.psum("lp_ps", [128, TC], F32) for _ in range(NB)]
    epname = ep.eng
    scrs = [p.sbuf("lp_scr", [128, TC], F32R) for _ in range(2)] \
        if epname == "sv3" else None
    fin_eng = "v" if epname in ("v", "sv3") else "s"

    if epname == "sv3":
        nfin = ep.nv_pre + ep.nv_post
    else:
        nfin = ep.nops

    if collect:
        for ch in range(NCH):
            t0, t1 = ch * TC, (ch + 1) * TC
            srcb = srcbs[0]
            with nc.Block() as blk:
                @blk.sync
                def _(e):
                    if src is not None:
                        ins = load_cm_chunk(e, p, srcb, 0, src, t0, t1, nct=kt)
                        p.inc(ins, "dsp", 16)
                    if not w_res:
                        ins = e.dma_start(
                            wcols[0][:].rearrange("p (ci o) -> p ci o", ci=kt),
                            wT.rearrange("(ci p) o -> p ci o", p=128)[:, :, 0:128])
                        p.inc(ins, "dsp", 16)
                    e.wait_ge(p.S["dsp"], p.c["dsp"])
            pe0 = p.c["pe"]
            v00 = p.c["v"]
            s00 = p.c["s"]

            def fin_count(k):
                return (v00 if fin_eng == "v" else s00) + nfin * k

            wready = [p.c["dsp"]] * nt
            with nc.Block() as blk:
                if not w_res:
                    @blk.sync
                    def _(e):
                        for k in range(nt - 1):
                            if k >= 1:
                                e.wait_ge(p.S["pe"], pe0 + k)
                            ins = e.dma_start(
                                wcols[(k + 1) % 2][:].rearrange(
                                    "p (ci o) -> p ci o", ci=kt),
                                wT.rearrange("(ci p) o -> p ci o", p=128)
                                [:, :, (k + 1) * 128:(k + 2) * 128])
                            p.inc(ins, "dsp", 16)
                            wready[k + 1] = p.c["dsp"]

                @blk.tensor
                def _(t):
                    for ot in range(nt):
                        if not w_res:
                            t.wait_ge(p.S["dsp"], wready[ot])
                        if ot >= 2:
                            t.wait_ge(p.S[fin_eng], fin_count(ot - 1))
                        for ci in range(kt):
                            ins = t.matmul(
                                psm[ot % 2][:],
                                (w_sb[:, ci * Nd + ot * 128:
                                      ci * Nd + ot * 128 + 128] if w_res else
                                 wcols[ot % 2][:, ci * 128:(ci + 1) * 128]),
                                srcb[:, ci * TC:(ci + 1) * TC],
                                start=(ci == 0), stop=(ci == kt - 1))
                        p.inc(ins, "pe")

                def ep_sec(eng):
                    issue = (sch if epname == "s" else vch)(eng, p)
                    for ot in range(nt):
                        eng.wait_ge(p.S["pe"], pe0 + ot + 1)
                        ep.fn(eng, issue, ot,
                              psm[ot % 2][:],
                              out_sb[:, ot * TC:(ot + 1) * TC], ch, None)

                if epname == "s":
                    @blk.scalar
                    def _(s):
                        ep_sec(s)
                else:
                    @blk.vector
                    def _(v):
                        ep_sec(v)

            if finalize is not None:
                finalize(ch, out_sb, srcb)
        return

    pe0 = p.c["pe"]
    v00 = p.c["v"]
    s00 = p.c["s"]
    dp0 = p.c["dpool"]

    def fin_count(k):
        return (v00 if fin_eng == "v" else s00) + nfin * k

    def issue_plain(eng_sem):
        def f(ins):
            p.inc(ins, eng_sem)
            return ins
        return f

    sched = []
    if w_res:
        for ch in range(NCH):
            for ot in range(nt):
                sched.append((ch, ot, None))
    else:
        for pr in range(NCH // 2):
            for ot in range(nt):
                for sub in range(2):
                    sched.append((2 * pr + sub, ot, pr * nt + ot))
    NG = len(sched)
    loadmark = [0] * NCH
    wmark = [0] * (NCH // 2 * nt + 1)
    auxmark = [0] * NG
    last_g_of_ch = {}
    for g, (ch, ot, ws) in enumerate(sched):
        last_g_of_ch[ch] = g

    with nc.Block() as blk:
        @blk.sync
        def _(e):
            gw = 0
            if w_res:
                for ch in range(NCH):
                    if src is not None:
                        if ch >= nsrc:
                            e.wait_ge(p.S["pe"],
                                      pe0 + last_g_of_ch[ch - nsrc] + 1)
                        ins = load_cm_chunk(e, p, srcbs[ch % nsrc], 0, src,
                                            ch * TC, (ch + 1) * TC, nct=kt)
                        p.inc(ins, "dsp", 16)
                    loadmark[ch] = p.c["dsp"]
                    if aux:
                        for ot in range(nt):
                            g = ch * nt + ot
                            if g >= 2:
                                e.wait_ge(p.S[fin_eng], fin_count(g - 1))
                            ins = e.dma_start(
                                auxb[g % 2][:],
                                dram_cm(aux)[:, ot, ch * TC:(ch + 1) * TC])
                            p.inc(ins, "dsp", 16)
                            auxmark[g] = p.c["dsp"]
            else:
                for pr in range(NCH // 2):
                    for sub in range(2):
                        ch = 2 * pr + sub
                        if src is not None:
                            if ch >= nsrc:
                                e.wait_ge(p.S["pe"],
                                          pe0 + last_g_of_ch[ch - nsrc] + 1)
                            ins = load_cm_chunk(e, p, srcbs[ch % nsrc], 0, src,
                                                ch * TC, (ch + 1) * TC, nct=kt)
                            p.inc(ins, "dsp", 16)
                        loadmark[ch] = p.c["dsp"]
                    for ot in range(nt):
                        ws = pr * nt + ot
                        if ws >= 2:
                            e.wait_ge(p.S["pe"], pe0 + 2 * (ws - 1))
                        ins = e.dma_start(
                            wcols[ws % 2][:].rearrange("p (ci o) -> p ci o",
                                                       ci=kt),
                            wT.rearrange("(ci p) o -> p ci o", p=128)
                            [:, :, ot * 128:(ot + 1) * 128])
                        p.inc(ins, "dsp", 16)
                        wmark[ws] = p.c["dsp"]

        @blk.tensor
        def _(t):
            for g, (ch, ot, ws) in enumerate(sched):
                need = loadmark[ch]
                if ws is not None:
                    need = max(need, wmark[ws])
                if aux:
                    need = max(need, auxmark[g])
                t.wait_ge(p.S["dsp"], need)
                if g >= NB:
                    t.wait_ge(p.S[fin_eng], fin_count(g - NB + 1))
                for ci in range(kt):
                    ins = t.matmul(
                        psm[g % NB][:],
                        (w_sb[:, ci * Nd + ot * 128:
                              ci * Nd + ot * 128 + 128] if w_res else
                         wcols[ws % 2][:, ci * 128:(ci + 1) * 128]),
                        srcbs[ch % nsrc][:, ci * TC:(ci + 1) * TC],
                        start=(ci == 0), stop=(ci == kt - 1))
                p.inc(ins, "pe")

        if epname == "sv3":
            @blk.vector
            def _(v):
                issue = vch(v, p)
                for g, (ch, ot, ws) in enumerate(sched):
                    v.wait_ge(p.S["pe"], pe0 + g + 1)
                    if aux:
                        v.wait_ge(p.S["dsp"], auxmark[g])
                    if g >= 2:
                        v.wait_ge(p.S["dpool"], dp0 + 16 * (g - 1))
                    ep.fn_vpre(v, issue, ot, psm[g % NB][:],
                               stages[g % 2][:], ch,
                               auxb[g % 2][:] if aux else None,
                               scrs[g % 2][:])
                    v.wait_ge(p.S["s"], s00 + ep.ns * (g + 1))
                    ep.fn_vpost(v, issue, ot, psm[g % NB][:],
                                stages[g % 2][:], ch,
                                auxb[g % 2][:] if aux else None,
                                scrs[g % 2][:])

            @blk.scalar
            def _(s):
                issue = sch(s, p)
                for g, (ch, ot, ws) in enumerate(sched):
                    s.wait_ge(p.S["v"],
                              v00 + (ep.nv_pre + ep.nv_post) * g + ep.nv_pre)
                    ep.fn_s(s, issue, ot, psm[g % NB][:], stages[g % 2][:],
                            ch, auxb[g % 2][:] if aux else None,
                            scrs[g % 2][:])
        else:
            def ep_sec(eng):
                issue = issue_plain(epname)
                for g, (ch, ot, ws) in enumerate(sched):
                    eng.wait_ge(p.S["pe"], pe0 + g + 1)
                    if aux:
                        eng.wait_ge(p.S["dsp"], auxmark[g])
                    if g >= 2:
                        eng.wait_ge(p.S["dpool"], dp0 + 16 * (g - 1))
                    ep.fn(eng, issue, ot, psm[g % NB][:], stages[g % 2][:],
                          ch, auxb[g % 2][:] if aux else None)

            if epname == "s":
                @blk.scalar
                def _(s):
                    ep_sec(s)
            else:
                @blk.vector
                def _(v):
                    ep_sec(v)

        @blk.gpsimd
        def _(g_):
            for g, (ch, ot, ws) in enumerate(sched):
                g_.wait_ge(p.S[fin_eng], fin_count(g + 1))
                ins = g_.dma_start(dram_cm(dst)[:, ot,
                                               ch * TC:(ch + 1) * TC],
                                   stages[g % 2][:])
                p.inc(ins, "dpool", 16)

    with nc.Block() as blk:
        @blk.gpsimd
        def _(g_):
            g_.wait_ge(p.S["dpool"], p.c["dpool"])


def emit_resid_ln(p, cn, lns, *, ch, y_tiles, xres, out_dram, g_ap=None,
                  b_ap=None, xbuf=None, extra_res=None):
    nc = p.nc
    t0, t1 = ch * TC, (ch + 1) * TC
    with nc.Block() as blk:
        @blk.sync
        def _(e):
            ins = load_cm_chunk(e, p, xbuf, 0, xres, t0, t1)
            p.inc(ins, "dsp", 16)
            e.wait_ge(p.S["dsp"], p.c["dsp"])
    with nc.Block() as blk:
        @blk.vector
        def _(v):
            vi = vch(v, p)
            for ci in range(NCT):
                vi(v.tensor_add(y_tiles(ci), y_tiles(ci),
                                xbuf[:, ci * TC:(ci + 1) * TC]))
    ln_chunk(p, cn, z_tiles=y_tiles, out_tiles=y_tiles, g_ap=g_ap, b_ap=b_ap,
             **lns)
    if out_dram is not None:
        with nc.Block() as blk:
            @blk.gpsimd
            def _(g):
                for ci in range(NCT):
                    ins = g.dma_start(dram_cm(out_dram)[:, ci, t0:t1], y_tiles(ci))
                    p.inc(ins, "dpool", 16)
                g.wait_ge(p.S["dpool"], p.c["dpool"])



def emit_l0_ffn(p, cn, W, xin, xout):
    nc = p.nc
    sA = p.dram("l0f_sA", [4096, T], F32R)
    sB = p.dram("l0f_sB", [4096, T], F32R)
    sv = p.sbuf("l0f_scales", [128, 4 * 32 + 8], F32)
    with nc.Block() as blk:
        @blk.sync
        def _(e):
            for i, nmw in enumerate(["s1", "s2", "sh1", "sh2"]):
                ins = e.dma_start(sv[:, i * 32:(i + 1) * 32], W[nmw])
                p.inc(ins, "dsp", 16)
            ins = e.dma_start(sv[:, 128:136], W["s3"])
            p.inc(ins, "dsp", 16)
            e.wait_ge(p.S["dsp"], p.c["dsp"])

    def ep1(eng, issue, ot, ps, stage, ch, auxap):
        issue(eng.activation(stage, ps, AF.Silu, scale=sv[:, ot:ot + 1]))
    lin_pass(p, cn, wT=W["w1T"], Kd=C, Nd=4096, src=xin, dst=sA,
             ep=Ep("s", 1, ep1))

    def ep2(eng, issue, ot, ps, stage, ch, auxap):
        issue(eng.scalar_tensor_tensor(stage, ps, sv[:, 32 + ot:33 + ot],
                                       auxap, ALU.mult, ALU.mult))
    lin_pass(p, cn, wT=W["w2T"], Kd=C, Nd=4096, src=xin, dst=sB,
             ep=Ep("v", 1, ep2), aux=sA)

    def ep3(eng, issue, ot, ps, stage, ch, auxap):
        issue(eng.activation(stage, ps, AF.Gelu, scale=sv[:, 64 + ot:65 + ot]))
    lin_pass(p, cn, wT=W["h1T"], Kd=4096, Nd=4096, src=sB, dst=sA,
             ep=Ep("s", 1, ep3))

    def ep4(eng, issue, ot, ps, stage, ch, auxap):
        issue(eng.activation(stage, ps, AF.Gelu, scale=sv[:, 96 + ot:97 + ot]))
    lin_pass(p, cn, wT=W["h2T"], Kd=4096, Nd=4096, src=sA, dst=sB,
             ep=Ep("s", 1, ep4))

    lns = make_ln_scratch(p)
    xbuf = p.sbuf("l0f_xres", [128, NCT * TC], F32R)

    def ep5(eng, issue, ot, ps, stage, ch, auxap):
        issue(eng.tensor_scalar(stage, ps, sv[:, 128 + ot:129 + ot], None,
                                ALU.mult))

    def fin(ch, out_sb, srcb):
        emit_resid_ln(p, cn, lns, ch=ch,
                      y_tiles=lambda ci: out_sb[:, ci * TC:(ci + 1) * TC],
                      xres=xin, out_dram=xout, xbuf=xbuf)

    lin_pass(p, cn, wT=W["w3T"], Kd=4096, Nd=C, src=sB, dst=None,
             ep=Ep("v", 1, ep5), finalize=fin)



def emit_l2_ffn(p, cn, W, xin, xout):
    nc = p.nc
    w_sb = p.sbuf("l2f_w", [128, NCT * C], F32R)
    y1 = p.sbuf("l2f_y1", [128, 2 * 4 * 128], F32R)
    y2 = p.sbuf("l2f_y2", [128, 3 * 4 * 64], F32)
    g1 = p.sbuf("l2f_g1", [32, 2 * 128], F32R)
    g2 = p.sbuf("l2f_g2", [16, 2 * 64], F32R)
    bias = p.sbuf("l2f_b", [128, 8], F32)
    ps1 = p.psum("l2f_ps1", [128, 128], F32)
    ps2 = p.psum("l2f_ps2", [128, 64], F32)

    with nc.Block() as blk:
        @blk.sync
        def _(e):
            for i, nmw in enumerate(["g1r", "g1i"]):
                ins = e.dma_start(g1[:, i * 128:(i + 1) * 128], W[nmw])
                p.inc(ins, "dsp", 16)
            for i, nmw in enumerate(["g2r", "g2i"]):
                ins = e.dma_start(g2[:, i * 64:(i + 1) * 64], W[nmw])
                p.inc(ins, "dsp", 16)
            ins = e.dma_start(bias[:], W["bias"])
            p.inc(ins, "dsp", 16)
            e.wait_ge(p.S["dsp"], p.c["dsp"])

    for x in range(2):
        for rt in range(4):
            with nc.Block() as blk:
                @blk.tensor
                def _(t):
                    ins = t.matmul(ps1[:], cn.tt_e1[:, rt * 128:(rt + 1) * 128],
                                   g1[:, x * 128:(x + 1) * 128],
                                   start=True, stop=True)
                    p.inc(ins, "pe")
                    ins = t.matmul(ps2[:], cn.tt_e2[:],
                                   g2[:, x * 64:(x + 1) * 64],
                                   start=True, stop=True)
                    p.inc(ins, "pe")

                pe_done = p.c["pe"]

                @blk.vector
                def _(v):
                    vi = vch(v, p)
                    v.wait_ge(p.S["pe"], pe_done)
                    vi(v.tensor_copy(y1[:, (x * 4 + rt) * 128:(x * 4 + rt + 1) * 128],
                                     ps1[:]))
                    vi(v.tensor_copy(y2[:, (x * 4 + rt) * 64:(x * 4 + rt + 1) * 64],
                                     ps2[:]))
                    if x == 1:
                        vi(v.tensor_scalar(
                            y2[:, (8 + rt) * 64:(8 + rt + 1) * 64],
                            ps2[:], -1.0, None, ALU.mult))

    quads = [
        (0, 0, 0, 0),
        (4, 0, 1, 2),
        (0, 512, 1, 1),
        (4, 512, 0, 0),
    ]
    with nc.Block() as blk:
        @blk.vector
        def _(v):
            vi = vch(v, p)
            for cig, obase, x1, x2 in quads:
                for rt in range(4):
                    ci = cig + rt
                    for e in range(16):
                        base_col = ci * C + obase + e
                        tview = w_sb[:].rearrange("p (a b) -> p a b", b=16)[
                            :, (base_col // 16):(base_col // 16) + 32, e:e + 1
                        ].rearrange("p a b -> p (a b)")
                        for c in range(4):
                            y1s = y1[:, (x1 * 4 + rt) * 128 + c * 32:
                                     (x1 * 4 + rt) * 128 + c * 32 + 32]
                            y2c = y2[:, x2 * 4 * 64 + rt * 64 + c * 16 + e:
                                     x2 * 4 * 64 + rt * 64 + c * 16 + e + 1]
                            if c == 0:
                                vi(v.tensor_scalar(tview, y1s, y2c, None, ALU.mult))
                            else:
                                vi(v.scalar_tensor_tensor(tview, y1s, y2c, tview,
                                                          ALU.mult, ALU.add))

    lns = make_ln_scratch(p)
    xbuf = p.sbuf("l2f_xres", [128, NCT * TC], F32R)

    def epb(eng, issue, ot, ps, stage, ch, auxap):
        issue(eng.tensor_scalar(stage, ps, bias[:, ot:ot + 1], None, ALU.add))

    def fin(ch, out_sb, srcb):
        emit_resid_ln(p, cn, lns, ch=ch,
                      y_tiles=lambda ci: out_sb[:, ci * TC:(ci + 1) * TC],
                      xres=xin, out_dram=xout, xbuf=xbuf)

    lin_pass(p, cn, w_sb=w_sb, Kd=C, Nd=C, src=xin, dst=None,
             ep=Ep("v", 1, epb), finalize=fin)



def emit_l2_attn(p, cn, W, xin, xout, dbg=None):
    nc = p.nc
    EPS_SSM = 1e-4
    win = p.sbuf("l2a_win", [128, NCT * 80], F32R)
    msel = p.sbuf("l2a_msel", [80, 48], F32R)
    wout = p.sbuf("l2a_wout", [16, C], F32R)
    convw = p.sbuf("l2a_cw", [128, 32], F32)
    convb = p.sbuf("l2a_cb", [128, 8], F32)
    b_in = p.sbuf("l2a_bin", [80, 16], F32R)
    b_out = p.sbuf("l2a_bout", [128, 8], F32)
    alog = p.sbuf("l2a_alog", [16, 1], F32)
    acons = p.sbuf("l2a_acons", [16, 8], F32)
    xch = p.sbuf("l2a_x", [128, NCT * (TC + 3)], F32R)
    xc = p.sbuf("l2a_xc", [128, NCT * TC], F32R)
    winT = p.sbuf("l2a_winT", [80, NCT * 128], F32R)
    wsel3 = p.sbuf("l2a_wsel3", [128, NCT * 48], F32R)
    bsel = p.sbuf("l2a_bsel", [16, 3], F32)
    sel = p.sbuf("l2a_sel", [16, 3 * TC], F32R)
    tmp = p.sbuf("l2a_tmp", [16, 6 * TC], F32R)
    out_sb = p.sbuf("l2a_out", [128, NCT * TC], F32R)
    xbuf = p.sbuf("l2a_xres", [128, NCT * TC], F32R)

    with nc.Block() as blk:
        @blk.sync
        def _(e):
            ins = e.dma_start(win[:].rearrange("p (ci o) -> p ci o", ci=NCT),
                              W["win"].rearrange("(ci p) o -> p ci o", p=128))
            p.inc(ins, "dsp", 16)
            for nmw, tile in [("msel", msel), ("wout", wout), ("convw", convw),
                              ("convb", convb), ("b_in", b_in), ("b_out", b_out),
                              ("alog", alog)]:
                ins = e.dma_start(tile[:], W[nmw])
                p.inc(ins, "dsp", 16)
            e.wait_ge(p.S["dsp"], p.c["dsp"])

    a_col = acons[:, 0:1]
    mask_c = acons[:, 1:2]
    onem_c = acons[:, 2:3]
    inv_c = acons[:, 3:4]
    state_c = acons[:, 4:5]
    scr = acons[:, 5:6]
    with nc.Block() as blk:
        @blk.scalar
        def _(s):
            si = sch(s, p)
            si(s.activation(a_col, alog[:], AF.Exp, scale=1.0))
        sm = p.c["s"]

        @blk.vector
        def _(v):
            vi = vch(v, p)
            v.wait_ge(p.S["s"], sm)
            vi(v.tensor_scalar(a_col, a_col, -1.0, None, ALU.mult))
            vi(v.tensor_scalar(scr, a_col, -1.0, None, ALU.mult))
            vi(v.tensor_max(scr, scr, a_col))
            vi(v.tensor_scalar(mask_c, scr, -1.0, EPS_SSM, ALU.mult, ALU.add))
        vm = p.c["v"]

        @blk.scalar
        def _(s):
            si = sch(s, p)
            s.wait_ge(p.S["v"], vm)
            si(s.sign(mask_c, mask_c))
            si(s.sign(scr, a_col))
        sm2 = p.c["s"]

        @blk.vector
        def _(v):
            vi = vch(v, p)
            v.wait_ge(p.S["s"], sm2)
            vi(v.tensor_scalar_max(mask_c, mask_c, 0.0))
            vi(v.tensor_scalar(onem_c, mask_c, -1.0, 1.0, ALU.mult, ALU.add))
            vi(v.tensor_scalar(scr, scr, EPS_SSM, EPS_SSM, ALU.mult, ALU.add))
            vi(v.tensor_mul(scr, scr, mask_c))
            vi(v.scalar_tensor_tensor(inv_c, a_col, onem_c, scr, ALU.mult, ALU.add))
            vi(v.reciprocal(inv_c, inv_c))

    p.frames.append(ExitStack())
    ps_tr = p.psum("l2a_ptr", [128, 128], F32)
    with nc.Block() as blk:
        pe0 = p.c["pe"]
        v0 = p.c["v"]

        @blk.tensor
        def _(t):
            for ci in range(NCT):
                if ci >= 1:
                    t.wait_ge(p.S["v"], v0 + ci)
                ins = t.transpose(ps_tr[0:80, :].bitcast(F32R),
                                  win[:, ci * 80:(ci + 1) * 80],
                                  cn.ident_r[:])
                p.inc(ins, "pe")

        @blk.vector
        def _(v):
            vi = vch(v, p)
            for ci in range(NCT):
                v.wait_ge(p.S["pe"], pe0 + ci + 1)
                vi(v.tensor_copy(winT[:, ci * 128:(ci + 1) * 128],
                                 ps_tr[0:80, :]))
    with nc.Block() as blk:
        pe0 = p.c["pe"]
        v0 = p.c["v"]

        @blk.tensor
        def _(t):
            for ci in range(NCT):
                if ci >= 1:
                    t.wait_ge(p.S["v"], v0 + ci)
                ins = t.matmul(ps_tr[:, 0:48], winT[:, ci * 128:(ci + 1) * 128],
                               msel[:], start=True, stop=True)
                p.inc(ins, "pe")

        @blk.vector
        def _(v):
            vi = vch(v, p)
            for ci in range(NCT):
                v.wait_ge(p.S["pe"], pe0 + ci + 1)
                vi(v.tensor_copy(wsel3[:, ci * 48:(ci + 1) * 48],
                                 ps_tr[:, 0:48]))
    for g in range(3):
        with nc.Block() as blk:
            @blk.tensor
            def _(t):
                ins = t.matmul(ps_tr[0:16, 0:16], msel[:, 16 * g:16 * g + 16],
                               b_in[:], start=True, stop=True)
                p.inc(ins, "pe")

            pm = p.c["pe"]

            @blk.vector
            def _(v):
                vi = vch(v, p)
                v.wait_ge(p.S["pe"], pm)
                vi(v.tensor_copy(bsel[:, g:g + 1], ps_tr[0:16, 0:1]))

    p.frames.pop().close()
    ps_sel3 = p.psum("l2a_pss", [16, 3 * TC], F32)
    ps_out = [p.psum("l2a_pso", [128, TC], F32)]
    lns = make_ln_scratch(p)

    for ch in range(NCH):
        t0, t1 = ch * TC, (ch + 1) * TC

        def xt(ci, off, w=TC):
            return xch[:, ci * (TC + 3) + off: ci * (TC + 3) + off + w]

        lo = max(t0 - 1, 0)
        hi = min(t1 + 2, T)
        off = lo - (t0 - 1)
        with nc.Block() as blk:
            @blk.sync
            def _(e):
                ins = load_cm_chunk(e, p, xch, off, xin, lo, hi, stride=TC + 3)
                p.inc(ins, "dsp", 16)
                e.wait_ge(p.S["dsp"], p.c["dsp"])

            @blk.vector
            def _(v):
                vi = vch(v, p)
                if ch == 0:
                    for ci in range(NCT):
                        vi(v.memset(xt(ci, 0, 1).bitcast(F32), 0.0))
                if ch == NCH - 1:
                    for ci in range(NCT):
                        vi(v.memset(xt(ci, TC + 1, 2).bitcast(F32), 0.0))

        with nc.Block() as blk:
            @blk.vector
            def _(v):
                vi = vch(v, p)
                for ci in range(NCT):
                    o = xc[:, ci * TC:(ci + 1) * TC]
                    vi(v.tensor_scalar(o, xt(ci, 0), convw[:, ci * 4:ci * 4 + 1],
                                       None, ALU.mult))
                    for tap in range(1, 4):
                        vi(v.scalar_tensor_tensor(
                            o, xt(ci, tap), convw[:, ci * 4 + tap:ci * 4 + tap + 1],
                            o, ALU.mult, ALU.add))
                    vi(v.tensor_scalar(o, o, convb[:, ci:ci + 1], None, ALU.add))
                if ch == NCH - 1:
                    for ci in range(NCT):
                        vi(v.memset(xc[:, (ci + 1) * TC - 1:(ci + 1) * TC]
                                    .bitcast(F32), 0.0))

        with nc.Block() as blk:
            pe0 = p.c["pe"]

            @blk.tensor
            def _(t):
                for g in range(3):
                    for ci in range(NCT):
                        ins = t.matmul(ps_sel3[:, g * TC:(g + 1) * TC],
                                       wsel3[:, ci * 48 + g * 16:ci * 48 + g * 16 + 16],
                                       xc[:, ci * TC:(ci + 1) * TC],
                                       start=(ci == 0), stop=(ci == NCT - 1))
                    p.inc(ins, "pe")

            pm = p.c["pe"]

            @blk.vector
            def _(v):
                vi = vch(v, p)
                v.wait_ge(p.S["pe"], pm)
                for g in range(3):
                    vi(v.tensor_scalar(sel[:, g * TC:(g + 1) * TC],
                                       ps_sel3[:, g * TC:(g + 1) * TC],
                                       bsel[:, g:g + 1], None, ALU.add))

        delta = sel[:, 0:TC]
        Bin = sel[:, TC:2 * TC]
        Cc = sel[:, 2 * TC:3 * TC]
        if dbg is not None:
            with nc.Block() as blk:
                @blk.gpsimd
                def _(g):
                    ins = g.dma_start(dbg["sel"][:, ch * 3 * TC:(ch + 1) * 3 * TC], sel[:])
                    p.inc(ins, "dpool", 16)
                    ins = g.dma_start(dbg["xc"].rearrange("(ci p) t -> p ci t", p=128)[:, :, ch * TC:(ch + 1) * TC],
                                      xc[:].rearrange("p (ci t) -> p ci t", ci=NCT))
                    p.inc(ins, "dpool", 16)
                    g.wait_ge(p.S["dpool"], p.c["dpool"])
        sp = tmp[:, 0 * TC:1 * TC]
        t2 = tmp[:, 1 * TC:2 * TC]
        abar = tmp[:, 2 * TC:3 * TC]
        bbar = tmp[:, 3 * TC:4 * TC]
        hsc = tmp[:, 4 * TC:5 * TC]
        t3 = tmp[:, 5 * TC:6 * TC]

        with nc.Block() as blk:
            @blk.vector
            def _(v):
                vi = vch(v, p)
                vi(v.tensor_scalar(t2, delta, -1.0, None, ALU.mult))
                vi(v.tensor_max(t2, t2, delta))
            vm3 = p.c["v"]

            @blk.scalar
            def _(s):
                si = sch(s, p)
                s.wait_ge(p.S["v"], vm3)
                si(s.activation(t2, t2, AF.Exp, scale=-1.0))
                si(s.activation(t2, t2, AF.Ln, bias=1.0))
            sm3 = p.c["s"]

            @blk.vector
            def _(v):
                vi = vch(v, p)
                v.wait_ge(p.S["s"], sm3)
                vi(v.tensor_scalar(sp, delta, 0.0, None, ALU.max))
                vi(v.tensor_add(sp, sp, t2))
                vi(v.tensor_scalar(sp, sp, a_col, None, ALU.mult))
            vm4 = p.c["v"]

            @blk.scalar
            def _(s):
                si = sch(s, p)
                s.wait_ge(p.S["v"], vm4)
                si(s.activation(abar, sp, AF.Exp))
            sm4 = p.c["s"]

            @blk.vector
            def _(v):
                vi = vch(v, p)
                v.wait_ge(p.S["s"], sm4)
                vi(v.tensor_scalar(t2, sp, 1.0 / 6.0, 0.5, ALU.mult, ALU.add))
                vi(v.tensor_mul(t2, t2, sp))
                vi(v.tensor_scalar(t2, t2, 1.0, None, ALU.add))
                vi(v.tensor_scalar(t3, abar, -1.0, None, ALU.add))
                vi(v.tensor_scalar(t3, t3, inv_c, None, ALU.mult))
                vi(v.tensor_scalar(t2, t2, mask_c, None, ALU.mult))
                vi(v.scalar_tensor_tensor(t3, t3, onem_c, t2, ALU.mult, ALU.add))
                vi(v.tensor_mul(bbar, t3, Bin))
                vi(v.tensor_tensor_scan(hsc, abar, bbar,
                                        0.0 if ch == 0 else state_c,
                                        ALU.mult, ALU.add))
                vi(v.tensor_copy(state_c, hsc[:, TC - 1:TC]))
                vi(v.tensor_mul(hsc, hsc, Cc))
        if dbg is not None:
            with nc.Block() as blk:
                @blk.gpsimd
                def _(g):
                    ins = g.dma_start(dbg["h"][:, ch * TC:(ch + 1) * TC], hsc)
                    p.inc(ins, "dpool", 16)
                    ins = g.dma_start(dbg["abar"][:, ch * TC:(ch + 1) * TC], abar)
                    p.inc(ins, "dpool", 16)
                    ins = g.dma_start(dbg["bbar"][:, ch * TC:(ch + 1) * TC], bbar)
                    p.inc(ins, "dpool", 16)
                    g.wait_ge(p.S["dpool"], p.c["dpool"])

        with nc.Block() as blk:
            pe0 = p.c["pe"]
            v0 = p.c["v"]

            @blk.tensor
            def _(t):
                for ot in range(NCT):
                    if ot >= 1:
                        t.wait_ge(p.S["v"], v0 + ot)
                    ins = t.matmul(ps_out[0][:],
                                   wout[:, ot * 128:(ot + 1) * 128],
                                   hsc, start=True, stop=True)
                    p.inc(ins, "pe")

            @blk.vector
            def _(v):
                vi = vch(v, p)
                for ot in range(NCT):
                    v.wait_ge(p.S["pe"], pe0 + ot + 1)
                    vi(v.tensor_scalar(out_sb[:, ot * TC:(ot + 1) * TC],
                                       ps_out[0][:], b_out[:, ot:ot + 1],
                                       None, ALU.add))

        emit_resid_ln(p, cn, lns, ch=ch,
                      y_tiles=lambda ci: out_sb[:, ci * TC:(ci + 1) * TC],
                      xres=xin, out_dram=xout, xbuf=xbuf)



def ep_elu1(bias_tile):
    def vpre(v, vi, ot, ps, stage, ch, aux, scr):
        vi(v.tensor_scalar(scr, ps, bias_tile[:, ot:ot + 1], 0.0,
                           ALU.add, ALU.min))
        vi(v.tensor_scalar(stage, ps, bias_tile[:, ot:ot + 1], 0.0,
                           ALU.add, ALU.max))

    def fs(s, si, ot, ps, stage, ch, aux, scr):
        si(s.activation(scr, scr, AF.Exp))

    def vpost(v, vi, ot, ps, stage, ch, aux, scr):
        vi(v.tensor_add(stage, stage, scr))

    return Ep3(2, 1, 1, vpre, fs, vpost)


def emit_l1_attn(p, cn, W, xin, xout):
    nc = p.nc
    H, DH = 16, 64
    Qb = p.dram("l1a_Q", [C, T], F32R)
    Kb = p.dram("l1a_K", [C, T], F32R)
    Vb = p.dram("l1a_V", [C, T], F32R)
    xa = p.dram("l1a_xa", [C, T], F32R)
    y1b = p.dram("l1a_y1", [4096, T], F32R)

    vecs = p.sbuf("l1a_vecs", [128, 9 * 8 + 32], F32)

    def vc(i):
        return vecs[:, 8 * i:8 * (i + 1)]
    bq, bk, bv, bo, b2, g1, be1, g2, be2 = [vc(i) for i in range(9)]
    b1 = vecs[:, 72:104]
    with nc.Block() as blk:
        @blk.sync
        def _(e):
            for i, nmw in enumerate(["bq", "bk", "bv", "bo", "b2", "g1", "be1",
                                     "g2", "be2"]):
                ins = e.dma_start(vc(i), W[nmw])
                p.inc(ins, "dsp", 16)
            ins = e.dma_start(b1, W["b1"])
            p.inc(ins, "dsp", 16)
            e.wait_ge(p.S["dsp"], p.c["dsp"])

    lin_pass(p, cn, wT=W["wq"], Kd=C, Nd=C, src=xin, dst=Qb, ep=ep_elu1(bq))
    lin_pass(p, cn, wT=W["wk"], Kd=C, Nd=C, src=xin, dst=Kb, ep=ep_elu1(bk))

    def epv(eng, issue, ot, ps, stage, ch, aux):
        issue(eng.tensor_scalar(stage, ps, bv[:, ot:ot + 1], None, ALU.add))
    lin_pass(p, cn, wT=W["wv"], Kd=C, Nd=C, src=xin, dst=Vb, ep=Ep("v", 1, epv))

    with p.frame():
        kvacc = p.sbuf("l1a_kv", [128, 1024], F32)
        ksum = p.sbuf("l1a_ks", [128, 8], F32)
        kred = p.sbuf("l1a_kred", [128, 8], F32)
        ksr = p.sbuf("l1a_ksr", [128, 8], F32R)
        kvr = p.sbuf("l1a_kvr", [128, 1024], F32R)
        p.frames.append(ExitStack())
        kch = p.sbuf("l1a_kch", [128, NCT * TC], F32R)
        vch_ = p.sbuf("l1a_vch", [128, NCT * TC], F32R)
        ktb = p.sbuf("l1a_kt", [128, 4 * C], F32R)
        vtb = p.sbuf("l1a_vt", [128, 4 * C], F32R)
        ptr = p.psum("l1a_ptr", [128, 128], F32)
        pkv = p.psum("l1a_pkv", [128, 128], F32)

        with nc.Block() as blk:
            @blk.vector
            def _(v):
                vi = vch(v, p)
                vi(v.memset(kvacc[:], 0.0))
                vi(v.memset(ksum[:], 0.0))

        for ch in range(NCH):
            t0, t1 = ch * TC, (ch + 1) * TC
            with nc.Block() as blk:
                @blk.sync
                def _(e):
                    ins = load_cm_chunk(e, p, kch, 0, Kb, t0, t1)
                    p.inc(ins, "dsp", 16)
                    ins = load_cm_chunk(e, p, vch_, 0, Vb, t0, t1)
                    p.inc(ins, "dsp", 16)
                    e.wait_ge(p.S["dsp"], p.c["dsp"])

            with nc.Block() as blk:
                @blk.vector
                def _(v):
                    vi = vch(v, p)
                    for ci in range(NCT):
                        vi(v.reduce_sum(kred[:, ci:ci + 1],
                                        kch[:, ci * TC:(ci + 1) * TC], axis=AX.X))
                    vi(v.tensor_add(ksum[:], ksum[:], kred[:]))

            with nc.Block() as blk:
                pe0 = p.c["pe"]
                v0 = p.c["v"]

                @blk.tensor
                def _(t):
                    n = 0
                    for src_sb, dst_sb in ((kch, ktb), (vch_, vtb)):
                        for ci in range(NCT):
                            for st in range(4):
                                if n >= 1:
                                    t.wait_ge(p.S["v"], v0 + n)
                                ins = t.transpose(
                                    ptr[:].bitcast(F32R),
                                    src_sb[:, ci * TC + st * 128:
                                           ci * TC + (st + 1) * 128],
                                    cn.ident_r[:])
                                p.inc(ins, "pe")
                                n += 1

                @blk.vector
                def _(v):
                    vi = vch(v, p)
                    n = 0
                    for src_sb, dst_sb in ((kch, ktb), (vch_, vtb)):
                        for ci in range(NCT):
                            for st in range(4):
                                v.wait_ge(p.S["pe"], pe0 + n + 1)
                                vi(v.tensor_copy(
                                    dst_sb[:, st * C + ci * 128:
                                           st * C + (ci + 1) * 128],
                                    ptr[:]))
                                n += 1

            with nc.Block() as blk:
                pe0 = p.c["pe"]
                v0 = p.c["v"]

                @blk.tensor
                def _(t):
                    for hp in range(8):
                        if hp >= 1:
                            t.wait_ge(p.S["v"], v0 + hp)
                        for st in range(4):
                            ins = t.matmul(
                                pkv[:],
                                ktb[:, st * C + hp * 128: st * C + (hp + 1) * 128],
                                vtb[:, st * C + hp * 128: st * C + (hp + 1) * 128],
                                start=(st == 0), stop=(st == 3))
                        p.inc(ins, "pe")

                @blk.vector
                def _(v):
                    vi = vch(v, p)
                    for hp in range(8):
                        v.wait_ge(p.S["pe"], pe0 + hp + 1)
                        sl = kvacc[:, hp * 128:(hp + 1) * 128]
                        vi(v.tensor_add(sl, sl, pkv[:]))

        p.frames.pop().close()
        with nc.Block() as blk:
            @blk.vector
            def _(v):
                vi = vch(v, p)
                for hp in range(8):
                    vi(v.memset(kvacc[0:64, hp * 128 + 64:hp * 128 + 128], 0.0))
                    vi(v.memset(kvacc[64:128, hp * 128:hp * 128 + 64], 0.0))
                vi(v.tensor_copy(kvr[:], kvacc[:]))
                vi(v.tensor_copy(ksr[:], ksum[:]))

        wo_sb = p.sbuf("l1a_wo", [128, NCT * C], F32R)
        with nc.Block() as blk:
            @blk.sync
            def _(e):
                ins = e.dma_start(
                    wo_sb[:].rearrange("p (ci o) -> p ci o", ci=NCT),
                    W["wo"].rearrange("(ci p) o -> p ci o", p=128))
                p.inc(ins, "dsp", 16)
                e.wait_ge(p.S["dsp"], p.c["dsp"])

        qch = p.sbuf("l1a_qch", [128, NCT * TC], F32R)
        qs = p.sbuf("l1a_qs", [128, NCT * TC], F32R)
        zbuf = p.sbuf("l1a_z", [1, 16 * TC], F32R)
        zrepb = p.sbuf("l1a_zrep", [128, TC], F32)
        osb = p.sbuf("l1a_o", [128, NCT * TC], F32R)
        out_sb = p.sbuf("l1a_attn", [128, NCT * TC], F32R)
        xbuf = p.sbuf("l1a_xres", [128, NCT * TC], F32R)
        pz = p.psum("l1a_pz", [1, TC], F32)
        pzr = p.psum("l1a_pzr", [128, TC], F32)
        po = p.psum("l1a_po", [128, TC], F32)
        pat = p.psum("l1a_pat", [128, TC], F32)
        lns = make_ln_scratch(p)

        for ch in range(NCH):
            t0, t1 = ch * TC, (ch + 1) * TC
            with nc.Block() as blk:
                @blk.sync
                def _(e):
                    ins = load_cm_chunk(e, p, qch, 0, Qb, t0, t1)
                    p.inc(ins, "dsp", 16)
                    e.wait_ge(p.S["dsp"], p.c["dsp"])

            with nc.Block() as blk:
                pe0 = p.c["pe"]
                v0 = p.c["v"]

                @blk.tensor
                def _(t):
                    for h in range(16):
                        ci, half = h // 2, h % 2
                        if h >= 1:
                            t.wait_ge(p.S["v"], v0 + 2 * h)
                        ins = t.matmul(
                            pz[:],
                            ksr[half * 64:(half + 1) * 64, ci:ci + 1],
                            qch[half * 64:(half + 1) * 64,
                                ci * TC:(ci + 1) * TC],
                            start=True, stop=True)
                        p.inc(ins, "pe")

                @blk.vector
                def _(v):
                    vi = vch(v, p)
                    for h in range(16):
                        v.wait_ge(p.S["pe"], pe0 + h + 1)
                        sl = zbuf[0:1, h * TC:(h + 1) * TC]
                        vi(v.tensor_scalar(sl, pz[:], 1e-6, None, ALU.add))
                        with nc.allow_low_precision("f32r bits are fp32"):
                            vi(v.reciprocal(sl, sl))


            with nc.Block() as blk:
                pe0 = p.c["pe"]
                v0 = p.c["v"]

                @blk.tensor
                def _(t):
                    for ci in range(NCT):
                        if ci >= 1:
                            t.wait_ge(p.S["v"], v0 + 2 * ci)
                        h0, h1 = 2 * ci, 2 * ci + 1
                        ins = t.matmul(pzr[:], cn.sel_lo[:],
                                       zbuf[0:1, h0 * TC:(h0 + 1) * TC],
                                       start=True, stop=False)
                        ins = t.matmul(pzr[:], cn.sel_hi[:],
                                       zbuf[0:1, h1 * TC:(h1 + 1) * TC],
                                       start=False, stop=True)
                        p.inc(ins, "pe")

                @blk.vector
                def _(v):
                    vi = vch(v, p)
                    for ci in range(NCT):
                        v.wait_ge(p.S["pe"], pe0 + ci + 1)
                        vi(v.tensor_copy(zrepb[:], pzr[:]))
                        vi(v.tensor_mul(qs[:, ci * TC:(ci + 1) * TC],
                                        qch[:, ci * TC:(ci + 1) * TC], zrepb[:]))

            with nc.Block() as blk:
                pe0 = p.c["pe"]
                v0 = p.c["v"]

                @blk.tensor
                def _(t):
                    for ci in range(NCT):
                        if ci >= 1:
                            t.wait_ge(p.S["v"], v0 + ci)
                        ins = t.matmul(po[:],
                                       kvr[:, ci * 128:(ci + 1) * 128],
                                       qs[:, ci * TC:(ci + 1) * TC],
                                       start=True, stop=True)
                        p.inc(ins, "pe")

                @blk.vector
                def _(v):
                    vi = vch(v, p)
                    for ci in range(NCT):
                        v.wait_ge(p.S["pe"], pe0 + ci + 1)
                        vi(v.tensor_copy(osb[:, ci * TC:(ci + 1) * TC], po[:]))

            with nc.Block() as blk:
                pe0 = p.c["pe"]
                v0 = p.c["v"]

                @blk.tensor
                def _(t):
                    for ot in range(NCT):
                        if ot >= 1:
                            t.wait_ge(p.S["v"], v0 + ot)
                        for ci in range(NCT):
                            ins = t.matmul(pat[:],
                                           wo_sb[:, ci * C + ot * 128:
                                                 ci * C + ot * 128 + 128],
                                           osb[:, ci * TC:(ci + 1) * TC],
                                           start=(ci == 0), stop=(ci == NCT - 1))
                        p.inc(ins, "pe")

                @blk.vector
                def _(v):
                    vi = vch(v, p)
                    for ot in range(NCT):
                        v.wait_ge(p.S["pe"], pe0 + ot + 1)
                        vi(v.tensor_scalar(out_sb[:, ot * TC:(ot + 1) * TC],
                                           pat[:], bo[:, ot:ot + 1], None,
                                           ALU.add))

            emit_resid_ln(p, cn, lns, ch=ch,
                          y_tiles=lambda ci: out_sb[:, ci * TC:(ci + 1) * TC],
                          xres=xin, out_dram=xa, xbuf=xbuf,
                          g_ap=lambda ci: g1[:, ci:ci + 1],
                          b_ap=lambda ci: be1[:, ci:ci + 1])

    def ep4(eng, issue, ot, ps, stage, ch, aux):
        issue(eng.activation(stage, ps, AF.Gelu, bias=b1[:, ot:ot + 1]))
    lin_pass(p, cn, wT=W["w1"], Kd=C, Nd=4096, src=xa, dst=y1b, ep=Ep("s", 1, ep4))

    with p.frame():
        lns = make_ln_scratch(p)
        xbuf = p.sbuf("l1a_xres2", [128, NCT * TC], F32R)

        def ep5(eng, issue, ot, ps, stage, ch, aux):
            issue(eng.tensor_scalar(stage, ps, b2[:, ot:ot + 1], None, ALU.add))

        def fin(ch, out_sb, srcb):
            emit_resid_ln(p, cn, lns, ch=ch,
                          y_tiles=lambda ci: out_sb[:, ci * TC:(ci + 1) * TC],
                          xres=xa, out_dram=None, xbuf=xbuf,
                          g_ap=lambda ci: g2[:, ci:ci + 1],
                          b_ap=lambda ci: be2[:, ci:ci + 1])
            emit_resid_ln(p, cn, lns, ch=ch,
                          y_tiles=lambda ci: out_sb[:, ci * TC:(ci + 1) * TC],
                          xres=xin, out_dram=xout, xbuf=xbuf)

        lin_pass(p, cn, wT=W["w2"], Kd=4096, Nd=C, src=y1b, dst=None,
                 ep=Ep("v", 1, ep5), finalize=fin)



def _proj_stream(p, cn, *, wT, Kd, Nd, x_tiles, ep_fn, ep_eng, ep_nops, psms,
                 wcols):
    nc = p.nc
    kt, nt = Kd // 128, Nd // 128
    d0 = p.c["dsp"]
    pe0 = p.c["pe"]
    e0 = p.c[ep_eng]
    wready = [0] * nt
    with nc.Block() as blk:
        @blk.sync
        def _(e):
            for k in range(nt):
                if k >= 2:
                    e.wait_ge(p.S["pe"], pe0 + k - 1)
                ins = e.dma_start(
                    wcols[k % 2][:, 0:kt * 128].rearrange(
                        "p (ci o) -> p ci o", ci=kt),
                    wT.rearrange("(ci p) o -> p ci o", p=128)
                    [:, :, k * 128:(k + 1) * 128])
                p.inc(ins, "dsp", 16)
                wready[k] = p.c["dsp"]

        @blk.tensor
        def _(t):
            for ot in range(nt):
                t.wait_ge(p.S["dsp"], wready[ot])
                if ot >= 2:
                    t.wait_ge(p.S[ep_eng], e0 + ep_nops * (ot - 1))
                for ci in range(kt):
                    ins = t.matmul(psms[ot % 2][:],
                                   wcols[ot % 2][:, ci * 128:(ci + 1) * 128],
                                   x_tiles(ci), start=(ci == 0),
                                   stop=(ci == kt - 1))
                p.inc(ins, "pe")

        def _ep(eng):
            issue = (sch if ep_eng == "s" else vch)(eng, p)
            for ot in range(nt):
                eng.wait_ge(p.S["pe"], pe0 + ot + 1)
                ep_fn(eng, issue, ot, psms[ot % 2][:])

        if ep_eng == "s":
            @blk.scalar
            def _(s):
                _ep(s)
        else:
            @blk.vector
            def _(v):
                _ep(v)


def emit_l0_attn(p, cn, W, x0, xout):
    nc = p.nc
    hb = p.dram("l0a_h", [C, T], F32R)
    wkvb = p.dram("l0a_wkv", [C, T], F32R)
    zb = p.dram("l0a_z", [C, T], F32R)
    h2b = p.dram("l0a_h2", [C, T], F32R)
    kkb = p.dram("l0a_kk", [4096, T], F32R)
    srb = p.dram("l0a_sr", [C, T], F32R)

    vecs = p.sbuf("l0a_vecs", [128, 11 * 8], F32)
    names = ["ln1w", "ln1b", "ln2w", "ln2b", "decay", "tfirst", "tmk", "tmv",
             "tmr", "ftmk", "ftmr"]

    def vc(nmw):
        i = names.index(nmw)
        return vecs[:, 8 * i:8 * (i + 1)]

    lamc = p.sbuf("l0a_lam", [128, 8], F32)
    euc = p.sbuf("l0a_eu", [128, 8], F32)
    lam_rep = p.sbuf("l0a_lamrep", [128, NCT * TC], F32)
    stA = p.sbuf("l0a_stA", [128, 16], F32)

    with nc.Block() as blk:
        @blk.sync
        def _(e):
            for nmw in names:
                ins = e.dma_start(vc(nmw), W[nmw])
                p.inc(ins, "dsp", 16)
            e.wait_ge(p.S["dsp"], p.c["dsp"])

    with nc.Block() as blk:
        @blk.scalar
        def _(s):
            si = sch(s, p)
            si(s.activation(lamc[:], vc("decay"), AF.Exp))
            si(s.activation(euc[:], vc("tfirst"), AF.Exp))
        sm = p.c["s"]

        @blk.vector
        def _(v):
            vi = vch(v, p)
            v.wait_ge(p.S["s"], sm)
            vi(v.tensor_scalar(lamc[:], lamc[:], -1.0, None, ALU.mult))
        vm = p.c["v"]

        @blk.scalar
        def _(s):
            si = sch(s, p)
            s.wait_ge(p.S["v"], vm)
            si(s.activation(lamc[:], lamc[:], AF.Exp))
        sm2 = p.c["s"]

        @blk.vector
        def _(v):
            vi = vch(v, p)
            v.wait_ge(p.S["s"], sm2)
            vi(v.memset(stA[:], 0.0))
            vi(v.memset(lam_rep[:], 1.0))
            for ci in range(NCT):
                vi(v.tensor_scalar(lam_rep[:, ci * TC:(ci + 1) * TC],
                                   lam_rep[:, ci * TC:(ci + 1) * TC],
                                   lamc[:, ci:ci + 1], None, ALU.mult))

    with p.frame():
        xch = p.sbuf("l0a_x", [128, NCT * TC], F32R)
        hbuf = p.sbuf("l0a_hb", [128, NCT * (TC + 1)], F32R)
        ekbuf = p.sbuf("l0a_ek", [128, NCT * (TC + 1)], F32R)
        ekubuf = p.sbuf("l0a_eku", [128, NCT * (TC + 1)], F32R)
        xmix = p.sbuf("l0a_xm", [128, NCT * TC], F32R)
        vsb = p.sbuf("l0a_v", [128, NCT * TC], F32R)
        pa = p.sbuf("l0a_pa", [128, NCT * TC], F32R)
        pb = p.sbuf("l0a_pb", [128, NCT * TC], F32R)
        wcols = [p.sbuf("l0a_wc", [128, NCT * 128], F32R) for _ in range(2)]
        psms = [p.psum("l0a_ps", [128, TC], F32) for _ in range(2)]
        lns = make_ln_scratch(p)

        def hv(ci, off=1, w=TC):
            return hbuf[:, ci * (TC + 1) + off: ci * (TC + 1) + off + w]

        def ekv(ci, off=1, w=TC):
            return ekbuf[:, ci * (TC + 1) + off: ci * (TC + 1) + off + w]

        def ekuv(ci, off=1, w=TC):
            return ekubuf[:, ci * (TC + 1) + off: ci * (TC + 1) + off + w]

        for ch in range(NCH):
            t0, t1 = ch * TC, (ch + 1) * TC
            with nc.Block() as blk:
                @blk.sync
                def _(e):
                    ins = load_cm_chunk(e, p, xch, 0, x0, t0, t1)
                    p.inc(ins, "dsp", 16)
                    e.wait_ge(p.S["dsp"], p.c["dsp"])

                @blk.vector
                def _(v):
                    vi = vch(v, p)
                    for ci in range(NCT):
                        if ch == 0:
                            vi(v.memset(hv(ci, 0, 1).bitcast(F32), 0.0))
                            vi(v.memset(ekv(ci, 0, 1).bitcast(F32), 0.0))
                            vi(v.memset(ekuv(ci, 0, 1).bitcast(F32), 0.0))
                        else:
                            vi(v.tensor_copy(hv(ci, 0, 1), hv(ci, TC, 1)))
                            vi(v.tensor_copy(ekv(ci, 0, 1), ekv(ci, TC, 1)))
                            vi(v.tensor_copy(ekuv(ci, 0, 1), ekuv(ci, TC, 1)))

            ln_chunk(p, cn, z_tiles=lambda ci: xch[:, ci * TC:(ci + 1) * TC],
                     out_tiles=lambda ci: hv(ci),
                     g_ap=lambda ci: vc("ln1w")[:, ci:ci + 1],
                     b_ap=lambda ci: vc("ln1b")[:, ci:ci + 1], **lns)

            with nc.Block() as blk:
                @blk.gpsimd
                def _(g):
                    for ci in range(NCT):
                        ins = g.dma_start(dram_cm(hb)[:, ci, t0:t1], hv(ci))
                        p.inc(ins, "dpool", 16)

            with nc.Block() as blk:
                @blk.vector
                def _(v):
                    vi = vch(v, p)
                    for ci in range(NCT):
                        vi(v.tensor_sub(pa[:, ci * TC:(ci + 1) * TC],
                                        hv(ci), hv(ci, 0)))
                        vi(v.scalar_tensor_tensor(
                            xmix[:, ci * TC:(ci + 1) * TC],
                            pa[:, ci * TC:(ci + 1) * TC],
                            vc("tmk")[:, ci:ci + 1], hv(ci, 0),
                            ALU.mult, ALU.add))

            def ep_ek(eng, issue, ot, ps):
                issue(eng.activation(ekv(ot), ps, AF.Exp))
            _proj_stream(p, cn, wT=W["kwT"], Kd=C, Nd=C,
                         x_tiles=lambda ci: xmix[:, ci * TC:(ci + 1) * TC],
                         ep_fn=ep_ek, ep_eng="s", ep_nops=1, psms=psms,
                         wcols=wcols)

            with nc.Block() as blk:
                @blk.vector
                def _(v):
                    vi = vch(v, p)
                    for ci in range(NCT):
                        vi(v.scalar_tensor_tensor(
                            xmix[:, ci * TC:(ci + 1) * TC],
                            pa[:, ci * TC:(ci + 1) * TC],
                            vc("tmv")[:, ci:ci + 1], hv(ci, 0),
                            ALU.mult, ALU.add))

            def ep_v(eng, issue, ot, ps):
                issue(eng.tensor_copy(vsb[:, ot * TC:(ot + 1) * TC], ps))
            _proj_stream(p, cn, wT=W["vwT"], Kd=C, Nd=C,
                         x_tiles=lambda ci: xmix[:, ci * TC:(ci + 1) * TC],
                         ep_fn=ep_v, ep_eng="v", ep_nops=1, psms=psms,
                         wcols=wcols)

            with nc.Block() as blk:
                @blk.vector
                def _(v):
                    vi = vch(v, p)
                    for ci in range(NCT):
                        vi(v.tensor_mul(ekuv(ci), ekv(ci),
                                        vsb[:, ci * TC:(ci + 1) * TC]))
                    for ci in range(NCT):
                        vi(v.tensor_tensor_scan(
                            pa[:, ci * TC:(ci + 1) * TC],
                            lam_rep[:, ci * TC:(ci + 1) * TC],
                            ekuv(ci, 0), stA[:, ci:ci + 1],
                            ALU.mult, ALU.add))
                        vi(v.tensor_tensor_scan(
                            pb[:, ci * TC:(ci + 1) * TC],
                            lam_rep[:, ci * TC:(ci + 1) * TC],
                            ekv(ci, 0), stA[:, 8 + ci:9 + ci],
                            ALU.mult, ALU.add))
                        vi(v.scalar_tensor_tensor(
                            stA[:, ci:ci + 1],
                            pa[:, (ci + 1) * TC - 1:(ci + 1) * TC],
                            lamc[:, ci:ci + 1], ekuv(ci, TC, 1),
                            ALU.mult, ALU.add))
                        vi(v.scalar_tensor_tensor(
                            stA[:, 8 + ci:9 + ci],
                            pb[:, (ci + 1) * TC - 1:(ci + 1) * TC],
                            lamc[:, ci:ci + 1], ekv(ci, TC, 1),
                            ALU.mult, ALU.add))
                    for ci in range(NCT):
                        vi(v.scalar_tensor_tensor(
                            pa[:, ci * TC:(ci + 1) * TC], ekuv(ci),
                            euc[:, ci:ci + 1], pa[:, ci * TC:(ci + 1) * TC],
                            ALU.mult, ALU.add))
                        vi(v.scalar_tensor_tensor(
                            pb[:, ci * TC:(ci + 1) * TC], ekv(ci),
                            euc[:, ci:ci + 1], pb[:, ci * TC:(ci + 1) * TC],
                            ALU.mult, ALU.add))
                        with nc.allow_low_precision("f32r bits are fp32"):
                            vi(v.reciprocal(pb[:, ci * TC:(ci + 1) * TC],
                                            pb[:, ci * TC:(ci + 1) * TC]))
                        vi(v.tensor_mul(pa[:, ci * TC:(ci + 1) * TC],
                                        pa[:, ci * TC:(ci + 1) * TC],
                                        pb[:, ci * TC:(ci + 1) * TC]))

            with nc.Block() as blk:
                @blk.gpsimd
                def _(g):
                    for ci in range(NCT):
                        ins = g.dma_start(dram_cm(wkvb)[:, ci, t0:t1],
                                          pa[:, ci * TC:(ci + 1) * TC])
                        p.inc(ins, "dpool", 16)
                    g.wait_ge(p.S["dpool"], p.c["dpool"])

    with p.frame():
        hbuf = p.sbuf("l0a2_hb", [128, NCT * (TC + 1)], F32R)
        xr = p.sbuf("l0a2_xr", [128, NCT * TC], F32R)
        srw = p.sbuf("l0a2_sr", [128, NCT * TC], F32R)
        out_sb = p.sbuf("l0a2_out", [128, NCT * TC], F32R)
        xch = p.sbuf("l0a2_x", [128, NCT * TC], F32R)
        w_rw = p.sbuf("l0a2_wrw", [128, NCT * C], F32R)
        w_ow = p.sbuf("l0a2_wow", [128, NCT * C], F32R)
        psms = [p.psum("l0a2_ps", [128, TC], F32) for _ in range(2)]

        def hv(ci, off=1, w=TC):
            return hbuf[:, ci * (TC + 1) + off: ci * (TC + 1) + off + w]

        with nc.Block() as blk:
            @blk.sync
            def _(e):
                for wsb, nmw in ((w_rw, "rwT"), (w_ow, "owT")):
                    ins = e.dma_start(
                        wsb[:].rearrange("p (ci o) -> p ci o", ci=NCT),
                        W[nmw].rearrange("(ci p) o -> p ci o", p=128))
                    p.inc(ins, "dsp", 16)
                e.wait_ge(p.S["dsp"], p.c["dsp"])

        for ch in range(NCH):
            t0, t1 = ch * TC, (ch + 1) * TC
            lo = max(t0 - 1, 0)
            off = 1 - (t0 - lo)
            with nc.Block() as blk:
                @blk.sync
                def _(e):
                    ins = load_cm_chunk(e, p, hbuf, off, hb, lo, t1,
                                        stride=TC + 1)
                    p.inc(ins, "dsp", 16)
                    ins = load_cm_chunk(e, p, srw, 0, wkvb, t0, t1)
                    p.inc(ins, "dsp", 16)
                    ins = load_cm_chunk(e, p, xch, 0, x0, t0, t1)
                    p.inc(ins, "dsp", 16)
                    e.wait_ge(p.S["dsp"], p.c["dsp"])

                if ch == 0:
                    @blk.vector
                    def _(v):
                        vi = vch(v, p)
                        for ci in range(NCT):
                            vi(v.memset(hv(ci, 0, 1).bitcast(F32), 0.0))

            with nc.Block() as blk:
                @blk.vector
                def _(v):
                    vi = vch(v, p)
                    for ci in range(NCT):
                        vi(v.tensor_sub(xr[:, ci * TC:(ci + 1) * TC],
                                        hv(ci), hv(ci, 0)))
                        vi(v.scalar_tensor_tensor(
                            xr[:, ci * TC:(ci + 1) * TC],
                            xr[:, ci * TC:(ci + 1) * TC],
                            vc("tmr")[:, ci:ci + 1], hv(ci, 0),
                            ALU.mult, ALU.add))

            with nc.Block() as blk:
                pe0 = p.c["pe"]
                s0 = p.c["s"]
                v0 = p.c["v"]

                @blk.tensor
                def _(t):
                    for ot in range(NCT):
                        if ot >= 2:
                            t.wait_ge(p.S["v"], v0 + ot - 1)
                        for ci in range(NCT):
                            ins = t.matmul(psms[ot % 2][:],
                                           w_rw[:, ci * C + ot * 128:
                                                ci * C + ot * 128 + 128],
                                           xr[:, ci * TC:(ci + 1) * TC],
                                           start=(ci == 0),
                                           stop=(ci == NCT - 1))
                        p.inc(ins, "pe")

                @blk.scalar
                def _(s):
                    si = sch(s, p)
                    for ot in range(NCT):
                        s.wait_ge(p.S["pe"], pe0 + ot + 1)
                        si(s.activation(out_sb[:, ot * TC:(ot + 1) * TC],
                                        psms[ot % 2][:], AF.Sigmoid))

                @blk.vector
                def _(v):
                    vi = vch(v, p)
                    for ot in range(NCT):
                        v.wait_ge(p.S["s"], s0 + ot + 1)
                        vi(v.tensor_mul(srw[:, ot * TC:(ot + 1) * TC],
                                        srw[:, ot * TC:(ot + 1) * TC],
                                        out_sb[:, ot * TC:(ot + 1) * TC]))

            with nc.Block() as blk:
                pe0 = p.c["pe"]
                v0 = p.c["v"]

                @blk.tensor
                def _(t):
                    for ot in range(NCT):
                        if ot >= 2:
                            t.wait_ge(p.S["v"], v0 + ot - 1)
                        for ci in range(NCT):
                            ins = t.matmul(psms[ot % 2][:],
                                           w_ow[:, ci * C + ot * 128:
                                                ci * C + ot * 128 + 128],
                                           srw[:, ci * TC:(ci + 1) * TC],
                                           start=(ci == 0),
                                           stop=(ci == NCT - 1))
                        p.inc(ins, "pe")

                @blk.vector
                def _(v):
                    vi = vch(v, p)
                    for ot in range(NCT):
                        v.wait_ge(p.S["pe"], pe0 + ot + 1)
                        vi(v.tensor_add(out_sb[:, ot * TC:(ot + 1) * TC],
                                        xch[:, ot * TC:(ot + 1) * TC],
                                        psms[ot % 2][:]))

            with nc.Block() as blk:
                @blk.gpsimd
                def _(g):
                    for ci in range(NCT):
                        ins = g.dma_start(dram_cm(zb)[:, ci, t0:t1],
                                          out_sb[:, ci * TC:(ci + 1) * TC])
                        p.inc(ins, "dpool", 16)
                    g.wait_ge(p.S["dpool"], p.c["dpool"])

    with p.frame():
        zch = p.sbuf("l0b_z", [128, NCT * TC], F32R)
        lns = make_ln_scratch(p)
        for ch in range(NCH):
            t0, t1 = ch * TC, (ch + 1) * TC
            with nc.Block() as blk:
                @blk.sync
                def _(e):
                    ins = load_cm_chunk(e, p, zch, 0, zb, t0, t1)
                    p.inc(ins, "dsp", 16)
                    e.wait_ge(p.S["dsp"], p.c["dsp"])
            ln_chunk(p, cn, z_tiles=lambda ci: zch[:, ci * TC:(ci + 1) * TC],
                     out_tiles=lambda ci: zch[:, ci * TC:(ci + 1) * TC],
                     g_ap=lambda ci: vc("ln2w")[:, ci:ci + 1],
                     b_ap=lambda ci: vc("ln2b")[:, ci:ci + 1], **lns)
            with nc.Block() as blk:
                @blk.gpsimd
                def _(g):
                    for ci in range(NCT):
                        ins = g.dma_start(dram_cm(h2b)[:, ci, t0:t1],
                                          zch[:, ci * TC:(ci + 1) * TC])
                        p.inc(ins, "dpool", 16)
                    g.wait_ge(p.S["dpool"], p.c["dpool"])

    def mixed_pass(wT, Nd, mu_name, dstd, ep_fn, ep_eng, ep_nops, w_res):
        with p.frame():
            hbuf = p.sbuf("l0m_hb", [128, NCT * (TC + 1)], F32R)
            xk = p.sbuf("l0m_xk", [128, NCT * TC], F32R)
            stg = p.sbuf("l0m_stg", [128, 2 * TC], F32R)
            psms = [p.psum("l0m_ps", [128, TC], F32) for _ in range(2)]
            w_sb = None
            wcols = None
            if w_res:
                w_sb = p.sbuf("l0m_w", [128, NCT * Nd], F32R)
                with nc.Block() as blk:
                    @blk.sync
                    def _(e):
                        ins = e.dma_start(
                            w_sb[:].rearrange("p (ci o) -> p ci o", ci=NCT),
                            wT.rearrange("(ci p) o -> p ci o", p=128))
                        p.inc(ins, "dsp", 16)
                        e.wait_ge(p.S["dsp"], p.c["dsp"])
            else:
                wcols = [p.sbuf("l0m_wc", [128, NCT * 128], F32R)
                         for _ in range(2)]

            def hv(ci, off=1, w=TC):
                return hbuf[:, ci * (TC + 1) + off: ci * (TC + 1) + off + w]

            for ch in range(NCH):
                t0, t1 = ch * TC, (ch + 1) * TC
                lo = max(t0 - 1, 0)
                off = 1 - (t0 - lo)
                with nc.Block() as blk:
                    @blk.sync
                    def _(e):
                        ins = load_cm_chunk(e, p, hbuf, off, h2b, lo, t1,
                                            stride=TC + 1)
                        p.inc(ins, "dsp", 16)
                        e.wait_ge(p.S["dsp"], p.c["dsp"])

                    if ch == 0:
                        @blk.vector
                        def _(v):
                            vi = vch(v, p)
                            for ci in range(NCT):
                                vi(v.memset(hv(ci, 0, 1).bitcast(F32), 0.0))

                with nc.Block() as blk:
                    @blk.vector
                    def _(v):
                        vi = vch(v, p)
                        for ci in range(NCT):
                            vi(v.tensor_sub(xk[:, ci * TC:(ci + 1) * TC],
                                            hv(ci), hv(ci, 0)))
                            vi(v.scalar_tensor_tensor(
                                xk[:, ci * TC:(ci + 1) * TC],
                                xk[:, ci * TC:(ci + 1) * TC],
                                vc(mu_name)[:, ci:ci + 1], hv(ci, 0),
                                ALU.mult, ALU.add))

                if w_res:
                    nt = Nd // 128
                    with nc.Block() as blk:
                        pe0 = p.c["pe"]
                        e0 = p.c[ep_eng]
                        g0 = p.c["dpool"]

                        @blk.tensor
                        def _(t):
                            for ot in range(nt):
                                if ot >= 2:
                                    t.wait_ge(p.S[ep_eng],
                                              e0 + ep_nops * (ot - 1))
                                for ci in range(NCT):
                                    ins = t.matmul(
                                        psms[ot % 2][:],
                                        w_sb[:, ci * Nd + ot * 128:
                                             ci * Nd + ot * 128 + 128],
                                        xk[:, ci * TC:(ci + 1) * TC],
                                        start=(ci == 0), stop=(ci == NCT - 1))
                                p.inc(ins, "pe")

                        def _ep(eng):
                            issue = (sch if ep_eng == "s" else vch)(eng, p)
                            for ot in range(nt):
                                eng.wait_ge(p.S["pe"], pe0 + ot + 1)
                                if ot >= 2:
                                    eng.wait_ge(p.S["dpool"],
                                                g0 + 16 * (ot - 1))
                                ep_fn(eng, issue, ot, psms[ot % 2][:],
                                      stg[:, (ot % 2) * TC:(ot % 2 + 1) * TC])

                        if ep_eng == "s":
                            @blk.scalar
                            def _(s):
                                _ep(s)
                        else:
                            @blk.vector
                            def _(v):
                                _ep(v)

                        @blk.gpsimd
                        def _(g):
                            for ot in range(nt):
                                g.wait_ge(p.S[ep_eng], e0 + ep_nops * (ot + 1))
                                ins = g.dma_start(
                                    dram_cm(dstd)[:, ot, t0:t1],
                                    stg[:, (ot % 2) * TC:(ot % 2 + 1) * TC])
                                p.inc(ins, "dpool", 16)
                else:
                    nt = Nd // 128
                    with nc.Block() as blk:
                        pass
                with nc.Block() as blk:
                    @blk.gpsimd
                    def _(g):
                        g.wait_ge(p.S["dpool"], p.c["dpool"])

    def ep_kk(eng, issue, ot, ps, stage):
        issue(eng.activation(stage, ps, AF.Relu))
        issue(eng.activation(stage, stage, AF.Square))
    mixed_pass(W["fkwT"], 4096, "ftmk", kkb, ep_kk, "s", 2, True)

    def ep_sr(eng, issue, ot, ps, stage):
        issue(eng.activation(stage, ps, AF.Sigmoid))
    mixed_pass(W["frwT"], C, "ftmr", srb, ep_sr, "s", 1, True)

    with p.frame():
        lns = make_ln_scratch(p)
        xbuf = p.sbuf("l0c_xres", [128, NCT * TC], F32R)
        aux2 = p.sbuf("l0c_aux", [128, 2 * NCT * TC], F32R)

        def epc(eng, issue, ot, ps, stage, ch, aux):
            issue(eng.tensor_copy(stage, ps))

        def fin(ch, out_sb, srcb):
            t0, t1 = ch * TC, (ch + 1) * TC
            with nc.Block() as blk:
                @blk.sync
                def _(e):
                    ins = load_cm_chunk(e, p, aux2, 0, srb, t0, t1,
                                        stride=2 * TC)
                    p.inc(ins, "dsp", 16)
                    ins = load_cm_chunk(e, p, aux2, TC, zb, t0, t1,
                                        stride=2 * TC)
                    p.inc(ins, "dsp", 16)
                    e.wait_ge(p.S["dsp"], p.c["dsp"])
            with nc.Block() as blk:
                @blk.vector
                def _(v):
                    vi = vch(v, p)
                    for ci in range(NCT):
                        sl = out_sb[:, ci * TC:(ci + 1) * TC]
                        vi(v.tensor_mul(sl, sl,
                                        aux2[:, 2 * ci * TC:
                                             (2 * ci + 1) * TC]))
                        vi(v.tensor_add(sl, sl,
                                        aux2[:, (2 * ci + 1) * TC:
                                             (2 * ci + 2) * TC]))
            emit_resid_ln(p, cn, lns, ch=ch,
                          y_tiles=lambda ci: out_sb[:, ci * TC:(ci + 1) * TC],
                          xres=x0, out_dram=xout, xbuf=xbuf)

        lin_pass(p, cn, wT=W["fvwT"], Kd=4096, Nd=C, src=kkb, dst=None,
                 ep=Ep("v", 1, epc), finalize=fin)



def emit_l3_attn(p, cn, xin, xout):
    nc = p.nc
    ptb = p.dram("l3a_pt", [C, T], BF16)
    xtokb = p.dram("l3a_xtok", [T, C], BF16)
    IB = 512

    logpT = p.sbuf("l3a_logpT", [128, NCT * T], BF16)
    ne_row = p.sbuf("l3a_ne", [1, T], F32)
    ne_bf = p.sbuf("l3a_nebf", [1, T], BF16)
    ones_row_bf = p.sbuf("l3a_ones_bf", [1, 128], BF16)

    with nc.Block() as blk:
        @blk.sync
        def _(e):
            ins = e.dma_start(ones_row_bf[:],
                              cn.c["ones_bf_col"].rearrange("a b -> b a"))
            p.inc(ins, "dsp", 16)
            e.wait_ge(p.S["dsp"], p.c["dsp"])

    with p.frame():
        xch = p.sbuf("l3a_x", [128, NCT * TC], F32R)
        Ech = p.sbuf("l3a_E", [128, NCT * TC], F32R)
        t1b = p.sbuf("l3a_t1", [128, TC], F32)
        t2b = p.sbuf("l3a_t2", [128, TC], F32)
        ptst = p.sbuf("l3a_pst", [128, NCT * TC], BF16)
        xbf = p.sbuf("l3a_xbf", [128, NCT * TC], BF16)
        xtt = p.sbuf("l3a_xtt", [128, NCT * TC], BF16)
        srow = p.sbuf("l3a_srow", [1, 4 * TC], F32R)
        ps_sum = p.psum("l3a_psum", [1, 2 * TC], F32)
        ps_b = p.psum("l3a_pb", [128, 2 * TC], F32)
        ps_tr = p.psum("l3a_ptr", [128, 128], BF16)

        s_row = srow[0:1, 0:TC]
        lse_row = srow[0:1, TC:2 * TC]
        rs_row = srow[0:1, 2 * TC:3 * TC]

        for ch in range(NCH):
            t0, t1 = ch * TC, (ch + 1) * TC
            with nc.Block() as blk:
                @blk.sync
                def _(e):
                    ins = load_cm_chunk(e, p, xch, 0, xin, t0, t1)
                    p.inc(ins, "dsp", 16)
                    e.wait_ge(p.S["dsp"], p.c["dsp"])

            with nc.Block() as blk:
                @blk.scalar
                def _(s):
                    si = sch(s, p)
                    for ci in range(NCT):
                        si(s.activation(Ech[:, ci * TC:(ci + 1) * TC],
                                        xch[:, ci * TC:(ci + 1) * TC], AF.Exp))

                @blk.vector
                def _(v):
                    vi = vch(v, p)
                    vi(v.tensor_copy(xbf[:], xch[:]))

            with nc.Block() as blk:
                pe0 = p.c["pe"]

                @blk.tensor
                def _(t):
                    for ci in range(NCT):
                        ins = t.matmul(ps_sum[0:1, 0:TC], cn.ones_col[:],
                                       Ech[:, ci * TC:(ci + 1) * TC],
                                       start=(ci == 0), stop=(ci == NCT - 1))
                    p.inc(ins, "pe")

                pm = p.c["pe"]

                @blk.vector
                def _(v):
                    vi = vch(v, p)
                    v.wait_ge(p.S["pe"], pm)
                    vi(v.tensor_copy(s_row, ps_sum[0:1, 0:TC]))
                    with nc.allow_low_precision("f32r bits are fp32"):
                        vi(v.reciprocal(rs_row, s_row))
                vm = p.c["v"]

                @blk.scalar
                def _(s):
                    si = sch(s, p)
                    s.wait_ge(p.S["v"], vm)
                    si(s.activation(lse_row, s_row, AF.Ln))

            with nc.Block() as blk:
                pe0 = p.c["pe"]

                @blk.tensor
                def _(t):
                    ins = t.matmul(ps_b[:, 0:TC], cn.ones_row[:], lse_row,
                                   start=True, stop=True)
                    p.inc(ins, "pe")
                    ins = t.matmul(ps_b[:, TC:2 * TC], cn.ones_row[:], rs_row,
                                   start=True, stop=True)
                    p.inc(ins, "pe")

                pm = p.c["pe"]
                v0 = p.c["v"]

                @blk.vector
                def _(v):
                    vi = vch(v, p)
                    v.wait_ge(p.S["pe"], pm)
                    for ci in range(NCT):
                        vi(v.tensor_sub(t1b[:], xch[:, ci * TC:(ci + 1) * TC],
                                        ps_b[:, 0:TC]))
                        vi(v.tensor_mul(t2b[:], Ech[:, ci * TC:(ci + 1) * TC],
                                        ps_b[:, TC:2 * TC]))
                        vi(v.tensor_copy(
                            logpT[:, ci * T + t0:ci * T + t1], t1b[:]))
                        vi(v.tensor_copy(ptst[:, ci * TC:(ci + 1) * TC],
                                         t2b[:]))
                        vi(v.tensor_mul(t1b[:], t1b[:], t2b[:]))
                        vi(v.tensor_copy(Ech[:, ci * TC:(ci + 1) * TC],
                                         t1b[:]))

            with nc.Block() as blk:
                @blk.tensor
                def _(t):
                    for ci in range(NCT):
                        ins = t.matmul(ps_sum[0:1, TC:2 * TC], cn.ones_col[:],
                                       Ech[:, ci * TC:(ci + 1) * TC],
                                       start=(ci == 0), stop=(ci == NCT - 1))
                    p.inc(ins, "pe")

                pm = p.c["pe"]

                @blk.vector
                def _(v):
                    vi = vch(v, p)
                    v.wait_ge(p.S["pe"], pm)
                    vi(v.tensor_copy(ne_row[0:1, t0:t1], ps_sum[0:1, TC:2 * TC]))

            with nc.Block() as blk:
                pe0 = p.c["pe"]
                v0 = p.c["v"]

                @blk.tensor
                def _(t):
                    n = 0
                    for ci in range(NCT):
                        for st in range(4):
                            if n >= 1:
                                t.wait_ge(p.S["v"], v0 + n)
                            ins = t.transpose(
                                ps_tr[:],
                                xbf[:, ci * TC + st * 128:
                                    ci * TC + (st + 1) * 128],
                                cn.ident_bf[:])
                            p.inc(ins, "pe")
                            n += 1

                @blk.vector
                def _(v):
                    vi = vch(v, p)
                    n = 0
                    for ci in range(NCT):
                        for st in range(4):
                            v.wait_ge(p.S["pe"], pe0 + n + 1)
                            vi(v.tensor_copy(
                                xtt[:, st * C + ci * 128:
                                    st * C + (ci + 1) * 128], ps_tr[:]))
                            n += 1

            with nc.Block() as blk:
                @blk.gpsimd
                def _(g):
                    ins = g.dma_start(
                        ptb.rearrange("(ci p) t -> p ci t", p=128)[:, :, t0:t1],
                        ptst[:].rearrange("p (ci t) -> p ci t", ci=NCT))
                    p.inc(ins, "dpool", 16)
                    ins = g.dma_start(
                        xtokb.rearrange("(st pp) c -> pp st c", pp=128)
                        [:, 4 * ch:4 * (ch + 1), :],
                        xtt[:].rearrange("p (st c) -> p st c", st=4))
                    p.inc(ins, "dpool", 16)
                    g.wait_ge(p.S["dpool"], p.c["dpool"])

        with nc.Block() as blk:
            @blk.vector
            def _(v):
                vi = vch(v, p)
                vi(v.tensor_copy(ne_bf[:], ne_row[:]))

    with p.frame():
        pt_sb = p.sbuf("l3a_pti", [128, NCT * IB], BF16)
        expT = p.sbuf("l3a_expT", [128, 32 * IB], BF16)
        negent_sb = p.sbuf("l3a_neg", [128, IB], F32)
        crs = p.sbuf("l3a_crs", [128, IB], F32)
        zpad = p.sbuf("l3a_zpad", [128, IB], F32)
        zr = p.sbuf("l3a_zr", [128, 4], F32)
        xtj = [p.sbuf("l3a_xtj", [128, TC], BF16) for _ in range(2)]
        otok = p.sbuf("l3a_otok", [128, TC], F32)
        y_sb = p.sbuf("l3a_y", [128, NCT * IB], F32R)
        xbuf = p.sbuf("l3a_xres", [128, NCT * IB], F32R)

        for ib in range(T // IB):
            i0, i1 = ib * IB, (ib + 1) * IB
            p.frames.append(ExitStack())
            ps_cr = p.psum("l3a_pcr", [128, IB], F32)
            ps_z = p.psum("l3a_pz", [1, IB], F32)
            ps_nb = p.psum("l3a_pnb", [128, IB], F32)

            with nc.Block() as blk:
                @blk.sync
                def _(e):
                    ins = e.dma_start(
                        pt_sb[:].rearrange("p (ci i) -> p ci i", ci=NCT),
                        ptb.rearrange("(ci p) t -> p ci t", p=128)[:, :, i0:i1])
                    p.inc(ins, "dsp", 16)
                    e.wait_ge(p.S["dsp"], p.c["dsp"])

            with nc.Block() as blk:
                pe0 = p.c["pe"]

                @blk.tensor
                def _(t):
                    ins = t.matmul(ps_nb[:], ones_row_bf[:],
                                   ne_bf[0:1, i0:i1], start=True, stop=True)
                    p.inc(ins, "pe")

                pm = p.c["pe"]

                @blk.vector
                def _(v):
                    vi = vch(v, p)
                    v.wait_ge(p.S["pe"], pm)
                    vi(v.tensor_copy(negent_sb[:], ps_nb[:]))
                    vi(v.memset(zpad[:], 0.0))

            with nc.Block() as blk:
                pe0 = p.c["pe"]
                v0 = p.c["v"]
                s0 = p.c["s"]

                @blk.tensor
                def _(t):
                    for jt in range(32):
                        if jt >= 1:
                            t.wait_ge(p.S["v"], v0 + jt)
                        for ci in range(NCT):
                            ins = t.matmul(
                                ps_cr[:],
                                logpT[:, ci * T + jt * 128:
                                      ci * T + (jt + 1) * 128],
                                pt_sb[:, ci * IB:(ci + 1) * IB],
                                start=(ci == 0), stop=(ci == NCT - 1))
                        p.inc(ins, "pe")
                        t.wait_ge(p.S["s"], s0 + jt + 1)
                        ins = t.matmul(ps_z[:], cn.ones_bf_col[:],
                                       expT[:, jt * IB:(jt + 1) * IB],
                                       start=(jt == 0), stop=(jt == 31))
                        p.inc(ins, "pe")

                @blk.vector
                def _(v):
                    vi = vch(v, p)
                    for jt in range(32):
                        v.wait_ge(p.S["pe"], pe0 + 2 * jt + 1)
                        vi(v.tensor_sub(crs[:], ps_cr[:], negent_sb[:]))

                @blk.scalar
                def _(s):
                    si = sch(s, p)
                    for jt in range(32):
                        s.wait_ge(p.S["v"], v0 + jt + 1)
                        si(s.activation(expT[:, jt * IB:(jt + 1) * IB],
                                        crs[:], AF.Exp))

            with nc.Block() as blk:
                @blk.vector
                def _(v):
                    vi = vch(v, p)
                    vi(v.tensor_copy(zpad[0:1, :], ps_z[0:1, :]))
                    vi(v.reciprocal(zpad[0:1, :], zpad[0:1, :]))

            with nc.Block() as blk:
                pe0 = p.c["pe"]
                v0 = p.c["v"]

                @blk.tensor
                def _(t):
                    for k in range(4):
                        if k >= 1:
                            t.wait_ge(p.S["v"], v0 + k)
                        ins = t.transpose(ps_nb[:, 0:128],
                                          zpad[:, k * 128:(k + 1) * 128],
                                          cn.ident[:])
                        p.inc(ins, "pe")

                @blk.vector
                def _(v):
                    vi = vch(v, p)
                    for k in range(4):
                        v.wait_ge(p.S["pe"], pe0 + k + 1)
                        vi(v.tensor_copy(zr[:, k:k + 1], ps_nb[:, 0:1]))

            p.frames.pop().close()

            p.frames.append(ExitStack())
            ps_o = [p.psum("l3a_po", [128, TC], F32) for _ in range(4)]
            ps_tr2 = p.psum("l3a_ptr2", [128, 128], F32)
            for dh in range(2):
                with nc.Block() as blk:
                    dready = [0] * 32
                    pe_base = p.c["pe"]

                    @blk.sync
                    def _(e):
                        for jt in range(32):
                            if jt >= 2:
                                e.wait_ge(p.S["pe"], pe_base + 4 * (jt - 1))
                            ins = e.dma_start(
                                xtj[jt % 2][:],
                                xtokb[jt * 128:(jt + 1) * 128,
                                      dh * TC:(dh + 1) * TC])
                            p.inc(ins, "dsp", 16)
                            dready[jt] = p.c["dsp"]

                    @blk.tensor
                    def _(t):
                        for jt in range(32):
                            t.wait_ge(p.S["dsp"], dready[jt])
                            for isub in range(4):
                                ins = t.matmul(
                                    ps_o[isub][:],
                                    expT[:, jt * IB + isub * 128:
                                         jt * IB + (isub + 1) * 128],
                                    xtj[jt % 2][:],
                                    start=(jt == 0), stop=(jt == 31))
                                p.inc(ins, "pe")

                for isub in range(4):
                    with nc.Block() as blk:
                        @blk.vector
                        def _(v):
                            vi = vch(v, p)
                            vi(v.tensor_scalar(otok[:], ps_o[isub][:],
                                               zr[:, isub:isub + 1], None,
                                               ALU.mult))
                    with nc.Block() as blk:
                        pe0t = p.c["pe"]
                        v0t = p.c["v"]

                        @blk.tensor
                        def _(t):
                            for ds in range(4):
                                if ds >= 1:
                                    t.wait_ge(p.S["v"], v0t + ds)
                                ins = t.transpose(
                                    ps_tr2[:],
                                    otok[:, ds * 128:(ds + 1) * 128],
                                    cn.ident[:])
                                p.inc(ins, "pe")

                        @blk.vector
                        def _(v):
                            vi = vch(v, p)
                            for ds in range(4):
                                v.wait_ge(p.S["pe"], pe0t + ds + 1)
                                vi(v.tensor_copy(
                                    y_sb[:, (dh * 4 + ds) * IB + isub * 128:
                                         (dh * 4 + ds) * IB + (isub + 1) * 128],
                                    ps_tr2[:]))
            p.frames.pop().close()

            p.frames.append(ExitStack())
            lns = make_ln_scratch(p, width=IB)
            emit_resid_ln(p, cn, lns, ch=ib,
                          y_tiles=lambda ci: y_sb[:, ci * IB:(ci + 1) * IB],
                          xres=xin, out_dram=xout, xbuf=xbuf)
            p.frames.pop().close()



ECAP = 1280
NEXP = 32
BIGSLOT = 1.0e9


def emit_l3_ffn(p, cn, W, xin, out_final):
    nc = p.nc
    NT = 32
    xtok = p.dram("l3f_xtok", [T, C], F32R)
    cc_cnt_in = p.dram("l3f_cci", [1, NEXP], F32)
    cc_cnt_out = p.dram("l3f_cco", [8, NEXP], F32)
    idx_in = p.dram("l3f_idxi", [NEXP * ECAP, 1], F32)
    idx_out = p.dram("l3f_idxo", [NEXP * ECAP, 1], F32)
    xg = p.dram("l3f_xg", [8 * T, C], F32R)
    y_own = p.dram("l3f_yown", [4 * ECAP, C], F32R)
    yg = p.dram("l3f_yg", [NEXP * ECAP, C], F32R)

    wr_sb = p.sbuf("l3f_wr", [128, NCT * NEXP], F32R)
    cmask = p.sbuf("l3f_cm", [8, 1], F32R)
    cb = p.sbuf("l3f_cb", [128, 1], F32)
    cb2 = p.sbuf("l3f_cb2", [128, 1], F32)
    gate_sb = p.sbuf("l3f_gate", [128, NT], F32)
    acc_sb = p.sbuf("l3f_acc", [128, NT], F32)
    slots_sb = p.sbuf("l3f_slots", [128, NT], F32)
    idxf_sb = p.sbuf("l3f_idxf", [128, NT], F32)
    oh_sb = p.sbuf("l3f_oh", [128, NT * NEXP], F32R)
    pos_sb = p.sbuf("l3f_pos", [128, NT * NEXP], F32)
    cnts_sb = p.sbuf("l3f_cnts", [1, NT * NEXP], F32)
    offs_sb = p.sbuf("l3f_offs", [1, (NT + 1) * NEXP], F32R)
    coff_sb = p.sbuf("l3f_coff", [1, NEXP], F32R)

    with nc.Block() as blk:
        @blk.sync
        def _(e):
            ins = e.dma_start(wr_sb[:].rearrange("p (ci o) -> p ci o", ci=NCT),
                              W["w_router"].rearrange("(ci p) o -> p ci o", p=128))
            p.inc(ins, "dsp", 16)
            for tile, nmw in ((cmask, "coremask"), (cb, "cb"), (cb2, "cb2")):
                ins = e.dma_start(tile[:], W[nmw])
                p.inc(ins, "dsp", 16)
            e.wait_ge(p.S["dsp"], p.c["dsp"])

    with p.frame():
        xch = p.sbuf("l3f_x", [128, NCT * TC], F32R)
        xtt = p.sbuf("l3f_xtt", [128, 4 * C], F32R)
        e_sb = p.sbuf("l3f_e", [128, NEXP], F32)
        prob = p.sbuf("l3f_prob", [128, NEXP], F32)
        max8 = p.sbuf("l3f_max8", [128, 8], F32)
        idxu = p.sbuf("l3f_idxu", [128, 8], U32)
        stats = p.sbuf("l3f_st", [128, 4], F32)
        ps_tr = p.psum("l3f_ptr", [128, 128], F32)
        ps_lg = p.psum("l3f_plg", [128, NEXP], F32)
        ps_pos = p.psum("l3f_ppos", [128, NEXP], F32)
        ps_cnt = p.psum("l3f_pcnt", [1, NEXP], F32)

        for ch in range(NCH):
            t0, t1 = ch * TC, (ch + 1) * TC
            with nc.Block() as blk:
                @blk.sync
                def _(e):
                    ins = load_cm_chunk(e, p, xch, 0, xin, t0, t1)
                    p.inc(ins, "dsp", 16)
                    e.wait_ge(p.S["dsp"], p.c["dsp"])

            with nc.Block() as blk:
                pe0 = p.c["pe"]
                v0 = p.c["v"]

                @blk.tensor
                def _(t):
                    n = 0
                    for ci in range(NCT):
                        for st in range(4):
                            if n >= 1:
                                t.wait_ge(p.S["v"], v0 + n)
                            ins = t.transpose(
                                ps_tr[:].bitcast(F32R),
                                xch[:, ci * TC + st * 128:
                                    ci * TC + (st + 1) * 128],
                                cn.ident_r[:])
                            p.inc(ins, "pe")
                            n += 1

                @blk.vector
                def _(v):
                    vi = vch(v, p)
                    n = 0
                    for ci in range(NCT):
                        for st in range(4):
                            v.wait_ge(p.S["pe"], pe0 + n + 1)
                            vi(v.tensor_copy(
                                xtt[:, st * C + ci * 128:
                                    st * C + (ci + 1) * 128],
                                ps_tr[:].bitcast(F32R)))
                            n += 1

            with nc.Block() as blk:
                @blk.gpsimd
                def _(g):
                    ins = g.dma_start(
                        xtok.rearrange("(st pp) c -> pp st c", pp=128)
                        [:, 4 * ch:4 * (ch + 1), :],
                        xtt[:].rearrange("p (st c) -> p st c", st=4))
                    p.inc(ins, "dpool", 16)
                    g.wait_ge(p.S["dpool"], p.c["dpool"])

            for st in range(4):
                tile = 4 * ch + st
                with nc.Block() as blk:
                    pe0 = p.c["pe"]

                    @blk.tensor
                    def _(t):
                        for ci in range(NCT):
                            ins = t.matmul(
                                ps_lg[:],
                                xch[:, ci * TC + st * 128:
                                    ci * TC + (st + 1) * 128],
                                wr_sb[:, ci * NEXP:(ci + 1) * NEXP],
                                start=(ci == 0), stop=(ci == NCT - 1))
                        p.inc(ins, "pe")

                    pm = p.c["pe"]
                    v0 = p.c["v"]
                    s0 = p.c["s"]

                    @blk.vector
                    def _(v):
                        vi = vch(v, p)
                        v.wait_ge(p.S["pe"], pm)
                        vi(v.reduce_max(stats[:, 0:1], ps_lg[:], axis=AX.X))
                        vi(v.tensor_scalar(stats[:, 0:1], stats[:, 0:1], -1.0,
                                           None, ALU.mult))
                        v.wait_ge(p.S["s"], s0 + 1)
                        vi(v.reduce_sum(stats[:, 1:2], e_sb[:], axis=AX.X))
                        vi(v.reciprocal(stats[:, 2:3], stats[:, 1:2]))
                        vi(v.tensor_scalar(prob[:], e_sb[:], stats[:, 2:3],
                                           None, ALU.mult))
                        vi(v.max(max8[:], prob[:]))
                        vi(v.max_index(idxu[:], max8[:], prob[:]))
                        vi(v.tensor_copy(gate_sb[:, tile:tile + 1],
                                         max8[:, 0:1]))
                        vi(v.tensor_copy(idxf_sb[:, tile:tile + 1],
                                         idxu[:, 0:1]))
                        vi(v.tensor_scalar(prob[:], cn.iota32[:],
                                           idxf_sb[:, tile:tile + 1], None,
                                           ALU.subtract))
                    vm = p.c["v"]

                    @blk.scalar
                    def _(s):
                        si = sch(s, p)
                        s.wait_ge(p.S["v"], v0 + 2)
                        si(s.activation(e_sb[:], ps_lg[:], AF.Exp,
                                        bias=stats[:, 0:1]))
                        s.wait_ge(p.S["v"], vm)
                        si(s.activation(prob[:], prob[:], AF.Square))
                    sm = p.c["s"]

                    @blk.vector
                    def _(v):
                        vi = vch(v, p)
                        v.wait_ge(p.S["s"], sm)
                        vi(v.tensor_scalar(
                            oh_sb[:, tile * NEXP:(tile + 1) * NEXP],
                            prob[:], -1.0, 1.0, ALU.mult, ALU.add))
                        vi(v.tensor_scalar_max(
                            oh_sb[:, tile * NEXP:(tile + 1) * NEXP],
                            oh_sb[:, tile * NEXP:(tile + 1) * NEXP], 0.0))
                    vm2 = p.c["v"]

                    @blk.tensor
                    def _(t):
                        t.wait_ge(p.S["v"], vm2)
                        ins = t.matmul(ps_pos[:], cn.ustrict[:],
                                       oh_sb[:, tile * NEXP:(tile + 1) * NEXP],
                                       start=True, stop=True)
                        p.inc(ins, "pe")
                        ins = t.matmul(ps_cnt[:], cn.ones_col[:],
                                       oh_sb[:, tile * NEXP:(tile + 1) * NEXP],
                                       start=True, stop=True)
                        p.inc(ins, "pe")

                    pm2 = p.c["pe"]

                    @blk.vector
                    def _(v):
                        vi = vch(v, p)
                        v.wait_ge(p.S["pe"], pm2)
                        vi(v.tensor_copy(
                            pos_sb[:, tile * NEXP:(tile + 1) * NEXP],
                            ps_pos[:]))
                        vi(v.tensor_copy(cnts_sb[0:1, tile * NEXP:
                                                 (tile + 1) * NEXP],
                                         ps_cnt[0:1, :]))

    with nc.Block() as blk:
        @blk.gpsimd
        def _(g):
            g.collective_compute(
                "AllGather", ALU.bypass, replica_groups=[list(range(8))],
                ins=[xtok.opt()], outs=[xg.opt()],
            ).then_inc(p.S["cc"])
            p.c["cc"] += 1

    with nc.Block() as blk:
        @blk.vector
        def _(v):
            vi = vch(v, p)
            vi(v.memset(offs_sb[0:1, 0:NEXP].bitcast(F32), 0.0))
            for k in range(NT):
                vi(v.tensor_add(offs_sb[0:1, (k + 1) * NEXP:(k + 2) * NEXP]
                                .bitcast(F32),
                                offs_sb[0:1, k * NEXP:(k + 1) * NEXP]
                                .bitcast(F32),
                                cnts_sb[0:1, k * NEXP:(k + 1) * NEXP]))

    with nc.Block() as blk:
        @blk.gpsimd
        def _(g):
            ins = g.dma_start(cc_cnt_in,
                              offs_sb[0:1, NT * NEXP:(NT + 1) * NEXP]
                              .bitcast(F32))
            p.inc(ins, "dpool", 16)
            g.wait_ge(p.S["dpool"], p.c["dpool"])
            g.collective_compute(
                "AllGather", ALU.bypass, replica_groups=[list(range(8))],
                ins=[cc_cnt_in.opt()], outs=[cc_cnt_out.opt()],
            ).then_inc(p.S["cc"])
            p.c["cc"] += 1
            g.wait_ge(p.S["cc"], p.c["cc"])

    agc = p.sbuf("l3f_agc", [8, NEXP], F32R)
    agc0 = p.sbuf("l3f_agc0", [8, NEXP], F32)
    ps_coff = p.psum("l3f_pcoff", [1, NEXP], F32)
    with nc.Block() as blk:
        @blk.sync
        def _(e):
            ins = e.dma_start(agc0[:], cc_cnt_out)
            p.inc(ins, "dsp", 16)
            e.wait_ge(p.S["dsp"], p.c["dsp"])
    with nc.Block() as blk:
        @blk.vector
        def _(v):
            vi = vch(v, p)
            vi(v.tensor_copy(agc[:], agc0[:]))

    with nc.Block() as blk:
        @blk.tensor
        def _(t):
            ins = t.matmul(ps_coff[:], cmask[:], agc[:], start=True, stop=True)
            p.inc(ins, "pe")

        pm = p.c["pe"]

        @blk.vector
        def _(v):
            vi = vch(v, p)
            v.wait_ge(p.S["pe"], pm)
            vi(v.tensor_copy(coff_sb[:], ps_coff[:]))

    with p.frame():
        combo = p.sbuf("l3f_combo", [1, NEXP], F32R)
        rep = p.psum("l3f_rep", [128, NEXP], F32)
        tposf = p.sbuf("l3f_tposf", [128, NEXP], F32)
        tok_sb = p.sbuf("l3f_tok", [128, 2], F32)
        slotu = p.sbuf("l3f_slotu", [128, 1], U32)
        zrow = p.sbuf("l3f_zrow", [128, 320], F32)

        with nc.Block() as blk:
            @blk.vector
            def _(v):
                vi = vch(v, p)
                vi(v.memset(zrow[:], 0.0))

        with nc.Block() as blk:
            @blk.gpsimd
            def _(g):
                ins = g.dma_start(
                    idx_in.rearrange("(pp a) c -> pp (a c)", pp=128),
                    zrow[:])
                p.inc(ins, "dpool", 16)
                g.wait_ge(p.S["dpool"], p.c["dpool"])

        for tile in range(NT):
            with nc.Block() as blk:
                @blk.vector
                def _(v):
                    vi = vch(v, p)
                    vi(v.tensor_add(combo[:], coff_sb[:],
                                    offs_sb[0:1, tile * NEXP:(tile + 1) * NEXP]))
                vm = p.c["v"]

                @blk.tensor
                def _(t):
                    t.wait_ge(p.S["v"], vm)
                    ins = t.matmul(rep[:], cn.ones_row[:], combo[:],
                                   start=True, stop=True)
                    p.inc(ins, "pe")

                pm = p.c["pe"]

                @blk.vector
                def _(v):
                    vi = vch(v, p)
                    v.wait_ge(p.S["pe"], pm)
                    vi(v.tensor_add(tposf[:],
                                    pos_sb[:, tile * NEXP:(tile + 1) * NEXP],
                                    rep[:]))
                    vi(v.tensor_mul(tposf[:], tposf[:],
                                    oh_sb[:, tile * NEXP:(tile + 1) * NEXP]))
                    vi(v.reduce_sum(tok_sb[:, 0:1], tposf[:], axis=AX.X))
                    vi(v.tensor_scalar(tok_sb[:, 1:2], tok_sb[:, 0:1], -1.0,
                                       float(ECAP), ALU.mult, ALU.add))
                    vi(v.tensor_scalar_min(tok_sb[:, 1:2], tok_sb[:, 1:2], 1.0))
                    vi(v.tensor_scalar_max(acc_sb[:, tile:tile + 1],
                                           tok_sb[:, 1:2], 0.0))
                    vi(v.scalar_tensor_tensor(
                        slots_sb[:, tile:tile + 1],
                        idxf_sb[:, tile:tile + 1], float(ECAP),
                        tok_sb[:, 0:1], ALU.mult, ALU.add))
                    vi(v.tensor_scalar(tok_sb[:, 1:2],
                                       acc_sb[:, tile:tile + 1],
                                       -BIGSLOT, BIGSLOT, ALU.mult, ALU.add))
                    vi(v.scalar_tensor_tensor(
                        slots_sb[:, tile:tile + 1],
                        slots_sb[:, tile:tile + 1],
                        acc_sb[:, tile:tile + 1],
                        tok_sb[:, 1:2], ALU.mult, ALU.add))
                    vi(v.tensor_copy(slotu[:], slots_sb[:, tile:tile + 1]))
                    vi(v.tensor_scalar(tok_sb[:, 0:1], cn.iota_col[:], 1.0,
                                       float(tile * 128), ALU.mult, ALU.add))
                    vi(v.tensor_add(tok_sb[:, 0:1], tok_sb[:, 0:1], cb[:]))

            with nc.Block() as blk:
                @blk.gpsimd
                def _(g):
                    ins = g.indirect_dma_start(
                        idx_in, IndirectOffsetOnAxis(slotu[:], 0),
                        tok_sb[:, 0:1], None,
                        bounds_check=NEXP * ECAP, oob_is_err=False)
                    p.inc(ins, "dpool", 16)
                    g.wait_ge(p.S["dpool"], p.c["dpool"])

        with nc.Block() as blk:
            @blk.gpsimd
            def _(g):
                g.wait_ge(p.S["cc"], p.c["cc"])
                g.collective_compute(
                    "AllReduce", ALU.add, replica_groups=[list(range(8))],
                    ins=[idx_in.opt()], outs=[idx_out.opt()],
                ).then_inc(p.S["cc"])
                p.c["cc"] += 1
                g.wait_ge(p.S["cc"], p.c["cc"])

    with p.frame():
        NTOK = 4 * ECAP // 128
        idxl = p.sbuf("l3f_idxl", [128, 1], F32)
        idxlu = p.sbuf("l3f_idxlu", [128, 1], U32)
        offu = p.sbuf("l3f_offu", [128, 1], U32)
        offf = p.sbuf("l3f_offf", [128, 1], F32)
        xrow = p.sbuf("l3f_xrow", [128, C], F32R)
        xeT = p.sbuf("l3f_xeT", [128, NCT * ECAP], F32R)
        hT = p.sbuf("l3f_hT", [128, NCT * ECAP], F32R)
        w_sb = p.sbuf("l3f_we", [128, NCT * C], F32R)
        yrow = p.sbuf("l3f_yrow", [128, C], F32)
        ps_tr = p.psum("l3f_ptr2", [128, 128], F32)
        psm = [p.psum("l3f_pse", [128, TC], F32) for _ in range(2)]
        tchunks = [(0, 512), (512, 1024), (1024, 1280)]

        for eo in range(4):
            for gt in range(10):
                with nc.Block() as blk:
                    @blk.vector
                    def _(v):
                        vi = vch(v, p)
                        vi(v.tensor_scalar(offf[:], cn.iota_col[:], 1.0,
                                           float((eo * 10 + gt) * 128),
                                           ALU.mult, ALU.add))
                        vi(v.tensor_add(offf[:], offf[:], cb2[:]))
                        vi(v.tensor_copy(offu[:], offf[:]))

                with nc.Block() as blk:
                    @blk.gpsimd
                    def _(g):
                        ins = g.indirect_dma_start(
                            idxl[:], None,
                            idx_out, IndirectOffsetOnAxis(offu[:], 0))
                        p.inc(ins, "dpool", 16)
                        g.wait_ge(p.S["dpool"], p.c["dpool"])

                with nc.Block() as blk:
                    @blk.vector
                    def _(v):
                        vi = vch(v, p)
                        vi(v.tensor_copy(idxlu[:], idxl[:]))

                with nc.Block() as blk:
                    @blk.gpsimd
                    def _(g):
                        ins = g.indirect_dma_start(
                            xrow[:], None,
                            xg, IndirectOffsetOnAxis(idxlu[:], 0))
                        p.inc(ins, "dpool", 16)
                        g.wait_ge(p.S["dpool"], p.c["dpool"])

                with nc.Block() as blk:
                    pe0 = p.c["pe"]
                    v0 = p.c["v"]

                    @blk.tensor
                    def _(t):
                        for ci in range(NCT):
                            if ci >= 1:
                                t.wait_ge(p.S["v"], v0 + ci)
                            ins = t.transpose(
                                ps_tr[:].bitcast(F32R),
                                xrow[:, ci * 128:(ci + 1) * 128],
                                cn.ident_r[:])
                            p.inc(ins, "pe")

                    @blk.vector
                    def _(v):
                        vi = vch(v, p)
                        for ci in range(NCT):
                            v.wait_ge(p.S["pe"], pe0 + ci + 1)
                            vi(v.tensor_copy(
                                xeT[:, ci * ECAP + gt * 128:
                                    ci * ECAP + (gt + 1) * 128],
                                ps_tr[:].bitcast(F32R)))

            def stage_mm(wname, src_sb, ep_fn, ep_eng, ep_nops):
                with nc.Block() as blk:
                    @blk.sync
                    def _(e):
                        ins = e.dma_start(
                            w_sb[:].rearrange("p (ci o) -> p ci o", ci=NCT),
                            W[wname][eo].rearrange("(ci p) o -> p ci o", p=128))
                        p.inc(ins, "dsp", 16)
                        e.wait_ge(p.S["dsp"], p.c["dsp"])
                with nc.Block() as blk:
                    pe0 = p.c["pe"]
                    e0 = p.c[ep_eng]

                    @blk.tensor
                    def _(t):
                        n = 0
                        for ot in range(NCT):
                            for tci, (c0, c1) in enumerate(tchunks):
                                if n >= 2:
                                    t.wait_ge(p.S[ep_eng],
                                              e0 + ep_nops * (n - 1))
                                for ci in range(NCT):
                                    ins = t.matmul(
                                        psm[n % 2][:, 0:c1 - c0],
                                        w_sb[:, ci * C + ot * 128:
                                             ci * C + ot * 128 + 128],
                                        src_sb[:, ci * ECAP + c0:
                                               ci * ECAP + c1],
                                        start=(ci == 0), stop=(ci == NCT - 1))
                                p.inc(ins, "pe")
                                n += 1

                    def _ep(eng):
                        issue = (sch if ep_eng == "s" else vch)(eng, p)
                        n = 0
                        for ot in range(NCT):
                            for tci, (c0, c1) in enumerate(tchunks):
                                eng.wait_ge(p.S["pe"], pe0 + n + 1)
                                ep_fn(eng, issue, ot, c0, c1,
                                      psm[n % 2][:, 0:c1 - c0])
                                n += 1

                    if ep_eng == "s":
                        @blk.scalar
                        def _(s):
                            _ep(s)
                    else:
                        @blk.vector
                        def _(v):
                            _ep(v)

            def ep_g(eng, issue, ot, c0, c1, ps):
                issue(eng.activation(hT[:, ot * ECAP + c0:ot * ECAP + c1],
                                     ps, AF.Silu))
            stage_mm("wg", xeT, ep_g, "s", 1)

            def ep_u(eng, issue, ot, c0, c1, ps):
                issue(eng.tensor_mul(hT[:, ot * ECAP + c0:ot * ECAP + c1],
                                     hT[:, ot * ECAP + c0:ot * ECAP + c1], ps))
            stage_mm("wu", xeT, ep_u, "v", 1)

            def ep_y(eng, issue, ot, c0, c1, ps):
                issue(eng.tensor_copy(xeT[:, ot * ECAP + c0:ot * ECAP + c1],
                                      ps))
            stage_mm("wd", hT, ep_y, "v", 1)

            for gt in range(10):
                with nc.Block() as blk:
                    pe0 = p.c["pe"]
                    v0 = p.c["v"]

                    @blk.tensor
                    def _(t):
                        for ci in range(NCT):
                            if ci >= 1:
                                t.wait_ge(p.S["v"], v0 + ci)
                            ins = t.transpose(
                                ps_tr[:].bitcast(F32R),
                                xeT[:, ci * ECAP + gt * 128:
                                    ci * ECAP + (gt + 1) * 128],
                                cn.ident_r[:])
                            p.inc(ins, "pe")

                    @blk.vector
                    def _(v):
                        vi = vch(v, p)
                        for ci in range(NCT):
                            v.wait_ge(p.S["pe"], pe0 + ci + 1)
                            vi(v.tensor_copy(yrow[:, ci * 128:(ci + 1) * 128],
                                             ps_tr[:]))

                with nc.Block() as blk:
                    @blk.gpsimd
                    def _(g):
                        ins = g.dma_start(
                            y_own[(eo * 10 + gt) * 128:
                                  (eo * 10 + gt + 1) * 128, :],
                            yrow[:].bitcast(F32R))
                        p.inc(ins, "dpool", 16)
                        g.wait_ge(p.S["dpool"], p.c["dpool"])

    with nc.Block() as blk:
        @blk.gpsimd
        def _(g):
            g.collective_compute(
                "AllGather", ALU.bypass, replica_groups=[list(range(8))],
                ins=[y_own.opt()], outs=[yg.opt()],
            ).then_inc(p.S["cc"])
            p.c["cc"] += 1
            g.wait_ge(p.S["cc"], p.c["cc"])

    with p.frame():
        slotu2 = p.sbuf("l3f_slotu2", [128, 1], U32)
        ytok = p.sbuf("l3f_ytok", [128, C], F32)
        xtrow = p.sbuf("l3f_xtrow", [128, C], F32R)
        sq = p.sbuf("l3f_sq", [128, C], F32)
        st4 = p.sbuf("l3f_st4", [128, 6], F32)

        for tile in range(NT):
            with nc.Block() as blk:
                @blk.vector
                def _(v):
                    vi = vch(v, p)
                    vi(v.tensor_scalar_min(ytok[:, 0:1], slots_sb[:, tile:tile + 1],
                                           float(NEXP * ECAP - 1)))
                    vi(v.tensor_copy(slotu2[:], ytok[:, 0:1]))
                    vi(v.memset(ytok[:], 0.0))

                @blk.sync
                def _(e):
                    ins = e.dma_start(xtrow[:],
                                      xtok[tile * 128:(tile + 1) * 128, :])
                    p.inc(ins, "dsp", 16)
                    e.wait_ge(p.S["dsp"], p.c["dsp"])

            with nc.Block() as blk:
                @blk.gpsimd
                def _(g):
                    ins = g.indirect_dma_start(
                        ytok[:].bitcast(F32R), None,
                        yg, IndirectOffsetOnAxis(slotu2[:], 0))
                    p.inc(ins, "dpool", 16)
                    g.wait_ge(p.S["dpool"], p.c["dpool"])

            with nc.Block() as blk:
                s0 = p.c["s"]
                v0 = p.c["v"]

                @blk.vector
                def _(v):
                    vi = vch(v, p)
                    vi(v.tensor_scalar(ytok[:], ytok[:],
                                       acc_sb[:, tile:tile + 1],
                                       gate_sb[:, tile:tile + 1],
                                       ALU.mult, ALU.mult))
                    vi(v.scalar_tensor_tensor(ytok[:], xtrow[:], 2.0, ytok[:],
                                              ALU.mult, ALU.add))
                    vi(v.reduce_sum(st4[:, 0:1], ytok[:], axis=AX.X))
                    vi(v.tensor_scalar(st4[:, 1:2], st4[:, 0:1], 1.0 / C,
                                       None, ALU.mult))
                    v.wait_ge(p.S["s"], s0 + 1)
                    vi(v.reduce_sum(st4[:, 2:3], sq[:], axis=AX.X))
                    vi(v.tensor_scalar(st4[:, 2:3], st4[:, 2:3], 1.0 / C,
                                       None, ALU.mult))
                    vi(v.scalar_tensor_tensor(st4[:, 3:4], st4[:, 1:2],
                                              st4[:, 1:2], st4[:, 2:3],
                                              ALU.mult, ALU.subtract))
                    vi(v.tensor_scalar(st4[:, 3:4], st4[:, 3:4], -1.0,
                                       LN_EPS, ALU.mult, ALU.add))
                    v.wait_ge(p.S["s"], s0 + 2)
                    vi(v.reciprocal(st4[:, 4:5], st4[:, 4:5]))
                    vi(v.tensor_scalar(ytok[:], ytok[:], st4[:, 1:2],
                                       st4[:, 4:5], ALU.subtract, ALU.mult))

                @blk.scalar
                def _(s):
                    si = sch(s, p)
                    s.wait_ge(p.S["v"], v0 + 2)
                    si(s.activation(sq[:], ytok[:], AF.Square))
                    s.wait_ge(p.S["v"], v0 + 8)
                    si(s.activation(st4[:, 4:5], st4[:, 3:4], AF.Sqrt))

            with nc.Block() as blk:
                @blk.gpsimd
                def _(g):
                    ins = g.dma_start(
                        out_final[tile * 128:(tile + 1) * 128, :], ytok[:])
                    p.inc(ins, "dpool", 16)
                    g.wait_ge(p.S["dpool"], p.c["dpool"])



def msel_const():
    M = np.zeros((80, 48), np.float32)
    for n in range(16):
        M[n, n] = 1.0
        M[16 + 2 * n, 16 + n] = 0.5
        M[17 + 2 * n, 16 + n] = 0.5
        M[48 + 2 * n, 32 + n] = 0.5
        M[49 + 2 * n, 32 + n] = 0.5
    return M


def build_full():
    nc = bass.Bass("TRN2", target_bir_lowering=False)

    def din(name, shape, dt=F32R):
        return nc.dram_tensor(name, list(shape), dt, kind="ExternalInput").ap()

    x0 = din("x0", [C, T])
    out = nc.dram_tensor("out", [T, C], F32, kind="ExternalOutput").ap()

    W0a = dict(
        kwT=din("l0_kwT", [C, C]), vwT=din("l0_vwT", [C, C]),
        rwT=din("l0_rwT", [C, C]), owT=din("l0_owT", [C, C]),
        fkwT=din("l0_fkwT", [C, 4096]), frwT=din("l0_frwT", [C, C]),
        fvwT=din("l0_fvwT", [4096, C]),
        **{k: din("l0_" + k, [128, 8], F32) for k in
           ["ln1w", "ln1b", "ln2w", "ln2b", "decay", "tfirst", "tmk", "tmv",
            "tmr", "ftmk", "ftmr"]},
    )
    W0f = dict(
        w1T=din("l0f_w1T", [C, 4096]), w2T=din("l0f_w2T", [C, 4096]),
        h1T=din("l0f_h1T", [4096, 4096]), h2T=din("l0f_h2T", [4096, 4096]),
        w3T=din("l0f_w3T", [4096, C]),
        s1=din("l0f_s1", [128, 32], F32), s2=din("l0f_s2", [128, 32], F32),
        sh1=din("l0f_sh1", [128, 32], F32), sh2=din("l0f_sh2", [128, 32], F32),
        s3=din("l0f_s3", [128, 8], F32),
    )
    W1a = dict(
        wq=din("l1_wq", [C, C]), wk=din("l1_wk", [C, C]), wv=din("l1_wv", [C, C]),
        wo=din("l1_wo", [C, C]), w1=din("l1_w1", [C, 4096]),
        w2=din("l1_w2", [4096, C]),
        b1=din("l1_b1", [128, 32], F32),
        **{k: din("l1_" + k, [128, 8], F32) for k in
           ["bq", "bk", "bv", "bo", "b2", "g1", "be1", "g2", "be2"]},
    )
    W1f = dict(
        wrT=din("l1f_wrT", [C, C]), wkT=din("l1f_wkT", [C, C]),
        wvT=din("l1f_wvT", [C, C]),
        mu_r=din("l1f_mur", [128, 8], F32), mu_k=din("l1f_muk", [128, 8], F32),
    )
    W2a = dict(
        convw=din("l2_convw", [128, 32], F32), convb=din("l2_convb", [128, 8], F32),
        win=din("l2_win", [C, 80]), b_in=din("l2_bin", [80, 16]),
        msel=din("l2_msel", [80, 48]), wout=din("l2_wout", [16, C]),
        b_out=din("l2_bout", [128, 8], F32), alog=din("l2_alog", [16, 1], F32),
    )
    W2f = dict(
        g1r=din("l2f_g1r", [32, 128]), g1i=din("l2f_g1i", [32, 128]),
        g2r=din("l2f_g2r", [16, 64]), g2i=din("l2f_g2i", [16, 64]),
        bias=din("l2f_bias", [128, 8], F32),
    )
    W3f = dict(
        w_router=din("l3_router", [C, 32]),
        wg=din("l3_wg", [4, C, C]), wu=din("l3_wu", [4, C, C]),
        wd=din("l3_wd", [4, C, C]),
        coremask=din("l3_cmask", [8, 1]),
        cb=din("l3_cb", [128, 1], F32), cb2=din("l3_cb2", [128, 1], F32),
    )
    cd = {k: nc.dram_tensor("c_" + k, list(v.shape), const_dtype(k),
                            kind="ExternalInput").ap()
          for k, v in host_consts().items()}

    xA = nc.dram_tensor("xA", [C, T], F32R, kind="Internal").ap()
    xB = nc.dram_tensor("xB", [C, T], F32R, kind="Internal").ap()

    with ExitStack() as stack:
        p = P(nc, stack)
        cn = Consts(p, cd)
        cn.emit_load(p)
        with p.frame():
            emit_l0_attn(p, cn, W0a, x0, xA)
        with p.frame():
            emit_l0_ffn(p, cn, W0f, xA, xB)
        with p.frame():
            emit_l1_attn(p, cn, W1a, xB, xA)
        with p.frame():
            emit_l1_ffn(p, cn, W1f, xA, xB)
        with p.frame():
            emit_l2_attn(p, cn, W2a, xB, xA)
        with p.frame():
            emit_l2_ffn(p, cn, W2f, xA, xB)
        with p.frame():
            emit_l3_attn(p, cn, xB, xA)
        with p.frame():
            emit_l3_ffn(p, cn, W3f, xA, out)
    return nc


def _ct(a):
    return np.ascontiguousarray(np.asarray(a, np.float32))


def kernel(**inputs):
    g = lambda k: np.asarray(inputs[k], np.float32)
    consts = {"c_" + k: v for k, v in host_consts().items()}

    shared = dict(
        l0_kwT=_ct(g("L0_attn_att_kw").T), l0_vwT=_ct(g("L0_attn_att_vw").T),
        l0_rwT=_ct(g("L0_attn_att_rw").T), l0_owT=_ct(g("L0_attn_att_ow").T),
        l0_fkwT=_ct(g("L0_attn_f_kw").T), l0_frwT=_ct(g("L0_attn_f_rw").T),
        l0_fvwT=_ct(g("L0_attn_f_vw").T),
        l0_ln1w=vec8(g("L0_attn_ln1_w")), l0_ln1b=vec8(g("L0_attn_ln1_b")),
        l0_ln2w=vec8(g("L0_attn_ln2_w")), l0_ln2b=vec8(g("L0_attn_ln2_b")),
        l0_decay=vec8(g("L0_attn_time_decay")),
        l0_tfirst=vec8(g("L0_attn_time_first")),
        l0_tmk=vec8(g("L0_attn_tmk")), l0_tmv=vec8(g("L0_attn_tmv")),
        l0_tmr=vec8(g("L0_attn_tmr")),
        l0_ftmk=vec8(g("L0_attn_f_tmk")), l0_ftmr=vec8(g("L0_attn_f_tmr")),
        l0f_w1T=_ct(g("L0_ffn_w1_w").T), l0f_w2T=_ct(g("L0_ffn_w2_w").T),
        l0f_h1T=_ct(g("L0_ffn_h1_w").T), l0f_h2T=_ct(g("L0_ffn_h2_w").T),
        l0f_w3T=_ct(g("L0_ffn_w3_w").T),
        l0f_s1=_ct(g("L0_ffn_w1_s").reshape(-1, 128).T),
        l0f_s2=_ct(g("L0_ffn_w2_s").reshape(-1, 128).T),
        l0f_sh1=_ct(g("L0_ffn_h1_s").reshape(-1, 128).T),
        l0f_sh2=_ct(g("L0_ffn_h2_s").reshape(-1, 128).T),
        l0f_s3=vec8(g("L0_ffn_w3_s")),
        l1_wq=_ct(g("L1_attn_Wq")), l1_wk=_ct(g("L1_attn_Wk")),
        l1_wv=_ct(g("L1_attn_Wv")), l1_wo=_ct(g("L1_attn_Wo")),
        l1_w1=_ct(g("L1_attn_W1")), l1_w2=_ct(g("L1_attn_W2")),
        l1_bq=vec8(g("L1_attn_bq")), l1_bk=vec8(g("L1_attn_bk")),
        l1_bv=vec8(g("L1_attn_bv")), l1_bo=vec8(g("L1_attn_bo")),
        l1_b1=_ct(g("L1_attn_b1").reshape(-1, 128).T),
        l1_b2=vec8(g("L1_attn_b2")),
        l1_g1=vec8(g("L1_attn_g1")), l1_be1=vec8(g("L1_attn_be1")),
        l1_g2=vec8(g("L1_attn_g2")), l1_be2=vec8(g("L1_attn_be2")),
        l1f_wrT=_ct(g("L1_ffn_W_r").T), l1f_wkT=_ct(g("L1_ffn_W_k").T),
        l1f_wvT=_ct(g("L1_ffn_W_v").T),
        l1f_mur=vec8(g("L1_ffn_mu_r")), l1f_muk=vec8(g("L1_ffn_mu_k")),
        l2_convw=_ct(g("L2_attn_conv_w")[:, 0, :].reshape(8, 128, 4)
                     .transpose(1, 0, 2).reshape(128, 32)),
        l2_convb=vec8(g("L2_attn_conv_b")),
        l2_win=_ct(g("L2_attn_Win")),
        l2_bin=_ct(np.repeat(g("L2_attn_b_in").reshape(80, 1), 16, 1)),
        l2_msel=msel_const(),
        l2_wout=_ct(g("L2_attn_Wout")),
        l2_bout=vec8(g("L2_attn_b_out")),
        l2_alog=_ct(g("L2_attn_A_log").reshape(16, 1)),
        l2f_g1r=_ct(g("L2_ffn_G1_R")[0].transpose(2, 1, 0).reshape(32, 128)),
        l2f_g1i=_ct(g("L2_ffn_G1_I")[0].transpose(2, 1, 0).reshape(32, 128)),
        l2f_g2r=_ct(g("L2_ffn_G2_R")[:, :, 0, :].transpose(2, 0, 1).reshape(16, 64)),
        l2f_g2i=_ct(g("L2_ffn_G2_I")[:, :, 0, :].transpose(2, 0, 1).reshape(16, 64)),
        l2f_bias=vec8(np.concatenate([g("L2_ffn_bias_real"),
                                      g("L2_ffn_bias_imag")])),
        l3_router=_ct(g("L3_ffn_w_router")),
    )
    shared.update(consts)

    x = g("L0_attn_x")
    wg = g("L3_ffn_w_gate")
    wu = g("L3_ffn_w_up")
    wd = g("L3_ffn_w_down")
    in_maps = []
    for c in range(NCORE):
        m = dict(shared)
        m["x0"] = _ct(x[c].T)
        m["l3_wg"] = _ct(wg[c * 4:(c + 1) * 4])
        m["l3_wu"] = _ct(wu[c * 4:(c + 1) * 4])
        m["l3_wd"] = _ct(wd[c * 4:(c + 1) * 4])
        m["l3_cmask"] = _ct((np.arange(8) < c).reshape(8, 1))
        m["l3_cb"] = np.full((128, 1), float(c * T), np.float32)
        m["l3_cb2"] = np.full((128, 1), float(c * 4 * ECAP), np.float32)
        in_maps.append(m)

    nc = build_full()
    res = run_bass_kernel_spmd(nc, in_maps, core_ids=list(range(NCORE)))
    return np.stack([np.asarray(res.results[c]["out"]) for c in range(NCORE)])
